# version 20
# baseline (speedup 1.0000x reference)
from contextlib import ExitStack

import numpy as np
import ml_dtypes
import concourse.bass as bass
import concourse.mybir as mybir
from concourse.bass_utils import run_bass_kernel_spmd
from concourse.hw_specs import get_activation_tables

F32 = mybir.dt.float32
BF16 = mybir.dt.bfloat16
F16 = mybir.dt.float16
I32 = mybir.dt.int32
ALU = mybir.AluOpType
ACT = mybir.ActivationFunctionType
AX = mybir.AxisListType.X

B = 2
T = 2048
D = 1024
DI = 2816
NT = T // 128
NEG = -1.0e9

KEYS = [(qt + 1) * 128 for qt in range(NT)]
NKC = [(k + 511) // 512 for k in KEYS]
KCW = [[min(512, KEYS[qt] - kc * 512) for kc in range(NKC[qt])] for qt in range(NT)]
CUMPT = [0]
for _qt in range(NT):
    CUMPT.append(CUMPT[-1] + _qt + 1)

STAGE = 99


def build_nc():
    nc = bass.Bass("TRN2", target_bir_lowering=False, debug=False)

    x_in = nc.dram_tensor("x", [B, T, D], F32, kind="ExternalInput").ap()
    c_in = nc.dram_tensor("c", [B, D], F32, kind="ExternalInput").ap()
    wmod = nc.dram_tensor("wmod", [D, 6 * D], F32, kind="ExternalInput").ap()
    aw = nc.dram_tensor("aw", [D, 3 * D], F32, kind="ExternalInput").ap()
    ab = nc.dram_tensor("ab", [1, 3 * D], F32, kind="ExternalInput").ap()
    clips = nc.dram_tensor("clips", [1, 2], F32, kind="ExternalInput").ap()
    wg_in = nc.dram_tensor("wg", [DI, D], F32, kind="ExternalInput").ap()
    wu_in = nc.dram_tensor("wu", [DI, D], F32, kind="ExternalInput").ap()
    wd_in = nc.dram_tensor("wd", [D, DI], F32, kind="ExternalInput").ap()
    idb = nc.dram_tensor("idb", [128, 128], BF16, kind="ExternalInput").ap()
    idf = nc.dram_tensor("idf", [128, 128], F32, kind="ExternalInput").ap()
    maskc = nc.dram_tensor("maskc", [128, 128], F32, kind="ExternalInput").ap()
    out = nc.dram_tensor("out", [B, T, D], F32, kind="ExternalOutput").ap()

    h1s = nc.dram_tensor("h1s", [T, D], BF16).ap()
    hs = nc.dram_tensor("hs", [B * T, D], F16).ap()
    pss = nc.dram_tensor("pss", [2, 128, T], BF16).ap()
    ss = nc.dram_tensor("ss", [B * T, DI], F16).ap()
    wgq = nc.dram_tensor("wgq", [DI, D], BF16).ap()
    wuq = nc.dram_tensor("wuq", [DI, D], BF16).ap()
    wdq = nc.dram_tensor("wdq", [B, D, DI], BF16).ap()
    pmx = nc.dram_tensor("pmx", [1, 128], F32).ap()
    ccb = [[nc.dram_tensor(f"cc{j}{d}", [1, 16], F32) for d in "io"] for j in range(2)]

    act_sets = list(get_activation_tables(nc.m.arch).keys())

    def load_set(eng, name):
        eng.add_instruction(mybir.InstLoadActFuncSet(
            name=nc.get_next_instruction_name(),
            act_func_set_id=act_sets.index(name), ins=[], outs=[]))

    with ExitStack() as top:
        names = ["dg", "dsy", "dsc", "tt", "vv", "sc",
                 "vvw", "scw", "cc",
                 "dgx0", "dgx1", "dwv0", "dwv1", "dwo0", "dwo1",
                 "dwt0", "dwt1", "dh10", "dh11", "dps0", "dps1",
                 "dpt0", "dpt1", "dou0", "dou1", "dss0", "dss1",
                 "dwq0", "dwq1", "dpts0", "dpts1",
                 "vv2", "sc2", "dlxA0", "dlxA1",
                 "vv3", "sc3", "dlxB0", "dlxB1", "dh20", "dh21",
                 "dlxC0", "dlxC1", "dwtS0", "dwtS1"]
        H = {}
        V = {}
        for n in names:
            H[n] = top.enter_context(nc.semaphore(n))
            V[n] = 0

        def snap():
            return dict(V)

        sb = top.enter_context
        ones32 = sb(nc.sbuf_tensor("ones32", [1, 128], F32))
        onesb = sb(nc.sbuf_tensor("onesb", [1, 512], BF16))
        zc = sb(nc.sbuf_tensor("zc", [128, 2], F32))
        idbt = sb(nc.sbuf_tensor("idbt", [128, 128], BF16))
        idft = sb(nc.sbuf_tensor("idft", [128, 128], F32))
        maskt = sb(nc.sbuf_tensor("maskt", [128, 128], F32))

        ca1T = sb(nc.sbuf_tensor("ca1T", [128, 8, B], F32))
        gbT = sb(nc.sbuf_tensor("gbT", [128, 8, B], F32))
        sigf = sb(nc.sbuf_tensor("sigf", [1, 8], F32))
        scl = sb(nc.sbuf_tensor("scl", [128, 4], F32))
        gacc = sb(nc.sbuf_tensor("gacc", [128, 2], F32))
        gmr = sb(nc.sbuf_tensor("gmr", [1, 128], F32))
        r16 = sb(nc.sbuf_tensor("r16", [1, 16], F32))

        pk = [sb(nc.psum_tensor(f"pk{i}", [128, 512], F32)) for i in range(7)]

        mid = top.enter_context(ExitStack())
        cbr = [mid.enter_context(nc.sbuf_tensor(f"cbr{b}", [1, D], BF16))
               for b in range(B)]
        sbr = [mid.enter_context(nc.sbuf_tensor(f"sbr{b}", [1, D], BF16))
               for b in range(B)]
        gar = [mid.enter_context(nc.sbuf_tensor(f"gar{b}", [1, D], BF16))
               for b in range(B)]
        beff = [mid.enter_context(nc.sbuf_tensor(f"beff{b}", [1, 3 * D], BF16))
                for b in range(B)]

        ring = top.enter_context(ExitStack())
        rng = ring.enter_context
        xt = [rng(nc.sbuf_tensor(f"xt{i}", [128, D], F32)) for i in range(2)]
        stt = [rng(nc.sbuf_tensor(f"stt{i}", [128, 16], F32)) for i in range(2)]
        stt_b = [rng(nc.sbuf_tensor(f"stt{i}", [128, 16], F32))
                 for i in range(2, 4)]
        h1t = [rng(nc.sbuf_tensor(f"h1t{i}", [128, D], BF16)) for i in range(2)]
        ht = [h1t[0].ap().bitcast(F16), h1t[1].ap().bitcast(F16)]

        def make_ln_chain(bs, seq, second, src_x, spill, xt_, stt_, out_, ht_,
                          SVV, SSC, SDX, SDH, extra_wait=None):
            spt, vpt = 3, (3 if second else 2)

            def g_i(g, i):
                if extra_wait is not None:
                    extra_wait(g, i)
                if i >= 2:
                    g.wait_ge(H[SSC], bs[SSC] + spt * (i - 2) + 3)
                g.dma_start(out=xt_[i % 2], in_=src_x(i)).then_inc(
                    H[SDX + str(i % 2)], 16)

            def v_i(v, i):
                st = stt_[i % 2]
                v.wait_ge(H[SDX + str(i % 2)], bs[SDX + str(i % 2)]
                          + 16 * (i // 2 + 1))
                if i >= 2:
                    v.wait_ge(H[SSC], bs[SSC] + spt * (i - 2) + 3)
                v.reduce_sum(st[:, 0:1], xt_[i % 2], AX)
                v.wait_ge(H[SSC], bs[SSC] + spt * i + 1)
                v.tensor_tensor(st[:, 1:2], st[:, 11:12], st[:, 12:13], ALU.add)
                v.drain()
                v.tensor_scalar(st[:, 2:3], st[:, 0:1], 1.0 / D, None, ALU.mult)
                v.tensor_scalar(st[:, 3:4], st[:, 1:2], 1.0 / D, None, ALU.mult)
                v.drain()
                v.tensor_tensor(st[:, 4:5], st[:, 2:3], st[:, 2:3], ALU.mult)
                v.drain()
                v.tensor_tensor(st[:, 5:6], st[:, 3:4], st[:, 4:5], ALU.subtract)
                v.drain().then_inc(H[SVV], 1)
                v.wait_ge(H[SSC], bs[SSC] + spt * i + 2)
                v.tensor_tensor(st[:, 8:9], st[:, 2:3], st[:, 7:8], ALU.mult)
                v.drain()
                v.tensor_scalar(st[:, 9:10], st[:, 8:9], -1.0, None, ALU.mult)
                v.drain().then_inc(H[SVV], 1)
                if second:
                    v.wait_ge(H[SSC], bs[SSC] + spt * i + 3)
                    v.wait_ge(H[SDH + str(i % 2)], bs[SDH + str(i % 2)]
                              + (16 * ((i - 2) // 2 + 1) if i >= 2 else 0))
                    v.tensor_tensor(out_[i % 2], out_[i % 2],
                                    cb1rep[seq].ap()[:, :], ALU.mult)
                    v.drain()
                    v.tensor_tensor(ht_[i % 2], out_[i % 2],
                                    sbrep[seq].ap()[:, :], ALU.add)
                    v.drain()
                    v.tensor_reduce(st[:, 10:11], ht_[i % 2], AX,
                                    ALU.max, apply_absolute_value=True)
                    v.drain()
                    v.tensor_tensor(gacc[:, 0:1], gacc[:, 0:1],
                                    st[:, 10:11], ALU.max)
                    v.drain().then_inc(H[SVV], 1)

            def s_i(s, i):
                st = stt_[i % 2]
                s.wait_ge(H[SDX + str(i % 2)], bs[SDX + str(i % 2)]
                          + 16 * (i // 2 + 1))
                if not second:
                    s.wait_ge(H[SDH + str(i % 2)], bs[SDH + str(i % 2)]
                              + (16 * ((i - 2) // 2 + 1) if i >= 2 else 0))
                if second and i >= 2:
                    s.wait_ge(H[SVV], bs[SVV] + vpt * (i - 2) + 3)
                s.drain()
                s.activation(pk[6][:, :], xt_[i % 2][:, 0:512], ACT.Square,
                             accum_out=st[:, 11:12])
                s.drain()
                s.activation(pk[6][:, :], xt_[i % 2][:, 512:1024], ACT.Square,
                             accum_out=st[:, 12:13]).then_inc(H[SSC], 1)
                s.wait_ge(H[SVV], bs[SVV] + vpt * i + 1)
                s.activation(st[:, 6:7], st[:, 5:6], ACT.Ln,
                             bias=zc[:, 1:2], scale=1.0)
                s.drain()
                s.activation(st[:, 7:8], st[:, 6:7], ACT.Exp,
                             bias=zc[:, 0:1], scale=-0.5).then_inc(H[SSC], 1)
                s.wait_ge(H[SVV], bs[SVV] + vpt * i + 2)
                s.drain()
                s.activation(out_[i % 2], xt_[i % 2], ACT.Identity,
                             bias=st[:, 9:10], scale=st[:, 7:8]).then_inc(H[SSC], 1)

            def sy_i(sy, i):
                if second:
                    sy.wait_ge(H[SVV], bs[SVV] + vpt * i + 3)
                    srcap = ht_[i % 2]
                else:
                    sy.wait_ge(H[SSC], bs[SSC] + spt * i + 3)
                    srcap = out_[i % 2]
                sy.dma_start(out=spill(i), in_=srcap).then_inc(
                    H[SDH + str(i % 2)], 16)

            def done():
                V[SVV] += vpt * NT
                V[SSC] += spt * NT
                V[SDX + "0"] += NT * 8
                V[SDX + "1"] += NT * 8
                V[SDH + "0"] += NT * 8
                V[SDH + "1"] += NT * 8

            def tg(eng):
                eng.wait_ge(H[SSC], bs[SSC] + spt * NT)
                eng.wait_ge(H[SDH + "0"], bs[SDH + "0"] + NT * 8)
                eng.wait_ge(H[SDH + "1"], bs[SDH + "1"] + NT * 8)

            return g_i, v_i, s_i, sy_i, done, tg


        with (
            nc.sbuf_tensor("scrow", [B, D], F32) as scrow,
            nc.sbuf_tensor("scb", [B, D], BF16) as scb,
            nc.sbuf_tensor("scT", [128, 8, B], BF16) as scT,
            nc.sbuf_tensor("saT", [128, 8, B], BF16) as saT,
            nc.sbuf_tensor("modsb", [B, 6 * D], F32) as modsb,
            nc.psum_tensor("pT", [128, 128], BF16) as pT,
        ):
            with nc.Block() as blk:
                bs = snap()

                @blk.gpsimd
                def _(g):
                    for dst, src in [(idbt[:, :], idb), (idft[:, :], idf),
                                     (maskt[:, :], maskc), (scrow[:, :], c_in),
                                     (sigf[0:1, 0:2], clips)]:
                        g.dma_start(out=dst, in_=src).then_inc(H["dg"], 16)
                    V["dg"] += 5 * 16

                @blk.vector
                def _(v):
                    v.memset(zc[:, 0:1], 0.0)
                    v.memset(zc[:, 1:2], 1.0e-5)
                    v.memset(ones32[:, :], 1.0)
                    v.memset(onesb[:, :], 1.0)
                    v.memset(gacc[:, :], 0.0)
                    v.drain().then_inc(H["vv"], 1)
                    v.wait_ge(H["sc"], bs["sc"] + 1)
                    v.tensor_copy(scb[:, :], scrow[:, :])
                    v.wait_ge(H["sc"], bs["sc"] + 2)
                    v.tensor_scalar(sigf[0:1, 2:4], sigf[0:1, 2:4], 1.0, None, ALU.add)
                    v.drain()
                    v.reciprocal(sigf[0:1, 2:4], sigf[0:1, 2:4])
                    v.drain()
                    v.tensor_scalar(sigf[0:1, 2:4], sigf[0:1, 2:4],
                                    1.0 / 127.0, None, ALU.mult)
                    v.drain().then_inc(H["vv"], 1)
                    V["vv"] += 2

                @blk.scalar
                def _(s):
                    load_set(s, "silu_and_others")
                    s.wait_ge(H["dg"], bs["dg"] + 5 * 16)
                    s.activation(scrow[:, :], scrow[:, :], ACT.Silu).then_inc(H["sc"], 1)
                    load_set(s, "natural_log_exp_and_others")
                    s.wait_ge(H["vv"], bs["vv"] + 1)
                    s.activation(sigf[0:1, 2:4], sigf[0:1, 0:2], ACT.Exp,
                                 bias=zc[0:1, 0:1], scale=-1.0).then_inc(H["sc"], 1)
                    V["sc"] += 2

            if STAGE == 101:
                return nc
            ln1_0 = None
            if 2 <= STAGE < 100:
                ln1_0 = make_ln_chain(
                    snap(), 0, False,
                    lambda i: x_in[0, i * 128:(i + 1) * 128, :],
                    lambda i: h1s[i * 128:(i + 1) * 128, :],
                    [xt[0].ap()[:, :], xt[1].ap()[:, :]],
                    [stt[0].ap()[:, :], stt[1].ap()[:, :]],
                    [h1t[0].ap()[:, :], h1t[1].ap()[:, :]], None,
                    "vv2", "sc2", "dlxC", "dh1")
            with nc.sbuf_tensor("wmodsb", [128, 8, 3 * D], BF16) as wmodsb:
                for half in range(1 if STAGE in (1016, 1017) else 2):
                    with nc.Block() as blk:
                        bs = snap()

                        @blk.gpsimd
                        def _(g, half=half):
                            for dc in range(8):
                                g.dma_start(
                                    out=wmodsb[:, dc, :],
                                    in_=wmod[dc * 128:(dc + 1) * 128,
                                             half * 3 * D:(half + 1) * 3 * D],
                                ).then_inc(H["dg"], 16)
                            V["dg"] += 8 * 16

                        @blk.scalar
                        def _(s, half=half):
                            if ln1_0:
                                for j in range(8):
                                    ln1_0[2](s, 8 * half + j)

                        @blk.sync
                        def _(sy, half=half):
                            if ln1_0:
                                if half == 0:
                                    ln1_0[0](sy, 0)
                                    ln1_0[0](sy, 1)
                                for j in range(8 * half, 8 * half + 8):
                                    ln1_0[3](sy, j)
                                    if j + 2 <= 15:
                                        ln1_0[0](sy, j + 2)

                        @blk.tensor
                        def _(t, half=half, bs=bs):
                            if half == 0:
                                for dc in range(8):
                                    if dc > 0:
                                        t.wait_ge(H["vv"], bs["vv"] + dc)
                                    t.transpose(pT[:, 0:B],
                                                scb[0:B, dc * 128:(dc + 1) * 128],
                                                idbt[0:B, 0:B]).then_inc(H["tt"], 1)
                                V["tt"] += 8
                            t.wait_ge(H["dg"], bs["dg"] + 8 * 16)
                            nb = bs["vv"] + (8 if half == 0 else 0)
                            t.wait_ge(H["vv"], nb)
                            for nch in range(0 if STAGE == 1015 else (1 if STAGE == 1017 else 6)):
                                if nch >= 2:
                                    t.wait_ge(H["vv"], nb + nch - 1)
                                for dc in range(8):
                                    t.matmul(pk[nch % 2][0:B, :], scT[:, dc, :],
                                             wmodsb[:, dc, nch * 512:(nch + 1) * 512],
                                             start=(dc == 0), stop=(dc == 7))
                                t.drain().then_inc(H["tt"], 1)
                            V["tt"] += 0 if STAGE == 1015 else (1 if STAGE == 1017 else 6)

                        @blk.vector
                        def _(v, half=half, bs=bs):
                            tb = bs["tt"]
                            if half == 0:
                                for dc in range(8):
                                    v.wait_ge(H["tt"], bs["tt"] + dc + 1)
                                    v.tensor_copy(scT[:, dc, :], pT[:, 0:B])
                                    v.drain().then_inc(H["vv"], 1)
                                V["vv"] += 8
                                tb = bs["tt"] + 8
                            for nch in range(0 if STAGE == 1015 else 6):
                                v.wait_ge(H["tt"], tb + nch + 1)
                                off = half * 3 * D + nch * 512
                                v.tensor_copy(modsb[0:B, off:off + 512],
                                              pk[nch % 2][0:B, :])
                                v.drain().then_inc(H["vv"], 1)
                                if ln1_0:
                                    ln1_0[1](v, 8 * half + nch)
                            if ln1_0:
                                ln1_0[1](v, 8 * half + 6)
                                ln1_0[1](v, 8 * half + 7)
                            V["vv"] += 0 if STAGE == 1015 else (1 if STAGE == 1017 else 6)

            if ln1_0:
                ln1_0[4]()
            if STAGE in (102, 1015, 1016, 1017):
                return nc
            with (
                nc.sbuf_tensor("tmpg", [B, D], F32) as tmpg,
                nc.sbuf_tensor("tmpa", [B, D], F32) as tmpa,
                nc.sbuf_tensor("sab", [B, D], BF16) as sab,
                nc.Block() as blk,
            ):
                bs = snap()

                @blk.scalar
                def _(s):
                    s.activation(tmpa[:, :], modsb[0:B, 2 * D:3 * D],
                                 ACT.Abs).then_inc(H["sc"], 1)
                    s.wait_ge(H["vv"], bs["vv"] + 1)
                    s.activation(tmpa[:, :], modsb[0:B, 5 * D:6 * D],
                                 ACT.Abs).then_inc(H["sc"], 1)
                    V["sc"] += 2

                @blk.vector
                def _(v):
                    v.tensor_scalar(modsb[0:B, D:2 * D], modsb[0:B, D:2 * D],
                                    1.0, None, ALU.add)
                    v.tensor_scalar(modsb[0:B, 4 * D:5 * D], modsb[0:B, 4 * D:5 * D],
                                    1.0, None, ALU.add)
                    for gi, off in enumerate((2 * D, 5 * D)):
                        sl = modsb[0:B, off:off + D]
                        v.tensor_scalar(tmpg[:, :], sl, 0.0, 2.0, ALU.is_ge, ALU.mult)
                        v.drain()
                        v.tensor_scalar(tmpg[:, :], tmpg[:, :], -1.0, None, ALU.add)
                        v.wait_ge(H["sc"], bs["sc"] + gi + 1)
                        v.tensor_scalar(tmpa[:, :], tmpa[:, :], 0.25, None, ALU.max)
                        v.drain()
                        v.tensor_tensor(sl, tmpg[:, :], tmpa[:, :], ALU.mult)
                        v.drain().then_inc(H["vv"], 1)
                    v.tensor_copy(sab[:, :], modsb[0:B, 0:D])
                    v.drain().then_inc(H["vv"], 1)
                    V["vv"] += 3
                    for dc in range(8):
                        v.wait_ge(H["tt"], bs["tt"] + 3 * dc + 1)
                        v.tensor_copy(saT[:, dc, :], pT[:, 0:B])
                        v.drain().then_inc(H["vv"], 1)
                        v.wait_ge(H["tt"], bs["tt"] + 3 * dc + 2)
                        v.tensor_copy(ca1T[:, dc, :], pk[6][0:128, 0:B])
                        v.drain().then_inc(H["vv"], 1)
                        v.wait_ge(H["tt"], bs["tt"] + 3 * dc + 3)
                        v.tensor_copy(gbT[:, dc, :], pk[5][0:128, 0:B])
                        v.drain().then_inc(H["vv"], 1)

                    V["vv"] += 24

                @blk.gpsimd
                def _(g):
                    g.wait_ge(H["vv"], bs["vv"] + 3)
                    for b in range(B):
                        g.dma_start(out=gar[b].ap()[0:1, :],
                                    in_=modsb[b:b + 1, 2 * D:3 * D]
                                    ).then_inc(H["dg"], 16)
                        g.dma_start(out=cbr[b].ap()[0:1, :],
                                    in_=modsb[b:b + 1, 4 * D:5 * D]
                                    ).then_inc(H["dg"], 16)
                        g.dma_start(out=sbr[b].ap()[0:1, :],
                                    in_=modsb[b:b + 1, 3 * D:4 * D]
                                    ).then_inc(H["dg"], 16)
                    V["dg"] += 6 * 16

                @blk.tensor
                def _(t):
                    t.wait_ge(H["vv"], bs["vv"] + 3)
                    for dc in range(8):
                        if dc > 0:
                            t.wait_ge(H["vv"], bs["vv"] + 3 + 3 * dc)
                        t.transpose(pT[:, 0:B], sab[0:B, dc * 128:(dc + 1) * 128],
                                    idbt[0:B, 0:B]).then_inc(H["tt"], 1)
                        t.transpose(pk[6][0:128, 0:B],
                                    modsb[0:B, D + dc * 128:D + (dc + 1) * 128],
                                    idft[0:B, 0:B]).then_inc(H["tt"], 1)
                        t.transpose(pk[5][0:128, 0:B],
                                    modsb[0:B, 5 * D + dc * 128:
                                          5 * D + (dc + 1) * 128],
                                    idft[0:B, 0:B]).then_inc(H["tt"], 1)
                    V["tt"] += 24

            if STAGE == 103:
                return nc
            with (
                nc.sbuf_tensor("wqsb", [128, 8, 512], BF16) as wqsb,
                nc.sbuf_tensor("befr", [B, 3 * D], F32) as befr,
                nc.sbuf_tensor("brow", [B, 3 * D], F32) as brow,
                nc.sbuf_tensor("befb", [B, 3 * D], BF16) as befb,
                nc.Block() as blk,
            ):
                bs = snap()

                @blk.gpsimd
                def _(g):
                    for nch in range(6):
                        if nch >= 1:
                            g.wait_ge(H["tt"], bs["tt"] + nch)
                        for dc in range(8):
                            g.dma_start(
                                out=wqsb[:, dc, :],
                                in_=aw[dc * 128:(dc + 1) * 128,
                                       nch * 512:(nch + 1) * 512],
                            ).then_inc(H["dg"], 16)
                    V["dg"] += 6 * 8 * 16

                @blk.tensor
                def _(t):
                    for nch in range(6):
                        t.wait_ge(H["dg"], bs["dg"] + (nch + 1) * 8 * 16)
                        if nch >= 2:
                            t.wait_ge(H["vv"], bs["vv"] + nch - 1)
                        for dc in range(8):
                            t.matmul(pk[nch % 2][0:B, :], saT[:, dc, :],
                                     wqsb[:, dc, :],
                                     start=(dc == 0), stop=(dc == 7))
                        t.drain().then_inc(H["tt"], 1)
                    V["tt"] += 6

                @blk.vector
                def _(v):
                    for nch in range(6):
                        v.wait_ge(H["tt"], bs["tt"] + nch + 1)
                        v.tensor_copy(befr[0:B, nch * 512:(nch + 1) * 512],
                                      pk[nch % 2][0:B, :])
                        v.drain().then_inc(H["vv"], 1)
                    v.wait_ge(H["dsy"], bs["dsy"] + 32)
                    v.drain()
                    v.tensor_tensor(befr[0:B, :], befr[0:B, :], brow[0:B, :], ALU.add)
                    v.drain()
                    v.tensor_tensor(befr[0:B, 2 * D:3 * D], befr[0:B, 2 * D:3 * D],
                                    modsb[0:B, 2 * D:3 * D], ALU.mult)
                    v.drain()
                    v.tensor_copy(befb[0:B, :], befr[0:B, :])
                    v.drain().then_inc(H["vv"], 1)
                    V["vv"] += 7

                @blk.sync
                def _(sy):
                    sy.dma_start(out=brow[0:1, :], in_=ab).then_inc(H["dsy"], 16)
                    sy.dma_start(out=brow[1:2, :], in_=ab).then_inc(H["dsy"], 16)
                    V["dsy"] += 32
                    sy.wait_ge(H["vv"], bs["vv"] + 7)
                    for b in range(B):
                        sy.dma_start(out=beff[b].ap()[0:1, :],
                                     in_=befb[b:b + 1, :]).then_inc(H["dsy"], 16)
                    V["dsy"] += 32

        if STAGE <= 1:
            return nc

        def ln_block(seq, src_x, spill, second):
            with nc.Block() as blk:
                bs = snap()
                xt_ = [xt[0][:, :], xt[1][:, :]]
                stt_ = [stt[0][:, :], stt[1][:, :]]
                out_ = ([nrm[0][:, :], nrm[1][:, :]] if second
                        else [h1t[0][:, :], h1t[1][:, :]])
                ht2 = [ht[0][:, 0:D], ht[1][:, 0:D]] if second else None
                g_i, v_i, s_i, sy_i, done, _tg = make_ln_chain(
                    bs, seq, second, src_x, spill, xt_, stt_, out_, ht2,
                    "vv", "sc", "dgx", "dh1")

                @blk.gpsimd
                def _(g):
                    for i in range(NT):
                        g_i(g, i)

                @blk.vector
                def _(v):
                    for i in range(NT):
                        v_i(v, i)

                @blk.scalar
                def _(s):
                    for i in range(NT):
                        s_i(s, i)

                @blk.sync
                def _(sy):
                    for i in range(NT):
                        sy_i(sy, i)
                done()

        def make_wprep_chain(bs, tiles):
            def g_k(g, k):
                src_, _dst, j = tiles[k]
                if k >= 2:
                    g.wait_ge(H["scw"], bs["scw"] + (k - 2) + 1)
                g.dma_start(out=wpt[k % 2],
                            in_=src_[j * 128:(j + 1) * 128, :]
                            ).then_inc(H[f"dwtS{k % 2}"], 16)

            def v_k(v, k):
                st = wpst[k % 2]
                v.wait_ge(H[f"dwtS{k % 2}"],
                          bs[f"dwtS{k % 2}"] + 16 * (k // 2 + 1))
                v.tensor_reduce(st[:, 0:1], wpt[k % 2], AX, ALU.max,
                                apply_absolute_value=True)
                v.drain()
                v.tensor_scalar(st[:, 1:2], st[:, 0:1], 1.0 / 127.0, None, ALU.mult)
                v.drain()
                v.reciprocal(st[:, 2:3], st[:, 1:2])
                v.drain().then_inc(H["vvw"], 1)
                v.wait_ge(H["scw"], bs["scw"] + k + 1)
                v.wait_ge(H[f"dwq{k % 2}"], bs[f"dwq{k % 2}"]
                          + (16 * ((k - 2) // 2 + 1) if k >= 2 else 0))
                v.tensor_copy(wpi[:, :], wptm[:, :])
                v.drain()
                v.tensor_scalar(wpq[k % 2], wpi[:, :], st[:, 1:2],
                                None, ALU.mult)
                v.drain().then_inc(H["vvw"], 1)

            def s_k(s, k):
                s.wait_ge(H["vvw"], bs["vvw"] + 2 * k + 1)
                if k >= 1:
                    s.wait_ge(H["vvw"], bs["vvw"] + 2 * (k - 1) + 2)
                s.drain()
                s.activation(wptm[:, :], wpt[k % 2], ACT.Identity,
                             bias=zc[:, 0:1],
                             scale=wpst[k % 2][:, 2:3]).then_inc(H["scw"], 1)

            def sy_k(sy, k):
                _src, dst, j = tiles[k]
                sy.wait_ge(H["vvw"], bs["vvw"] + 2 * k + 2)
                sy.dma_start(out=dst[j * 128:(j + 1) * 128, :],
                             in_=wpq[k % 2]).then_inc(H[f"dwq{k % 2}"], 16)

            def done(n):
                V["vvw"] += 2 * n
                V["scw"] += n
                V["dwtS0"] += ((n + 1) // 2) * 16
                V["dwtS1"] += (n // 2) * 16
                V["dwq0"] += ((n + 1) // 2) * 16
                V["dwq1"] += (n // 2) * 16

            return g_k, v_k, s_k, sy_k, done

        def emit_seq(seq, upto=4, wprep_tiles=None, attn_extras=(),
                     qkv_extras=()):
            with nc.Block() as blk:
                bs = snap()

                @blk.scalar
                def _(s):
                    for nm in ("dh10", "dh11", "dh20", "dh21"):
                        s.wait_ge(H[nm], bs[nm])
                    for gch in range(4):
                        for dc in range(8):
                            s.dma_start(
                                out=h1T[:, dc, gch * 512:(gch + 1) * 512],
                                in_=h1s[gch * 512:(gch + 1) * 512,
                                        dc * 128:(dc + 1) * 128],
                                transpose=True,
                            ).then_inc(H["dsc"], 16)
                    V["dsc"] += 32 * 16

                @blk.vector
                def _(v):
                    v.wait_ge(H["dsc"], bs["dsc"] + 32 * 16)
                    for dc in range(8):
                        v.tensor_scalar(h1T[:, dc, :], h1T[:, dc, :],
                                        ca1T[:, dc, seq:seq + 1], None, ALU.mult)
                    v.drain().then_inc(H["vv"], 1)
                    V["vv"] += 1
            if upto <= 2:
                return

            with (
                nc.sbuf_tensor(f"woc{seq}", [128, 2, 8, 128], BF16) as woc,
                nc.sbuf_tensor(f"wvv{seq}", [128, 1, D], BF16) as wvv,
                nc.sbuf_tensor(f"garep{seq}", [128, D], BF16) as garep,
                nc.Block() as blk,
            ):
                bs = snap()
                nwp = len(wprep_tiles) if wprep_tiles else 0
                if nwp:
                    wg_k, wv_k, ws_k, wsy_k, wdone = make_wprep_chain(bs, wprep_tiles)
                qchains = [f(bs) for f in qkv_extras]
                alloc = [[] for _ in range(16)]
                for k in range(nwp):
                    alloc[k % 16].append(k) if False else None
                kk = 0
                for oc in range(16):
                    take = (nwp + 15 - oc) // 16
                    alloc[oc] = list(range(kk, kk + take))
                    kk += take

                @blk.gpsimd
                def _(g):
                    g.wait_ge(H["vv2"], bs["vv2"])
                    g.wait_ge(H["sc2"], bs["sc2"])
                    for dc in range(8):
                        if dc >= 1:
                            g.wait_ge(H["vv"], bs["vv"] + dc + 2)
                        g.dma_start(
                            out=wvv[:, 0, :],
                            in_=aw[dc * 128:(dc + 1) * 128, 2 * D:3 * D],
                        ).then_inc(H[f"dwv{dc % 2}"], 16)
                    for oc in range(16):
                        if oc >= 2:
                            g.wait_ge(H["tt"], bs["tt"] + oc + 1)
                        for dc in range(8):
                            g.dma_start(
                                out=woc[:, oc % 2, dc, :],
                                in_=aw[dc * 128:(dc + 1) * 128,
                                       oc * 128:(oc + 1) * 128],
                            ).then_inc(H[f"dwo{oc % 2}"], 16)
                    V["dwv0"] += 4 * 16
                    V["dwv1"] += 4 * 16
                    V["dwo0"] += 8 * 128
                    V["dwo1"] += 8 * 128

                @blk.tensor
                def _(t):
                    t.wait_ge(H["dsy"], bs["dsy"])
                    for hch in range(2):
                        t.matmul(pk[hch][:, :], onesb[0:1, 0:128],
                                 gar[seq].ap()[0:1, hch * 512:(hch + 1) * 512],
                                 start=True, stop=True)
                        t.drain().then_inc(H["tt"], 1)
                    for oc in range(16):
                        t.wait_ge(H[f"dwo{oc % 2}"],
                                  bs[f"dwo{oc % 2}"] + 128 * (oc // 2 + 1))
                        if oc >= 1:
                            t.wait_ge(H["vv"], bs["vv"] + 11 + 4 * oc)
                        for dc in range(8):
                            for gch in range(4):
                                t.matmul(pk[gch][:, :], woc[:, oc % 2, dc, :],
                                         h1T[:, dc, gch * 512:(gch + 1) * 512],
                                         start=(dc == 0), stop=False)
                        for gch in range(4):
                            t.matmul(pk[gch][:, :],
                                     beff[seq].ap()[0:1, oc * 128:(oc + 1) * 128],
                                     onesb[0:1, :], start=False, stop=True)
                        t.drain().then_inc(H["tt"], 1)
                    for ti in range(NT):
                        t.wait_ge(H["vv"], bs["vv"] + 11 + 64)
                        if ti >= 1:
                            t.wait_ge(H["vv"], bs["vv"] + 11 + 64 + 2 * ti)
                        for dc in range(8):
                            t.matmul(pk[4][:, :],
                                     h1T[:, dc, ti * 128:(ti + 1) * 128],
                                     wvs[:, dc, 0:512], start=(dc == 0), stop=False)
                            t.matmul(pk[5][:, :],
                                     h1T[:, dc, ti * 128:(ti + 1) * 128],
                                     wvs[:, dc, 512:1024], start=(dc == 0), stop=False)
                        t.matmul(pk[4][:, :], onesb[0:1, 0:128],
                                 beff[seq].ap()[0:1, 2 * D:2 * D + 512],
                                 start=False, stop=True)
                        t.matmul(pk[5][:, :], onesb[0:1, 0:128],
                                 beff[seq].ap()[0:1, 2 * D + 512:3 * D],
                                 start=False, stop=True)
                        t.drain().then_inc(H["tt"], 1)
                    V["tt"] += 2 + 16 + NT

                @blk.scalar
                def _(s):
                    s.wait_ge(H["dou0"], bs["dou0"])
                    s.wait_ge(H["dou1"], bs["dou1"])
                    s.wait_ge(H["sc2"], bs["sc2"])
                    s.wait_ge(H["dh20"], bs["dh20"])
                    s.wait_ge(H["dh21"], bs["dh21"])
                    for oc in range(16):
                        for k in alloc[oc]:
                            wg_k(s, k)
                        for ch in qchains:
                            ch[0](s, oc)
                        for k in alloc[oc]:
                            ws_k(s, k)
                        if oc >= 1:
                            for ch in qchains:
                                ch[2](s, oc - 1)
                    for ch in qchains:
                        ch[2](s, 15)

                @blk.sync
                def _(sy):
                    for nm in ("dsc", "dh10", "dh11", "dh20", "dh21"):
                        sy.wait_ge(H[nm], bs[nm])
                    for oc in range(16):
                        for k in alloc[oc]:
                            wsy_k(sy, k)
                        if oc >= 1:
                            for ch in qchains:
                                ch[3](sy, oc - 1)
                    for ch in qchains:
                        ch[3](sy, 15)
                if nwp:
                    wdone(nwp)
                for ch in qchains:
                    ch[4]()

                @blk.vector
                def _(v):
                    v.wait_ge(H["vv2"], bs["vv2"])
                    v.wait_ge(H["sc2"], bs["sc2"])
                    v.wait_ge(H["dh20"], bs["dh20"])
                    v.wait_ge(H["dh21"], bs["dh21"])
                    for hch in range(2):
                        v.wait_ge(H["tt"], bs["tt"] + hch + 1)
                        v.tensor_copy(garep[:, hch * 512:(hch + 1) * 512],
                                      pk[hch][:, :])
                        v.drain().then_inc(H["vv"], 1)
                    v.drain()
                    for dc in range(8):
                        v.wait_ge(H[f"dwv{dc % 2}"],
                                  bs[f"dwv{dc % 2}"] + 16 * (dc // 2 + 1))
                        v.tensor_tensor(wvs[:, dc, :], wvv[:, 0, :],
                                        garep[:, :], ALU.mult)
                        v.drain().then_inc(H["vv"], 1)
                    v.drain().then_inc(H["vv"], 1)
                    for oc in range(16):
                        v.wait_ge(H["tt"], bs["tt"] + 2 + oc + 1)
                        for gch in range(4):
                            v.tensor_copy(qkT[:, oc, gch * 512:(gch + 1) * 512],
                                          pk[gch][:, :])
                            v.drain().then_inc(H["vv"], 1)
                        for k in alloc[oc]:
                            wv_k(v, k)
                        if oc >= 1:
                            for ch in qchains:
                                ch[1](v, oc - 1)
                    for ch in qchains:
                        ch[1](v, 15)
                    for ti in range(NT):
                        if ti == 6:
                            for ch in qchains:
                                ch[5](v)
                        v.wait_ge(H["tt"], bs["tt"] + 2 + 16 + ti + 1)
                        v.tensor_copy(vsb[:, ti, 0:512], pk[4][:, :])
                        v.drain().then_inc(H["vv"], 1)
                        v.tensor_copy(vsb[:, ti, 512:1024], pk[5][:, :])
                        v.drain().then_inc(H["vv"], 1)
                    V["vv"] += 11 + 64 + 2 * NT
            if upto <= 3:
                return

            with (
                nc.sbuf_tensor(f"Pt{seq}", [128, 2, T], BF16) as Pt,
                nc.Block() as blk,
            ):
                bs = snap()
                x1t = nrm
                rst = idft.ap()[:, 0:32].rearrange("p (a b) -> p a b", b=16)
                h1b_ = h1T.ap().rearrange("p a b -> p (a b)")
                if seq == 0:
                    PTs = [h1b_[:, 6400:8448], h1b_[:, 8448:10496]]
                else:
                    PTs = [h1b_[:, 10240:12288], h1b_[:, 14336:16384]]
                PTt = [p.rearrange("p (a b) -> p a b", b=128) for p in PTs]

                tts = {0: 1}
                ttpv = {}
                pos = 1
                for qt in range(1, NT):
                    pos += 1
                    tts[qt] = pos
                    pos += 1
                    ttpv[qt - 1] = pos
                pos += 1
                ttpv[NT - 1] = pos
                ne = [qt // 2 + 1 for qt in range(NT)]
                no = [(qt + 1) // 2 for qt in range(NT)]
                cum_e = {}
                cum_o = {}
                tot = {0: [0, 0], 1: [0, 0]}
                for qt in range(NT):
                    p = qt % 2
                    tot[p][0] += ne[qt]
                    tot[p][1] += no[qt]
                    cum_e[qt] = tot[p][0]
                    cum_o[qt] = tot[p][1]

                chains = [f(bs) for f in attn_extras]

                @blk.gpsimd
                def _(g):
                    for qt in range(NT):
                        if qt >= 2:
                            g.wait_ge(H["vv"], bs["vv"] + 3 * (qt - 2) + 3)
                        g.dma_start(out=xt[qt % 2][:, :],
                                    in_=x_in[seq, qt * 128:(qt + 1) * 128, :]
                                    ).then_inc(H[f"dgx{qt % 2}"], 16)
                        for ch in chains:
                            ch[0](g, qt)
                    V["dgx0"] += NT * 8
                    V["dgx1"] += NT * 8

                @blk.tensor
                def _(t):
                    def emit_S(qt):
                        nk = NKC[qt]
                        if qt >= 1:
                            t.wait_ge(H["sc"], bs["sc"] + qt)
                        for dc in range(8):
                            for kc in range(nk):
                                w = KCW[qt][kc]
                                t.matmul(pk[kc][:, 0:w],
                                         qkT[:, dc, qt * 128:(qt + 1) * 128],
                                         qkT[:, 8 + dc, kc * 512:kc * 512 + w],
                                         start=(dc == 0), stop=(dc == 7))
                        t.drain().then_inc(H["tt"], 1)

                    def emit_PV(qt):
                        t.wait_ge(H[f"dpt{qt % 2}"],
                                  bs[f"dpt{qt % 2}"] + 16 * cum_e[qt])
                        t.wait_ge(H[f"dpts{qt % 2}"],
                                  bs[f"dpts{qt % 2}"] + 16 * cum_o[qt])
                        if qt >= 1:
                            t.wait_ge(H["vv"], bs["vv"] + 3 * (qt - 1) + 3)
                        for kc in range(qt + 1):
                            t.matmul(pk[4][:, :], PTt[qt % 2][:, kc, :],
                                     vsb[:, kc, 0:512],
                                     start=(kc == 0), stop=(kc == qt))
                            t.matmul(pk[5][:, :], PTt[qt % 2][:, kc, :],
                                     vsb[:, kc, 512:1024],
                                     start=(kc == 0), stop=(kc == qt))
                        t.drain().then_inc(H["tt"], 1)

                    emit_S(0)
                    for qt in range(1, NT):
                        emit_S(qt)
                        emit_PV(qt - 1)
                    emit_PV(NT - 1)
                    V["tt"] += 2 * NT

                @blk.vector
                def _(v):
                    v.wait_ge(H["vv2"], bs["vv2"])
                    v.wait_ge(H["sc2"], bs["sc2"])
                    for qt in range(NT):
                        nk = NKC[qt]
                        st = rst[:, qt % 2, :]
                        v.wait_ge(H["tt"], bs["tt"] + tts[qt])
                        if qt >= 2:
                            v.wait_ge(H["sc"], bs["sc"] + qt - 1)
                        dcol = qt * 128 - (nk - 1) * 512
                        v.tensor_tensor(pk[nk - 1][:, dcol:dcol + 128],
                                        pk[nk - 1][:, dcol:dcol + 128],
                                        maskt[:, :], ALU.add)
                        v.drain()
                        for kc in range(nk):
                            v.reduce_max(st[:, kc:kc + 1],
                                         pk[kc][:, 0:KCW[qt][kc]], AX)
                        v.drain()
                        v.reduce_max(st[:, 4:5], st[:, 0:nk], AX)
                        v.drain()
                        v.tensor_scalar(st[:, 5:6], st[:, 4:5], -1.0 / 32.0,
                                        None, ALU.mult)
                        v.drain().then_inc(H["vv"], 1)
                        v.wait_ge(H["sc"], bs["sc"] + qt + 1)
                        v.reduce_sum(st[:, 13:14], st[:, 8:8 + nk], AX)
                        v.drain()
                        v.reciprocal(st[:, 14:15], st[:, 13:14])
                        v.drain().then_inc(H["vv"], 1)
                        v.wait_ge(H["tt"], bs["tt"] + ttpv[qt])
                        v.wait_ge(H[f"dgx{qt % 2}"],
                                  bs[f"dgx{qt % 2}"] + 16 * (qt // 2 + 1))
                        if qt >= 2:
                            v.wait_ge(H[f"dou{qt % 2}"],
                                      bs[f"dou{qt % 2}"] + 16 * ((qt - 2) // 2 + 1))
                        else:
                            v.wait_ge(H[f"dou{qt % 2}"], bs[f"dou{qt % 2}"])
                        v.scalar_tensor_tensor(x1t[qt % 2][:, 0:512], pk[4][:, :],
                                               st[:, 14:15], xt[qt % 2][:, 0:512],
                                               ALU.mult, ALU.add)
                        v.scalar_tensor_tensor(x1t[qt % 2][:, 512:1024], pk[5][:, :],
                                               st[:, 14:15], xt[qt % 2][:, 512:1024],
                                               ALU.mult, ALU.add)
                        v.drain().then_inc(H["vv"], 1)
                        for ch in chains:
                            ch[1](v, qt)
                    V["vv"] += 3 * NT

                @blk.scalar
                def _(s):
                    s.wait_ge(H["vv2"], bs["vv2"])
                    s.wait_ge(H["sc2"], bs["sc2"])
                    for qt in range(NT):
                        nk = NKC[qt]
                        st = rst[:, qt % 2, :]
                        s.wait_ge(H["vv"], bs["vv"] + 3 * qt + 1)
                        if qt >= 2:
                            s.wait_ge(H[f"dps{qt % 2}"],
                                      bs[f"dps{qt % 2}"] + 16 * ((qt - 2) // 2 + 1))
                        for kc in range(nk):
                            w = KCW[qt][kc]
                            s.activation(Pt[:, qt % 2, kc * 512:kc * 512 + w],
                                         pk[kc][:, 0:w], ACT.Exp,
                                         bias=st[:, 5:6], scale=1.0 / 32.0,
                                         accum_out=st[:, 8 + kc:9 + kc])
                        s.drain().then_inc(H["sc"], 1)
                        s.wait_ge(H[f"dps{qt % 2}"],
                                  bs[f"dps{qt % 2}"] + 16 * (qt // 2 + 1))
                        if qt >= 2:
                            s.wait_ge(H["tt"], bs["tt"] + ttpv[qt - 2])
                        for kc in range(1, qt + 1, 2):
                            s.dma_start(out=PTt[qt % 2][:, kc, :],
                                        in_=pss[qt % 2, :, kc * 128:(kc + 1) * 128],
                                        transpose=True).then_inc(H[f"dpts{qt % 2}"], 16)
                        V[f"dpts{qt % 2}"] += 16 * no[qt]
                        for ch in chains:
                            ch[2](s, qt)
                    V["sc"] += NT

                @blk.sync
                def _(sy):
                    for qt in range(NT):
                        sy.wait_ge(H["sc"], bs["sc"] + qt + 1)
                        if qt >= 2:
                            sy.wait_ge(H["tt"], bs["tt"] + ttpv[qt - 2])
                            sy.wait_ge(H[f"dpts{qt % 2}"],
                                       bs[f"dpts{qt % 2}"] + 16 * cum_o.get(qt - 2, 0))
                            sy.wait_ge(H[f"dpt{qt % 2}"],
                                       bs[f"dpt{qt % 2}"] + 16 * cum_e.get(qt - 2, 0))
                        sy.dma_start(out=pss[qt % 2, :, 0:KEYS[qt]],
                                     in_=Pt[:, qt % 2, 0:KEYS[qt]]
                                     ).then_inc(H[f"dps{qt % 2}"], 16)
                        V[f"dps{qt % 2}"] += 16
                        sy.wait_ge(H[f"dps{qt % 2}"], V[f"dps{qt % 2}"])
                        for kc in range(0, qt + 1, 2):
                            sy.dma_start(out=PTt[qt % 2][:, kc, :],
                                         in_=pss[qt % 2, :, kc * 128:(kc + 1) * 128],
                                         transpose=True).then_inc(H[f"dpt{qt % 2}"], 16)
                        V[f"dpt{qt % 2}"] += 16 * ne[qt]
                        sy.wait_ge(H["vv"], bs["vv"] + 3 * qt + 3)
                        sy.dma_start(out=out[seq, qt * 128:(qt + 1) * 128, :],
                                     in_=x1t[qt % 2][:, :]).then_inc(H[f"dou{qt % 2}"], 16)
                        V[f"dou{qt % 2}"] += 16
                        for ch in chains:
                            ch[3](sy, qt)
                for ch in chains:
                    ch[4]()

        with (
            nc.sbuf_tensor("qkT", [128, 16, T], BF16) as qkT,
            nc.sbuf_tensor("vsb", [128, 16, D], BF16) as vsb,
            nc.sbuf_tensor("wvs", [128, 8, D], BF16) as wvs,
            nc.sbuf_tensor("h1T", [128, 8, T], BF16) as h1T,
            nc.sbuf_tensor("nrm0", [128, D], F32) as _n0,
            nc.sbuf_tensor("nrm1", [128, D], F32) as _n1,
            nc.sbuf_tensor("cb1repA", [128, D], BF16) as _cr0,
            nc.sbuf_tensor("cb1repB", [128, D], BF16) as _cr1,
            nc.sbuf_tensor("sbrepA", [128, D], BF16) as _sr0,
            nc.sbuf_tensor("sbrepB", [128, D], BF16) as _sr1,
        ):
            nrm = [_n0, _n1]
            cb1rep = [_cr0, _cr1]
            sbrep = [_sr0, _sr1]

            with nc.Block() as blk:
                bs = snap()

                @blk.tensor
                def _(t):
                    j = 0
                    for b in range(B):
                        for src_row, _d in ((cbr[b], cb1rep[b]),
                                            (sbr[b], sbrep[b])):
                            for hch in range(2):
                                if j >= 2:
                                    t.wait_ge(H["vv"], bs["vv"] + j - 1)
                                t.matmul(pk[j % 2][:, :], onesb[0:1, 0:128],
                                         src_row.ap()[0:1, hch * 512:
                                                      (hch + 1) * 512],
                                         start=True, stop=True)
                                t.drain().then_inc(H["tt"], 1)
                                j += 1
                    V["tt"] += 8

                @blk.vector
                def _(v):
                    j = 0
                    for b in range(B):
                        for _s, dst in ((cbr[b], cb1rep[b]),
                                        (sbr[b], sbrep[b])):
                            for hch in range(2):
                                v.wait_ge(H["tt"], bs["tt"] + j + 1)
                                v.tensor_copy(
                                    dst.ap()[:, hch * 512:(hch + 1) * 512],
                                    pk[j % 2][:, :])
                                v.drain().then_inc(H["vv"], 1)
                                j += 1
                    V["vv"] += 8

            wpt = [xt[0].ap()[:, :], xt[1].ap()[:, :]]
            wpst = [stt[0].ap()[:, 0:4], stt[1].ap()[:, 0:4]]
            wptm = nrm[0].ap()[:, :]
            wpi = nrm[1].ap().bitcast(I32)
            wpq = [h1t[0].ap()[:, :], h1t[1].ap()[:, :]]

            h1f = h1T.ap().rearrange("p a b -> p (a b)").bitcast(F32)
            h1h = h1T.ap().rearrange("p a b -> p (a b)").bitcast(F16)
            h1b = h1T.ap().rearrange("p a b -> p (a b)")
            wvf = wvs.ap().rearrange("p a b -> p (a b)").bitcast(F32)

            wp_tiles_g = [(wg_in, wgq, j) for j in range(22)]
            wp_tiles_u = [(wu_in, wuq, j) for j in range(22)]

            vsbb = vsb.ap().rearrange("p a b -> p (a b)")
            vsbf = vsb.ap().rearrange("p a b -> p (a b)").bitcast(F32)
            vsbh = vsb.ap().rearrange("p a b -> p (a b)").bitcast(F16)

            lnx_ = [vsbf[:, 3072:4096], vsbf[:, 4096:5120]]

            def ln1_s1_factory(bs):
                xt_ = lnx_
                out_ = [vsbb[:, 10240:11264], vsbb[:, 11264:12288]]
                stt_ = [stt_b[0].ap()[:, :], stt_b[1].ap()[:, :]]
                return make_ln_chain(
                    bs, 1, False,
                    lambda i: x_in[1, i * 128:(i + 1) * 128, :],
                    lambda i: h1s[i * 128:(i + 1) * 128, :],
                    xt_, stt_, out_, None, "vv2", "sc2", "dlxA", "dh2")

            def ln2_s0_factory(bs):
                xt_ = lnx_
                out_ = [vsbf[:, 5120:6144], vsbf[:, 6144:7168]]
                ht_ = [vsbh[:, 14336:15360], vsbh[:, 15360:16384]]
                stt_ = [stt_b[0].ap()[:, :], stt_b[1].ap()[:, :]]

                def xw(g, i):
                    g.wait_ge(H[f"dou{i % 2}"], bs[f"dou{i % 2}"])
                return make_ln_chain(
                    bs, 0, True,
                    lambda i: out[0, i * 128:(i + 1) * 128, :],
                    lambda i: hs[i * 128:(i + 1) * 128, :],
                    xt_, stt_, out_, ht_, "vv2", "sc2", "dlxA", "dh1",
                    extra_wait=xw)

            def ln2_s1_factory(bs):
                xt_ = [wvf[:, 0:1024], wvf[:, 1024:2048]]
                out_ = [wvf[:, 2048:3072], wvf[:, 3072:4096]]
                ht_ = [h1h[:, 12288:13312], h1h[:, 13312:14336]]
                stt_ = [stt_b[0].ap()[:, :], stt_b[1].ap()[:, :]]

                def xw(g, i):
                    g.wait_ge(H[f"dou{i % 2}"],
                              bs[f"dou{i % 2}"] + 16 * (i // 2 + 1))
                return make_ln_chain(
                    bs, 1, True,
                    lambda i: out[1, i * 128:(i + 1) * 128, :],
                    lambda i: hs[T + i * 128:T + (i + 1) * 128, :],
                    xt_, stt_, out_, ht_, "vv3", "sc3", "dlxB", "dh2",
                    extra_wait=xw)

            if STAGE >= 3:
                emit_seq(0, upto=min(STAGE - 1, 4), wprep_tiles=wp_tiles_g,
                         qkv_extras=([ln1_s1_factory] if STAGE >= 6 else ()))
            if STAGE >= 6:
                emit_seq(1, wprep_tiles=wp_tiles_u,
                         qkv_extras=([ln2_s0_factory] if STAGE >= 7 else ()),
                         attn_extras=([ln2_s1_factory] if STAGE >= 7 else ()))
            if STAGE <= 7:
                return nc

        ring.close()
        mid.close()
        def global_scale(col, ccio, sig_idx, scl_lo, pre_s=None):
            with nc.Block() as blk:
                bs = snap()

                if pre_s is not None:
                    @blk.scalar
                    def _(s):
                        pre_s(s, bs)

                @blk.sync
                def _(sy):
                    sy.dma_start(out=pmx, in_=gacc[:, col:col + 1]
                                 ).then_inc(H["dsy"], 16)
                    sy.wait_ge(H["dsy"], bs["dsy"] + 16)
                    sy.dma_start(out=gmr[0:1, :], in_=pmx).then_inc(H["dsy"], 16)
                    V["dsy"] += 32

                @blk.vector
                def _(v):
                    v.memset(r16[:, :], 0.0)
                    v.wait_ge(H["dsy"], bs["dsy"] + 32)
                    v.reduce_max(sigf[0:1, 4:5], gmr[0:1, :], AX)
                    v.drain()
                    v.tensor_single_scalar(r16[0:1, :], r16[0:1, :],
                                           sigf[0:1, 4:5], ALU.max)
                    v.drain().then_inc(H["vv"], 1)
                    v.wait_ge(H["dg"], bs["dg"] + 32)
                    v.tensor_tensor(sigf[0:1, 5:6], gmr[0:1, 0:1],
                                    sigf[0:1, sig_idx:sig_idx + 1], ALU.mult)
                    v.drain()
                    v.reciprocal(sigf[0:1, 6:7], sigf[0:1, 5:6])
                    v.drain().then_inc(H["vv"], 1)
                    v.wait_ge(H["tt"], bs["tt"] + 2)
                    v.tensor_copy(scl[:, scl_lo:scl_lo + 1], pk[0][:, 0:1])
                    v.tensor_copy(scl[:, scl_lo + 1:scl_lo + 2], pk[1][:, 0:1])
                    v.drain().then_inc(H["vv"], 1)
                    V["vv"] += 3

                @blk.gpsimd
                def _(g):
                    g.wait_ge(H["vv"], bs["vv"] + 1)
                    g.dma_start(out=ccio[0].ap(), in_=r16[0:1, :]).then_inc(H["dg"], 16)
                    g.wait_ge(H["dg"], bs["dg"] + 16)
                    g.collective_compute(
                        "AllReduce", ALU.max,
                        replica_groups=[list(range(8))],
                        ins=[ccio[0].ap().opt()],
                        outs=[ccio[1].ap().opt()],
                    ).then_inc(H["cc"], 1)
                    V["cc"] += 1
                    g.wait_ge(H["cc"], V["cc"])
                    g.dma_start(out=gmr[0:1, 0:16], in_=ccio[1].ap()
                                ).then_inc(H["dg"], 16)
                    V["dg"] += 32

                @blk.tensor
                def _(t):
                    t.wait_ge(H["vv"], bs["vv"] + 2)
                    t.matmul(pk[0][:, 0:1], ones32[0:1, :], sigf[0:1, 5:6],
                             start=True, stop=True)
                    t.matmul(pk[1][:, 0:1], ones32[0:1, :], sigf[0:1, 6:7],
                             start=True, stop=True)
                    t.drain().then_inc(H["tt"], 1)
                    t.drain().then_inc(H["tt"], 1)
                    V["tt"] += 2

        if STAGE <= 8:
            return nc
        global_scale(0, ccb[0], 2, 0)
        if STAGE <= 9:
            return nc

        with (
            nc.sbuf_tensor("xq", [128, 8, B * T], BF16) as xq,
            nc.sbuf_tensor("wgT", [128, 8, DI], BF16) as wgT,
            nc.sbuf_tensor("wuT", [128, 8, DI], BF16) as wuT,
        ):
            with (
                nc.sbuf_tensor("hT", [128, 2, B * T], F16) as hT,
                nc.sbuf_tensor("atm", [128, 2, 2048], F32) as atm,
                nc.sbuf_tensor("qi", [128, 2, 2048], I32) as qi,
                nc.Block() as blk,
            ):
                bs = snap()

                @blk.scalar
                def _(s):
                    for nm in ("dh10", "dh11", "dh20", "dh21",
                               "dwq0", "dwq1"):
                        s.wait_ge(H[nm], bs[nm])
                    s.wait_ge(H["vv"], bs["vv"])
                    for dc in range(8):
                        s.dma_start(out=hT[:, dc % 2, :],
                                    in_=hs[0:B * T, dc * 128:(dc + 1) * 128],
                                    transpose=True).then_inc(H["dsc"], 16)
                        V["dsc"] += 16
                        for hf in range(2):
                            j = 2 * dc + hf
                            s.wait_ge(H["dsc"], V["dsc"])
                            if j >= 2:
                                s.wait_ge(H["vv"], bs["vv"] + j - 1)
                            s.activation(atm[:, hf, :],
                                         hT[:, dc % 2, hf * 2048:(hf + 1) * 2048],
                                         ACT.Identity, bias=zc[:, 0:1],
                                         scale=scl[:, 1:2]).then_inc(H["sc"], 1)
                        s.dma_start(out=wgT[:, dc, :],
                                    in_=wgq[0:DI, dc * 128:(dc + 1) * 128],
                                    transpose=True).then_inc(H["dsc"], 16)
                        s.dma_start(out=wuT[:, dc, :],
                                    in_=wuq[0:DI, dc * 128:(dc + 1) * 128],
                                    transpose=True).then_inc(H["dsc"], 16)
                        V["dsc"] += 32
                    V["sc"] += 16

                @blk.vector
                def _(v):
                    for nm in ("dwq0", "dwq1"):
                        v.wait_ge(H[nm], bs[nm])
                    for j in range(16):
                        dc, hf = j // 2, j % 2
                        v.wait_ge(H["sc"], bs["sc"] + j + 1)
                        v.tensor_scalar(qi[:, hf, :], atm[:, hf, :],
                                        -128.0, 127.0, ALU.max, ALU.min)
                        v.drain()
                        v.tensor_scalar(xq[:, dc, hf * 2048:(hf + 1) * 2048],
                                        qi[:, hf, :], scl[:, 0:1], None,
                                        ALU.mult).then_inc(H["vv"], 1)
                    V["vv"] += 16

            if STAGE <= 10:
                return nc
            IBW = [1024, 1024, 768]
            IBO = [0, 1024, 2048]
            with (
                nc.sbuf_tensor("sil", [128, 2, 1024], F32) as sil,
                nc.sbuf_tensor("spb", [128, 2, 1024], F16) as spb,
                nc.sbuf_tensor("fst", [128, 2, 2], F32) as fst,
                nc.psum_tensor("pk7", [128, 512], F32) as pk7,
                nc.Block() as blk,
            ):
                bs = snap()
                pkf = pk + [pk7]
                NIT = 32 * 3

                @blk.scalar
                def _(s):
                    load_set(s, "silu_and_others")
                    for it in range(NIT):
                        ib = it % 3
                        w = IBW[ib]
                        pg = (it % 2) * 4
                        s.wait_ge(H["tt"], bs["tt"] + it + 1)
                        if it >= 2:
                            s.wait_ge(H["vv"], bs["vv"] + it - 1)
                        s.activation(sil[:, it % 2, 0:512], pkf[pg][:, :], ACT.Silu)
                        s.activation(sil[:, it % 2, 512:w], pkf[pg + 1][:, 0:w - 512],
                                     ACT.Silu).then_inc(H["sc"], 1)
                    V["sc"] += NIT

                @blk.tensor
                def _(t):
                    t.wait_ge(H["dsc"], bs["dsc"])
                    for it in range(NIT):
                        tt_i, ib = it // 3, it % 3
                        w = IBW[ib]
                        pg = (it % 2) * 4
                        if it >= 2:
                            t.wait_ge(H["vv"], bs["vv"] + (it - 2) + 1)
                        for dc in range(8):
                            lh = xq[:, dc, tt_i * 128:(tt_i + 1) * 128]
                            st_, sp_ = (dc == 0), (dc == 7)
                            t.matmul(pkf[pg][:, :], lh,
                                     wgT[:, dc, IBO[ib]:IBO[ib] + 512],
                                     start=st_, stop=sp_)
                            t.matmul(pkf[pg + 1][:, 0:w - 512], lh,
                                     wgT[:, dc, IBO[ib] + 512:IBO[ib] + w],
                                     start=st_, stop=sp_)
                            t.matmul(pkf[pg + 2][:, :], lh,
                                     wuT[:, dc, IBO[ib]:IBO[ib] + 512],
                                     start=st_, stop=sp_)
                            t.matmul(pkf[pg + 3][:, 0:w - 512], lh,
                                     wuT[:, dc, IBO[ib] + 512:IBO[ib] + w],
                                     start=st_, stop=sp_)
                        t.drain().then_inc(H["tt"], 1)
                    V["tt"] += NIT

                @blk.vector
                def _(v):
                    for it in range(NIT):
                        ib = it % 3
                        w = IBW[ib]
                        pg = (it % 2) * 4
                        v.wait_ge(H["sc"], bs["sc"] + it + 1)
                        if it >= 2:
                            v.wait_ge(H[f"dss{it % 2}"],
                                      bs[f"dss{it % 2}"] + 16 * ((it - 2) // 2 + 1))
                        v.tensor_tensor(spb[:, it % 2, 0:512], sil[:, it % 2, 0:512],
                                        pkf[pg + 2][:, :], ALU.mult)
                        v.tensor_tensor(spb[:, it % 2, 512:w], sil[:, it % 2, 512:w],
                                        pkf[pg + 3][:, 0:w - 512], ALU.mult)
                        v.drain()
                        v.tensor_reduce(fst[:, it % 2, 0:1], spb[:, it % 2, 0:w], AX,
                                        ALU.max, apply_absolute_value=True)
                        v.drain()
                        v.tensor_tensor(gacc[:, 1:2], gacc[:, 1:2],
                                        fst[:, it % 2, 0:1], ALU.max)
                        v.drain().then_inc(H["vv"], 1)
                    V["vv"] += NIT

                @blk.sync
                def _(sy):
                    for it in range(NIT):
                        tt_i, ib = it // 3, it % 3
                        w = IBW[ib]
                        sy.wait_ge(H["vv"], bs["vv"] + it + 1)
                        sy.dma_start(out=ss[tt_i * 128:(tt_i + 1) * 128,
                                            IBO[ib]:IBO[ib] + w],
                                     in_=spb[:, it % 2, 0:w]).then_inc(H[f"dss{it % 2}"], 16)
                    V["dss0"] += NIT * 8
                    V["dss1"] += NIT * 8

        if STAGE <= 11:
            return nc

        with (
            nc.sbuf_tensor("dwt", [128, 2, DI], F32) as dwt,
            nc.sbuf_tensor("dstt", [128, 2, 4], F32) as dstt,
            nc.sbuf_tensor("dtmp", [128, DI], F32) as dtmp,
            nc.sbuf_tensor("di32", [128, DI], I32) as di32,
            nc.sbuf_tensor("dqb", [128, 2, B, DI], BF16) as dqb,
            nc.Block() as blk,
        ):
            bs = snap()

            @blk.gpsimd
            def _(g):
                for dc in range(8):
                    if dc >= 2:
                        g.wait_ge(H["scw"], bs["scw"] + (dc - 2) + 1)
                    g.dma_start(out=dwt[:, dc % 2, :],
                                in_=wd_in[dc * 128:(dc + 1) * 128, :]
                                ).then_inc(H[f"dwt{dc % 2}"], 16)
                V["dwt0"] += 4 * 16
                V["dwt1"] += 4 * 16

            @blk.vector
            def _(v):
                for dc in range(8):
                    st = dstt[:, dc % 2, :]
                    v.wait_ge(H[f"dwt{dc % 2}"], bs[f"dwt{dc % 2}"] + 16 * (dc // 2 + 1))
                    v.tensor_reduce(st[:, 0:1], dwt[:, dc % 2, :], AX, ALU.max,
                                    apply_absolute_value=True)
                    v.drain()
                    v.tensor_scalar(st[:, 1:2], st[:, 0:1], 1.0 / 127.0, None, ALU.mult)
                    v.drain()
                    v.reciprocal(st[:, 2:3], st[:, 1:2])
                    v.tensor_tensor(st[:, 3:4], st[:, 1:2], gbT[:, dc, 0:1], ALU.mult)
                    v.tensor_tensor(st[:, 0:1], st[:, 1:2], gbT[:, dc, 1:2], ALU.mult)
                    v.drain().then_inc(H["vvw"], 1)
                    v.wait_ge(H["scw"], bs["scw"] + dc + 1)
                    if dc >= 2:
                        v.wait_ge(H[f"dwq{dc % 2}"],
                                  bs[f"dwq{dc % 2}"] + 32 * ((dc - 2) // 2 + 1))
                    v.tensor_copy(di32[:, :], dtmp[:, :])
                    v.drain()
                    v.tensor_scalar(dqb[:, dc % 2, 0, :], di32[:, :],
                                    st[:, 3:4], None, ALU.mult)
                    v.tensor_scalar(dqb[:, dc % 2, 1, :], di32[:, :],
                                    st[:, 0:1], None, ALU.mult)
                    v.drain().then_inc(H["vvw"], 1)
                V["vvw"] += 16

            @blk.scalar
            def _(s):
                for dc in range(8):
                    s.wait_ge(H["vvw"], bs["vvw"] + 2 * dc + 1)
                    if dc >= 1:
                        s.wait_ge(H["vvw"], bs["vvw"] + 2 * (dc - 1) + 2)
                    s.activation(dtmp[:, :], dwt[:, dc % 2, :], ACT.Identity,
                                 bias=zc[:, 0:1],
                                 scale=dstt[:, dc % 2, 2:3]).then_inc(H["scw"], 1)
                V["scw"] += 8

            @blk.sync
            def _(sy):
                for dc in range(8):
                    sy.wait_ge(H["vvw"], bs["vvw"] + 2 * dc + 2)
                    for b in range(B):
                        sy.dma_start(out=wdq[b, dc * 128:(dc + 1) * 128, :],
                                     in_=dqb[:, dc % 2, b, :]).then_inc(H[f"dwq{dc % 2}"], 16)
                V["dwq0"] += 4 * 32
                V["dwq1"] += 4 * 32

        if STAGE <= 12:
            return nc
        with (
            nc.sbuf_tensor("wdT", [128, B, 22, D], BF16) as wdT,
            nc.sbuf_tensor("sT", [128, 2, 22, 512], F16) as sT,
            nc.sbuf_tensor("h2q", [128, 2, 22, 512], BF16) as h2q,
            nc.sbuf_tensor("qtm", [128, 2, 512], F32) as qtm,
            nc.sbuf_tensor("qii", [128, 2, 512], I32) as qii,
            nc.sbuf_tensor("xr0", [128, D], F32) as _r0,
            nc.sbuf_tensor("xr1", [128, D], F32) as _r1,
        ):
            xr = [_r0, _r1]

            def pre_s(s, bsx):
                for nm in ("dwq0", "dwq1", "dss0", "dss1"):
                    s.wait_ge(H[nm], bsx[nm])
                for ic in range(22):
                    s.dma_start(out=sT[:, 0, ic, :],
                                in_=ss[0:512, ic * 128:(ic + 1) * 128],
                                transpose=True).then_inc(H["dsc"], 16)
                s.wait_ge(H["dsc"], bsx["dsc"] + 22 * 16)
                for b in range(B):
                    for ic in range(22):
                        s.dma_start(out=wdT[:, b, ic, :],
                                    in_=wdq[b, 0:D, ic * 128:(ic + 1) * 128],
                                    transpose=True).then_inc(H["dsc"], 16)

            ST0_DSC = V["dsc"] + 22 * 16
            global_scale(1, ccb[1], 3, 2, pre_s)
            V["dsc"] += 66 * 16
            PRE_DSC = V["dsc"]
            if STAGE <= 13:
                return nc

            bs = snap()
            dstk = ExitStack()
            blk = dstk.enter_context(nc.Block())
            NC8 = B * 4
            q_done = {}
            r_done = {}
            pos = 0
            for tc in range(NC8):
                if tc >= 1:
                    for j in range(4 * (tc - 1), 4 * tc):
                        pos += 1
                        r_done[j] = pos
                for ic in range(22):
                    pos += 1
                    q_done[(tc, ic)] = pos
            for j in range(4 * (NC8 - 1), 4 * NC8):
                pos += 1
                r_done[j] = pos

            @blk.scalar
            def _(s):
                s.wait_ge(H["vv"], bs["vv"])
                for tc in range(NC8):
                    if tc >= 1:
                        for ic in range(22):
                            s.dma_start(out=sT[:, tc % 2, ic, :],
                                        in_=ss[tc * 512:(tc + 1) * 512,
                                               ic * 128:(ic + 1) * 128],
                                        transpose=True).then_inc(H["dsc"], 16)
                        V["dsc"] += 22 * 16
                    s.wait_ge(H["dsc"], V["dsc"] if tc >= 1 else ST0_DSC)
                    for ic in range(22):
                        ji = tc * 22 + ic
                        if ji >= 2:
                            prev = (tc * 22 + ic - 2)
                            s.wait_ge(H["vv"], bs["vv"]
                                      + q_done[(prev // 22, prev % 22)])
                        s.activation(qtm[:, ic % 2, :], sT[:, tc % 2, ic, :],
                                     ACT.Identity, bias=zc[:, 0:1],
                                     scale=scl[:, 3:4]).then_inc(H["sc"], 1)
                V["sc"] += NC8 * 22

            @blk.vector
            def _(v):
                for tc in range(NC8):
                    if tc >= 1:
                        for j in range(4 * (tc - 1), 4 * tc):
                            v.wait_ge(H["tt"], bs["tt"] + j + 1)
                            v.wait_ge(H[f"dgx{j % 2}"],
                                      bs[f"dgx{j % 2}"] + 16 * (j // 2 + 1))
                            if j >= 2:
                                v.wait_ge(H[f"dou{j % 2}"],
                                          bs[f"dou{j % 2}"]
                                          + 16 * ((j - 2) // 2 + 1))
                            pg = (j % 2) * 2
                            v.tensor_tensor(xr[j % 2][:, 0:512], pk[pg][:, :],
                                            xr[j % 2][:, 0:512], ALU.add)
                            v.tensor_tensor(xr[j % 2][:, 512:1024], pk[pg + 1][:, :],
                                            xr[j % 2][:, 512:1024], ALU.add)
                            v.drain().then_inc(H["vv"], 1)
                    for ic in range(22):
                        v.wait_ge(H["sc"], bs["sc"] + tc * 22 + ic + 1)
                        v.tensor_scalar(qii[:, ic % 2, :], qtm[:, ic % 2, :],
                                        -128.0, 127.0, ALU.max, ALU.min)
                        v.drain()
                        v.tensor_scalar(h2q[:, tc % 2, ic, :], qii[:, ic % 2, :],
                                        scl[:, 2:3], None,
                                        ALU.mult).then_inc(H["vv"], 1)
                for j in range(4 * (NC8 - 1), 4 * NC8):
                    v.wait_ge(H["tt"], bs["tt"] + j + 1)
                    v.wait_ge(H[f"dgx{j % 2}"],
                              bs[f"dgx{j % 2}"] + 16 * (j // 2 + 1))
                    v.wait_ge(H[f"dou{j % 2}"],
                              bs[f"dou{j % 2}"] + 16 * ((j - 2) // 2 + 1))
                    pg = (j % 2) * 2
                    v.tensor_tensor(xr[j % 2][:, 0:512], pk[pg][:, :],
                                    xr[j % 2][:, 0:512], ALU.add)
                    v.tensor_tensor(xr[j % 2][:, 512:1024], pk[pg + 1][:, :],
                                    xr[j % 2][:, 512:1024], ALU.add)
                    v.drain().then_inc(H["vv"], 1)
                V["vv"] += pos

            @blk.gpsimd
            def _(g):
                for j in range(B * NT):
                    b, ti = j // NT, j % NT
                    g.wait_ge(H[f"dou{j % 2}"], bs[f"dou{j % 2}"]
                              + (16 * ((j - 2) // 2 + 1) if j >= 2 else 0))
                    g.dma_start(out=xr[j % 2][:, :],
                                in_=out[b, ti * 128:(ti + 1) * 128, :]
                                ).then_inc(H[f"dgx{j % 2}"], 16)
                V["dgx0"] += B * NT * 8
                V["dgx1"] += B * NT * 8

            @blk.tensor
            def _(t):
                t.wait_ge(H["dsc"], PRE_DSC)
                for j in range(B * NT):
                    b, ti = j // NT, j % NT
                    tc = (b * NT + ti) // 4
                    tl = ti % 4
                    pg = (j % 2) * 2
                    t.wait_ge(H["vv"], bs["vv"] + q_done[(tc, 21)])
                    if j >= 2:
                        t.wait_ge(H["vv"], bs["vv"] + r_done[j - 2])
                    for ic in range(22):
                        lh = h2q[:, tc % 2, ic, tl * 128:(tl + 1) * 128]
                        t.matmul(pk[pg][:, :], lh, wdT[:, b, ic, 0:512],
                                 start=(ic == 0), stop=(ic == 21))
                        t.matmul(pk[pg + 1][:, :], lh, wdT[:, b, ic, 512:1024],
                                 start=(ic == 0), stop=(ic == 21))
                    t.drain().then_inc(H["tt"], 1)
                V["tt"] += B * NT

            @blk.sync
            def _(sy):
                for j in range(B * NT):
                    b, ti = j // NT, j % NT
                    sy.wait_ge(H["vv"], bs["vv"] + r_done[j])
                    sy.dma_start(out=out[b, ti * 128:(ti + 1) * 128, :],
                                 in_=xr[j % 2][:, :]).then_inc(H[f"dou{j % 2}"], 16)
                V["dou0"] += B * NT * 8
                V["dou1"] += B * NT * 8
                sy.wait_ge(H["dou0"], V["dou0"])
                sy.wait_ge(H["dou1"], V["dou1"])
            dstk.close()

    return nc


def kernel(**inputs):
    x = np.ascontiguousarray(np.asarray(inputs["L0_attn_x"], np.float32))
    W = np.ascontiguousarray(np.asarray(inputs["L0_attn_W"], np.float32))
    b = np.ascontiguousarray(
        np.asarray(inputs["L0_attn_b"], np.float32)).reshape(1, 3 * D)
    cx = np.asarray(inputs["L0_ffn_clip_x"], np.float32).reshape(1)
    cd = np.asarray(inputs["L0_ffn_clip_down"], np.float32).reshape(1)
    wgt = np.ascontiguousarray(np.asarray(inputs["L0_ffn_w_gate"], np.float32))
    wup = np.ascontiguousarray(np.asarray(inputs["L0_ffn_w_up"], np.float32))
    wdn = np.ascontiguousarray(np.asarray(inputs["L0_ffn_w_down"], np.float32))
    c = np.ascontiguousarray(np.asarray(inputs["L0_c"], np.float32))
    wm = np.ascontiguousarray(np.asarray(inputs["L0_wmod"], np.float32))

    clips = np.ascontiguousarray(
        np.concatenate([cx, cd]).reshape(1, 2).astype(np.float32))
    idb_np = np.eye(128, dtype=ml_dtypes.bfloat16)
    idf_np = np.eye(128, dtype=np.float32)
    mask_np = np.triu(np.full((128, 128), NEG, np.float32), k=1)

    nc = build_nc()
    in_maps = []
    for core in range(8):
        in_maps.append(dict(
            x=np.ascontiguousarray(x[core * B:(core + 1) * B]),
            c=np.ascontiguousarray(c[core * B:(core + 1) * B]),
            wmod=wm, aw=W, ab=b, clips=clips,
            wg=wgt, wu=wup, wd=wdn,
            idb=idb_np, idf=idf_np, maskc=mask_np,
        ))
    res = run_bass_kernel_spmd(nc, in_maps, core_ids=list(range(8)))
    return np.concatenate([r["out"] for r in res.results], axis=0)


if __name__ == "__main__":
    nc_ = build_nc()
    n = sum(1 for _ in nc_.all_instructions())
    print("instructions:", n)


# version 21
# speedup vs baseline: 1.0224x; 1.0224x over previous
from contextlib import ExitStack

import numpy as np
import ml_dtypes
import concourse.bass as bass
import concourse.mybir as mybir
from concourse.bass_utils import run_bass_kernel_spmd
from concourse.hw_specs import get_activation_tables

F32 = mybir.dt.float32
BF16 = mybir.dt.bfloat16
F16 = mybir.dt.float16
I32 = mybir.dt.int32
ALU = mybir.AluOpType
ACT = mybir.ActivationFunctionType
AX = mybir.AxisListType.X

B = 2
T = 2048
D = 1024
DI = 2816
NT = T // 128
NEG = -1.0e9

KEYS = [(qt + 1) * 128 for qt in range(NT)]
NKC = [(k + 511) // 512 for k in KEYS]
KCW = [[min(512, KEYS[qt] - kc * 512) for kc in range(NKC[qt])] for qt in range(NT)]
CUMPT = [0]
for _qt in range(NT):
    CUMPT.append(CUMPT[-1] + _qt + 1)

STAGE = 99


def build_nc():
    nc = bass.Bass("TRN2", target_bir_lowering=False, debug=False)

    x_in = nc.dram_tensor("x", [B, T, D], F32, kind="ExternalInput").ap()
    c_in = nc.dram_tensor("c", [B, D], F32, kind="ExternalInput").ap()
    wmod = nc.dram_tensor("wmod", [D, 6 * D], F32, kind="ExternalInput").ap()
    aw = nc.dram_tensor("aw", [D, 3 * D], F32, kind="ExternalInput").ap()
    ab = nc.dram_tensor("ab", [1, 3 * D], F32, kind="ExternalInput").ap()
    clips = nc.dram_tensor("clips", [1, 2], F32, kind="ExternalInput").ap()
    wg_in = nc.dram_tensor("wg", [DI, D], F32, kind="ExternalInput").ap()
    wu_in = nc.dram_tensor("wu", [DI, D], F32, kind="ExternalInput").ap()
    wd_in = nc.dram_tensor("wd", [D, DI], F32, kind="ExternalInput").ap()
    idb = nc.dram_tensor("idb", [128, 128], BF16, kind="ExternalInput").ap()
    idf = nc.dram_tensor("idf", [128, 128], F32, kind="ExternalInput").ap()
    maskc = nc.dram_tensor("maskc", [128, 128], F32, kind="ExternalInput").ap()
    out = nc.dram_tensor("out", [B, T, D], F32, kind="ExternalOutput").ap()

    h1s = nc.dram_tensor("h1s", [T, D], BF16).ap()
    hs = nc.dram_tensor("hs", [B * T, D], F16).ap()
    pss = nc.dram_tensor("pss", [2, 128, T], BF16).ap()
    ss = nc.dram_tensor("ss", [B * T, DI], F16).ap()
    wgq = nc.dram_tensor("wgq", [DI, D], BF16).ap()
    wuq = nc.dram_tensor("wuq", [DI, D], BF16).ap()
    wdq = nc.dram_tensor("wdq", [B, D, DI], BF16).ap()
    pmx = nc.dram_tensor("pmx", [1, 128], F32).ap()
    ccb = [[nc.dram_tensor(f"cc{j}{d}", [1, 16], F32) for d in "io"] for j in range(2)]

    act_sets = list(get_activation_tables(nc.m.arch).keys())

    def load_set(eng, name):
        eng.add_instruction(mybir.InstLoadActFuncSet(
            name=nc.get_next_instruction_name(),
            act_func_set_id=act_sets.index(name), ins=[], outs=[]))

    with ExitStack() as top:
        names = ["dg", "dsy", "dsc", "tt", "vv", "sc",
                 "vvw", "scw", "cc",
                 "dgx0", "dgx1", "dwv0", "dwv1", "dwo0", "dwo1",
                 "dwt0", "dwt1", "dh10", "dh11", "dps0", "dps1",
                 "dpt0", "dpt1", "dou0", "dou1", "dss0", "dss1",
                 "dwq0", "dwq1", "dpts0", "dpts1",
                 "vv2", "sc2", "dlxA0", "dlxA1",
                 "vv3", "sc3", "dlxB0", "dlxB1", "dh20", "dh21",
                 "dlxC0", "dlxC1", "dwtS0", "dwtS1"]
        H = {}
        V = {}
        for n in names:
            H[n] = top.enter_context(nc.semaphore(n))
            V[n] = 0

        def snap():
            return dict(V)

        sb = top.enter_context
        ones32 = sb(nc.sbuf_tensor("ones32", [1, 128], F32))
        onesb = sb(nc.sbuf_tensor("onesb", [1, 512], BF16))
        zc = sb(nc.sbuf_tensor("zc", [128, 2], F32))
        idbt = sb(nc.sbuf_tensor("idbt", [128, 128], BF16))
        idft = sb(nc.sbuf_tensor("idft", [128, 128], F32))
        maskt = sb(nc.sbuf_tensor("maskt", [128, 128], F32))

        ca1T = sb(nc.sbuf_tensor("ca1T", [128, 8, B], F32))
        gbT = sb(nc.sbuf_tensor("gbT", [128, 8, B], F32))
        sigf = sb(nc.sbuf_tensor("sigf", [1, 8], F32))
        scl = sb(nc.sbuf_tensor("scl", [128, 4], F32))
        gacc = sb(nc.sbuf_tensor("gacc", [128, 2], F32))
        gmr = sb(nc.sbuf_tensor("gmr", [1, 128], F32))
        r16 = sb(nc.sbuf_tensor("r16", [1, 16], F32))

        pk = [sb(nc.psum_tensor(f"pk{i}", [128, 512], F32)) for i in range(7)]

        mid = top.enter_context(ExitStack())
        cbr = [mid.enter_context(nc.sbuf_tensor(f"cbr{b}", [1, D], BF16))
               for b in range(B)]
        sbr = [mid.enter_context(nc.sbuf_tensor(f"sbr{b}", [1, D], BF16))
               for b in range(B)]
        gar = [mid.enter_context(nc.sbuf_tensor(f"gar{b}", [1, D], BF16))
               for b in range(B)]
        beff = [mid.enter_context(nc.sbuf_tensor(f"beff{b}", [1, 3 * D], BF16))
                for b in range(B)]

        ring = top.enter_context(ExitStack())
        rng = ring.enter_context
        xt = [rng(nc.sbuf_tensor(f"xt{i}", [128, D], F32)) for i in range(2)]
        stt = [rng(nc.sbuf_tensor(f"stt{i}", [128, 16], F32)) for i in range(2)]
        stt_b = [rng(nc.sbuf_tensor(f"stt{i}", [128, 16], F32))
                 for i in range(2, 4)]
        h1t = [rng(nc.sbuf_tensor(f"h1t{i}", [128, D], BF16)) for i in range(2)]
        ht = [h1t[0].ap().bitcast(F16), h1t[1].ap().bitcast(F16)]

        def make_ln_chain(bs, seq, second, src_x, spill, xt_, stt_, out_, ht_,
                          SVV, SSC, SDX, SDH, extra_wait=None, x_ready=None):
            spt, vpt = 3, (3 if second else 2)

            def g_i(g, i):
                if src_x is None:
                    return
                if extra_wait is not None:
                    extra_wait(g, i)
                if i >= 2:
                    g.wait_ge(H[SSC], bs[SSC] + spt * (i - 2) + 3)
                g.dma_start(out=xt_[i % 2], in_=src_x(i)).then_inc(
                    H[SDX + str(i % 2)], 16)

            def xwait(eng, i):
                if x_ready is not None:
                    x_ready(eng, i)
                else:
                    eng.wait_ge(H[SDX + str(i % 2)], bs[SDX + str(i % 2)]
                                + 16 * (i // 2 + 1))

            def v_i(v, i):
                st = stt_[i % 2]
                xwait(v, i)
                if i >= 2:
                    v.wait_ge(H[SSC], bs[SSC] + spt * (i - 2) + 3)
                v.reduce_sum(st[:, 0:1], xt_[i % 2], AX)
                v.wait_ge(H[SSC], bs[SSC] + spt * i + 1)
                v.tensor_tensor(st[:, 1:2], st[:, 11:12], st[:, 12:13], ALU.add)
                v.drain()
                v.tensor_scalar(st[:, 2:3], st[:, 0:1], 1.0 / D, None, ALU.mult)
                v.tensor_scalar(st[:, 3:4], st[:, 1:2], 1.0 / D, None, ALU.mult)
                v.drain()
                v.tensor_tensor(st[:, 4:5], st[:, 2:3], st[:, 2:3], ALU.mult)
                v.drain()
                v.tensor_tensor(st[:, 5:6], st[:, 3:4], st[:, 4:5], ALU.subtract)
                v.drain().then_inc(H[SVV], 1)
                v.wait_ge(H[SSC], bs[SSC] + spt * i + 2)
                v.tensor_tensor(st[:, 8:9], st[:, 2:3], st[:, 7:8], ALU.mult)
                v.drain()
                v.tensor_scalar(st[:, 9:10], st[:, 8:9], -1.0, None, ALU.mult)
                v.drain().then_inc(H[SVV], 1)
                if second:
                    v.wait_ge(H[SSC], bs[SSC] + spt * i + 3)
                    v.wait_ge(H[SDH + str(i % 2)], bs[SDH + str(i % 2)]
                              + (16 * ((i - 2) // 2 + 1) if i >= 2 else 0))
                    v.tensor_tensor(out_[i % 2], out_[i % 2],
                                    cb1rep[seq].ap()[:, :], ALU.mult)
                    v.drain()
                    v.tensor_tensor(ht_[i % 2], out_[i % 2],
                                    sbrep[seq].ap()[:, :], ALU.add)
                    v.drain()
                    v.tensor_reduce(st[:, 10:11], ht_[i % 2], AX,
                                    ALU.max, apply_absolute_value=True)
                    v.drain()
                    v.tensor_tensor(gacc[:, 0:1], gacc[:, 0:1],
                                    st[:, 10:11], ALU.max)
                    v.drain().then_inc(H[SVV], 1)

            def s_i(s, i):
                st = stt_[i % 2]
                xwait(s, i)
                if not second:
                    s.wait_ge(H[SDH + str(i % 2)], bs[SDH + str(i % 2)]
                              + (16 * ((i - 2) // 2 + 1) if i >= 2 else 0))
                if second and i >= 2:
                    s.wait_ge(H[SVV], bs[SVV] + vpt * (i - 2) + 3)
                s.drain()
                s.activation(pk[6][:, :], xt_[i % 2][:, 0:512], ACT.Square,
                             accum_out=st[:, 11:12])
                s.drain()
                s.activation(pk[6][:, :], xt_[i % 2][:, 512:1024], ACT.Square,
                             accum_out=st[:, 12:13]).then_inc(H[SSC], 1)
                s.wait_ge(H[SVV], bs[SVV] + vpt * i + 1)
                s.activation(st[:, 6:7], st[:, 5:6], ACT.Ln,
                             bias=zc[:, 1:2], scale=1.0)
                s.drain()
                s.activation(st[:, 7:8], st[:, 6:7], ACT.Exp,
                             bias=zc[:, 0:1], scale=-0.5).then_inc(H[SSC], 1)
                s.wait_ge(H[SVV], bs[SVV] + vpt * i + 2)
                s.drain()
                s.activation(out_[i % 2], xt_[i % 2], ACT.Identity,
                             bias=st[:, 9:10], scale=st[:, 7:8]).then_inc(H[SSC], 1)

            def sy_i(sy, i):
                if second:
                    sy.wait_ge(H[SVV], bs[SVV] + vpt * i + 3)
                    srcap = ht_[i % 2]
                else:
                    sy.wait_ge(H[SSC], bs[SSC] + spt * i + 3)
                    srcap = out_[i % 2]
                sy.dma_start(out=spill(i), in_=srcap).then_inc(
                    H[SDH + str(i % 2)], 16)

            def done():
                V[SVV] += vpt * NT
                V[SSC] += spt * NT
                if src_x is not None:
                    V[SDX + "0"] += NT * 8
                    V[SDX + "1"] += NT * 8
                V[SDH + "0"] += NT * 8
                V[SDH + "1"] += NT * 8

            def tg(eng):
                eng.wait_ge(H[SSC], bs[SSC] + spt * NT)
                eng.wait_ge(H[SDH + "0"], bs[SDH + "0"] + NT * 8)
                eng.wait_ge(H[SDH + "1"], bs[SDH + "1"] + NT * 8)

            return g_i, v_i, s_i, sy_i, done, tg


        with (
            nc.sbuf_tensor("scrow", [B, D], F32) as scrow,
            nc.sbuf_tensor("scb", [B, D], BF16) as scb,
            nc.sbuf_tensor("scT", [128, 8, B], BF16) as scT,
            nc.sbuf_tensor("saT", [128, 8, B], BF16) as saT,
            nc.sbuf_tensor("modsb", [B, 6 * D], F32) as modsb,
            nc.psum_tensor("pT", [128, 128], BF16) as pT,
        ):
            with nc.Block() as blk:
                bs = snap()

                @blk.gpsimd
                def _(g):
                    for dst, src in [(idbt[:, :], idb), (idft[:, :], idf),
                                     (maskt[:, :], maskc), (scrow[:, :], c_in),
                                     (sigf[0:1, 0:2], clips)]:
                        g.dma_start(out=dst, in_=src).then_inc(H["dg"], 16)
                    V["dg"] += 5 * 16

                @blk.vector
                def _(v):
                    v.memset(zc[:, 0:1], 0.0)
                    v.memset(zc[:, 1:2], 1.0e-5)
                    v.memset(ones32[:, :], 1.0)
                    v.memset(onesb[:, :], 1.0)
                    v.memset(gacc[:, :], 0.0)
                    v.drain().then_inc(H["vv"], 1)
                    v.wait_ge(H["sc"], bs["sc"] + 1)
                    v.tensor_copy(scb[:, :], scrow[:, :])
                    v.wait_ge(H["sc"], bs["sc"] + 2)
                    v.tensor_scalar(sigf[0:1, 2:4], sigf[0:1, 2:4], 1.0, None, ALU.add)
                    v.drain()
                    v.reciprocal(sigf[0:1, 2:4], sigf[0:1, 2:4])
                    v.drain()
                    v.tensor_scalar(sigf[0:1, 2:4], sigf[0:1, 2:4],
                                    1.0 / 127.0, None, ALU.mult)
                    v.drain().then_inc(H["vv"], 1)
                    V["vv"] += 2

                @blk.scalar
                def _(s):
                    load_set(s, "silu_and_others")
                    s.wait_ge(H["dg"], bs["dg"] + 5 * 16)
                    s.activation(scrow[:, :], scrow[:, :], ACT.Silu).then_inc(H["sc"], 1)
                    load_set(s, "natural_log_exp_and_others")
                    s.wait_ge(H["vv"], bs["vv"] + 1)
                    s.activation(sigf[0:1, 2:4], sigf[0:1, 0:2], ACT.Exp,
                                 bias=zc[0:1, 0:1], scale=-1.0).then_inc(H["sc"], 1)
                    V["sc"] += 2

            if STAGE == 101:
                return nc
            ln1_0 = None
            if 2 <= STAGE < 100:
                ln1_0 = make_ln_chain(
                    snap(), 0, False,
                    lambda i: x_in[0, i * 128:(i + 1) * 128, :],
                    lambda i: h1s[i * 128:(i + 1) * 128, :],
                    [xt[0].ap()[:, :], xt[1].ap()[:, :]],
                    [stt[0].ap()[:, :], stt[1].ap()[:, :]],
                    [h1t[0].ap()[:, :], h1t[1].ap()[:, :]], None,
                    "vv2", "sc2", "dlxC", "dh1")
            with nc.sbuf_tensor("wmodsb", [128, 8, 3 * D], BF16) as wmodsb:
                for half in range(1 if STAGE in (1016, 1017) else 2):
                    with nc.Block() as blk:
                        bs = snap()

                        @blk.gpsimd
                        def _(g, half=half):
                            for dc in range(8):
                                g.dma_start(
                                    out=wmodsb[:, dc, :],
                                    in_=wmod[dc * 128:(dc + 1) * 128,
                                             half * 3 * D:(half + 1) * 3 * D],
                                ).then_inc(H["dg"], 16)
                            V["dg"] += 8 * 16

                        @blk.scalar
                        def _(s, half=half):
                            if ln1_0:
                                for j in range(8):
                                    ln1_0[2](s, 8 * half + j)

                        @blk.sync
                        def _(sy, half=half):
                            if ln1_0:
                                if half == 0:
                                    ln1_0[0](sy, 0)
                                    ln1_0[0](sy, 1)
                                for j in range(8 * half, 8 * half + 8):
                                    ln1_0[3](sy, j)
                                    if j + 2 <= 15:
                                        ln1_0[0](sy, j + 2)

                        @blk.tensor
                        def _(t, half=half, bs=bs):
                            if half == 0:
                                for dc in range(8):
                                    if dc > 0:
                                        t.wait_ge(H["vv"], bs["vv"] + dc)
                                    t.transpose(pT[:, 0:B],
                                                scb[0:B, dc * 128:(dc + 1) * 128],
                                                idbt[0:B, 0:B]).then_inc(H["tt"], 1)
                                V["tt"] += 8
                            t.wait_ge(H["dg"], bs["dg"] + 8 * 16)
                            nb = bs["vv"] + (8 if half == 0 else 0)
                            t.wait_ge(H["vv"], nb)
                            for nch in range(0 if STAGE == 1015 else (1 if STAGE == 1017 else 6)):
                                if nch >= 2:
                                    t.wait_ge(H["vv"], nb + nch - 1)
                                for dc in range(8):
                                    t.matmul(pk[nch % 2][0:B, :], scT[:, dc, :],
                                             wmodsb[:, dc, nch * 512:(nch + 1) * 512],
                                             start=(dc == 0), stop=(dc == 7))
                                t.drain().then_inc(H["tt"], 1)
                            V["tt"] += 0 if STAGE == 1015 else (1 if STAGE == 1017 else 6)

                        @blk.vector
                        def _(v, half=half, bs=bs):
                            tb = bs["tt"]
                            if half == 0:
                                for dc in range(8):
                                    v.wait_ge(H["tt"], bs["tt"] + dc + 1)
                                    v.tensor_copy(scT[:, dc, :], pT[:, 0:B])
                                    v.drain().then_inc(H["vv"], 1)
                                V["vv"] += 8
                                tb = bs["tt"] + 8
                            for nch in range(0 if STAGE == 1015 else 6):
                                v.wait_ge(H["tt"], tb + nch + 1)
                                off = half * 3 * D + nch * 512
                                v.tensor_copy(modsb[0:B, off:off + 512],
                                              pk[nch % 2][0:B, :])
                                v.drain().then_inc(H["vv"], 1)
                                if ln1_0:
                                    ln1_0[1](v, 8 * half + nch)
                            if ln1_0:
                                ln1_0[1](v, 8 * half + 6)
                                ln1_0[1](v, 8 * half + 7)
                            V["vv"] += 0 if STAGE == 1015 else (1 if STAGE == 1017 else 6)

            if ln1_0:
                ln1_0[4]()
            if STAGE in (102, 1015, 1016, 1017):
                return nc
            with (
                nc.sbuf_tensor("tmpg", [B, D], F32) as tmpg,
                nc.sbuf_tensor("tmpa", [B, D], F32) as tmpa,
                nc.sbuf_tensor("sab", [B, D], BF16) as sab,
                nc.Block() as blk,
            ):
                bs = snap()

                @blk.scalar
                def _(s):
                    s.activation(tmpa[:, :], modsb[0:B, 2 * D:3 * D],
                                 ACT.Abs).then_inc(H["sc"], 1)
                    s.wait_ge(H["vv"], bs["vv"] + 1)
                    s.activation(tmpa[:, :], modsb[0:B, 5 * D:6 * D],
                                 ACT.Abs).then_inc(H["sc"], 1)
                    V["sc"] += 2

                @blk.vector
                def _(v):
                    v.tensor_scalar(modsb[0:B, D:2 * D], modsb[0:B, D:2 * D],
                                    1.0, None, ALU.add)
                    v.tensor_scalar(modsb[0:B, 4 * D:5 * D], modsb[0:B, 4 * D:5 * D],
                                    1.0, None, ALU.add)
                    for gi, off in enumerate((2 * D, 5 * D)):
                        sl = modsb[0:B, off:off + D]
                        v.tensor_scalar(tmpg[:, :], sl, 0.0, 2.0, ALU.is_ge, ALU.mult)
                        v.drain()
                        v.tensor_scalar(tmpg[:, :], tmpg[:, :], -1.0, None, ALU.add)
                        v.wait_ge(H["sc"], bs["sc"] + gi + 1)
                        v.tensor_scalar(tmpa[:, :], tmpa[:, :], 0.25, None, ALU.max)
                        v.drain()
                        v.tensor_tensor(sl, tmpg[:, :], tmpa[:, :], ALU.mult)
                        v.drain().then_inc(H["vv"], 1)
                    v.tensor_copy(sab[:, :], modsb[0:B, 0:D])
                    v.drain().then_inc(H["vv"], 1)
                    V["vv"] += 3
                    for dc in range(8):
                        v.wait_ge(H["tt"], bs["tt"] + 3 * dc + 1)
                        v.tensor_copy(saT[:, dc, :], pT[:, 0:B])
                        v.drain().then_inc(H["vv"], 1)
                        v.wait_ge(H["tt"], bs["tt"] + 3 * dc + 2)
                        v.tensor_copy(ca1T[:, dc, :], pk[6][0:128, 0:B])
                        v.drain().then_inc(H["vv"], 1)
                        v.wait_ge(H["tt"], bs["tt"] + 3 * dc + 3)
                        v.tensor_copy(gbT[:, dc, :], pk[5][0:128, 0:B])
                        v.drain().then_inc(H["vv"], 1)

                    V["vv"] += 24

                @blk.gpsimd
                def _(g):
                    g.wait_ge(H["vv"], bs["vv"] + 3)
                    for b in range(B):
                        g.dma_start(out=gar[b].ap()[0:1, :],
                                    in_=modsb[b:b + 1, 2 * D:3 * D]
                                    ).then_inc(H["dg"], 16)
                        g.dma_start(out=cbr[b].ap()[0:1, :],
                                    in_=modsb[b:b + 1, 4 * D:5 * D]
                                    ).then_inc(H["dg"], 16)
                        g.dma_start(out=sbr[b].ap()[0:1, :],
                                    in_=modsb[b:b + 1, 3 * D:4 * D]
                                    ).then_inc(H["dg"], 16)
                    V["dg"] += 6 * 16

                @blk.tensor
                def _(t):
                    t.wait_ge(H["vv"], bs["vv"] + 3)
                    for dc in range(8):
                        if dc > 0:
                            t.wait_ge(H["vv"], bs["vv"] + 3 + 3 * dc)
                        t.transpose(pT[:, 0:B], sab[0:B, dc * 128:(dc + 1) * 128],
                                    idbt[0:B, 0:B]).then_inc(H["tt"], 1)
                        t.transpose(pk[6][0:128, 0:B],
                                    modsb[0:B, D + dc * 128:D + (dc + 1) * 128],
                                    idft[0:B, 0:B]).then_inc(H["tt"], 1)
                        t.transpose(pk[5][0:128, 0:B],
                                    modsb[0:B, 5 * D + dc * 128:
                                          5 * D + (dc + 1) * 128],
                                    idft[0:B, 0:B]).then_inc(H["tt"], 1)
                    V["tt"] += 24

            if STAGE == 103:
                return nc
            with (
                nc.sbuf_tensor("wqsb", [128, 8, 512], BF16) as wqsb,
                nc.sbuf_tensor("befr", [B, 3 * D], F32) as befr,
                nc.sbuf_tensor("brow", [B, 3 * D], F32) as brow,
                nc.sbuf_tensor("befb", [B, 3 * D], BF16) as befb,
                nc.Block() as blk,
            ):
                bs = snap()

                @blk.gpsimd
                def _(g):
                    for nch in range(6):
                        if nch >= 1:
                            g.wait_ge(H["tt"], bs["tt"] + nch)
                        for dc in range(8):
                            g.dma_start(
                                out=wqsb[:, dc, :],
                                in_=aw[dc * 128:(dc + 1) * 128,
                                       nch * 512:(nch + 1) * 512],
                            ).then_inc(H["dg"], 16)
                    V["dg"] += 6 * 8 * 16

                @blk.tensor
                def _(t):
                    for nch in range(6):
                        t.wait_ge(H["dg"], bs["dg"] + (nch + 1) * 8 * 16)
                        if nch >= 2:
                            t.wait_ge(H["vv"], bs["vv"] + nch - 1)
                        for dc in range(8):
                            t.matmul(pk[nch % 2][0:B, :], saT[:, dc, :],
                                     wqsb[:, dc, :],
                                     start=(dc == 0), stop=(dc == 7))
                        t.drain().then_inc(H["tt"], 1)
                    V["tt"] += 6

                @blk.vector
                def _(v):
                    for nch in range(6):
                        v.wait_ge(H["tt"], bs["tt"] + nch + 1)
                        v.tensor_copy(befr[0:B, nch * 512:(nch + 1) * 512],
                                      pk[nch % 2][0:B, :])
                        v.drain().then_inc(H["vv"], 1)
                    v.wait_ge(H["dsy"], bs["dsy"] + 32)
                    v.drain()
                    v.tensor_tensor(befr[0:B, :], befr[0:B, :], brow[0:B, :], ALU.add)
                    v.drain()
                    v.tensor_tensor(befr[0:B, 2 * D:3 * D], befr[0:B, 2 * D:3 * D],
                                    modsb[0:B, 2 * D:3 * D], ALU.mult)
                    v.drain()
                    v.tensor_copy(befb[0:B, :], befr[0:B, :])
                    v.drain().then_inc(H["vv"], 1)
                    V["vv"] += 7

                @blk.sync
                def _(sy):
                    sy.dma_start(out=brow[0:1, :], in_=ab).then_inc(H["dsy"], 16)
                    sy.dma_start(out=brow[1:2, :], in_=ab).then_inc(H["dsy"], 16)
                    V["dsy"] += 32
                    sy.wait_ge(H["vv"], bs["vv"] + 7)
                    for b in range(B):
                        sy.dma_start(out=beff[b].ap()[0:1, :],
                                     in_=befb[b:b + 1, :]).then_inc(H["dsy"], 16)
                    V["dsy"] += 32

        if STAGE <= 1:
            return nc

        def ln_block(seq, src_x, spill, second):
            with nc.Block() as blk:
                bs = snap()
                xt_ = [xt[0][:, :], xt[1][:, :]]
                stt_ = [stt[0][:, :], stt[1][:, :]]
                out_ = ([nrm[0][:, :], nrm[1][:, :]] if second
                        else [h1t[0][:, :], h1t[1][:, :]])
                ht2 = [ht[0][:, 0:D], ht[1][:, 0:D]] if second else None
                g_i, v_i, s_i, sy_i, done, _tg = make_ln_chain(
                    bs, seq, second, src_x, spill, xt_, stt_, out_, ht2,
                    "vv", "sc", "dgx", "dh1")

                @blk.gpsimd
                def _(g):
                    for i in range(NT):
                        g_i(g, i)

                @blk.vector
                def _(v):
                    for i in range(NT):
                        v_i(v, i)

                @blk.scalar
                def _(s):
                    for i in range(NT):
                        s_i(s, i)

                @blk.sync
                def _(sy):
                    for i in range(NT):
                        sy_i(sy, i)
                done()

        def make_wprep_chain(bs, tiles):
            def g_k(g, k):
                src_, _dst, j = tiles[k]
                if k >= 2:
                    g.wait_ge(H["scw"], bs["scw"] + (k - 2) + 1)
                g.dma_start(out=wpt[k % 2],
                            in_=src_[j * 128:(j + 1) * 128, :]
                            ).then_inc(H[f"dwtS{k % 2}"], 16)

            def v_k(v, k):
                st = wpst[k % 2]
                v.wait_ge(H[f"dwtS{k % 2}"],
                          bs[f"dwtS{k % 2}"] + 16 * (k // 2 + 1))
                v.tensor_reduce(st[:, 0:1], wpt[k % 2], AX, ALU.max,
                                apply_absolute_value=True)
                v.drain()
                v.tensor_scalar(st[:, 1:2], st[:, 0:1], 1.0 / 127.0, None, ALU.mult)
                v.drain()
                v.reciprocal(st[:, 2:3], st[:, 1:2])
                v.drain().then_inc(H["vvw"], 1)
                v.wait_ge(H["scw"], bs["scw"] + k + 1)
                v.wait_ge(H[f"dwq{k % 2}"], bs[f"dwq{k % 2}"]
                          + (16 * ((k - 2) // 2 + 1) if k >= 2 else 0))
                v.tensor_copy(wpi[:, :], wptm[:, :])
                v.drain()
                v.tensor_scalar(wpq[k % 2], wpi[:, :], st[:, 1:2],
                                None, ALU.mult)
                v.drain().then_inc(H["vvw"], 1)

            def s_k(s, k):
                s.wait_ge(H["vvw"], bs["vvw"] + 2 * k + 1)
                if k >= 1:
                    s.wait_ge(H["vvw"], bs["vvw"] + 2 * (k - 1) + 2)
                s.drain()
                s.activation(wptm[:, :], wpt[k % 2], ACT.Identity,
                             bias=zc[:, 0:1],
                             scale=wpst[k % 2][:, 2:3]).then_inc(H["scw"], 1)

            def sy_k(sy, k):
                _src, dst, j = tiles[k]
                sy.wait_ge(H["vvw"], bs["vvw"] + 2 * k + 2)
                sy.dma_start(out=dst[j * 128:(j + 1) * 128, :],
                             in_=wpq[k % 2]).then_inc(H[f"dwq{k % 2}"], 16)

            def done(n):
                V["vvw"] += 2 * n
                V["scw"] += n
                V["dwtS0"] += ((n + 1) // 2) * 16
                V["dwtS1"] += (n // 2) * 16
                V["dwq0"] += ((n + 1) // 2) * 16
                V["dwq1"] += (n // 2) * 16

            return g_k, v_k, s_k, sy_k, done

        def emit_seq(seq, upto=4, wprep_tiles=None, attn_extras=(),
                     qkv_extras=()):
            with nc.Block() as blk:
                bs = snap()

                @blk.scalar
                def _(s):
                    for nm in ("dh10", "dh11", "dh20", "dh21"):
                        s.wait_ge(H[nm], bs[nm])
                    for gch in range(4):
                        for dc in range(8):
                            s.dma_start(
                                out=h1T[:, dc, gch * 512:(gch + 1) * 512],
                                in_=h1s[gch * 512:(gch + 1) * 512,
                                        dc * 128:(dc + 1) * 128],
                                transpose=True,
                            ).then_inc(H["dsc"], 16)
                    V["dsc"] += 32 * 16

                @blk.vector
                def _(v):
                    v.wait_ge(H["dsc"], bs["dsc"] + 32 * 16)
                    for dc in range(8):
                        v.tensor_scalar(h1T[:, dc, :], h1T[:, dc, :],
                                        ca1T[:, dc, seq:seq + 1], None, ALU.mult)
                    v.drain().then_inc(H["vv"], 1)
                    V["vv"] += 1
            if upto <= 2:
                return

            with (
                nc.sbuf_tensor(f"woc{seq}", [128, 2, 8, 128], BF16) as woc,
                nc.sbuf_tensor(f"wvv{seq}", [128, 1, D], BF16) as wvv,
                nc.sbuf_tensor(f"garep{seq}", [128, D], BF16) as garep,
                nc.Block() as blk,
            ):
                bs = snap()
                nwp = len(wprep_tiles) if wprep_tiles else 0
                if nwp:
                    wg_k, wv_k, ws_k, wsy_k, wdone = make_wprep_chain(bs, wprep_tiles)
                qchains = [f(bs) for f in qkv_extras]
                alloc = [[] for _ in range(16)]
                for k in range(nwp):
                    alloc[k % 16].append(k) if False else None
                kk = 0
                for oc in range(16):
                    take = (nwp + 15 - oc) // 16
                    alloc[oc] = list(range(kk, kk + take))
                    kk += take

                @blk.gpsimd
                def _(g):
                    g.wait_ge(H["vv2"], bs["vv2"])
                    g.wait_ge(H["sc2"], bs["sc2"])
                    for dc in range(8):
                        if dc >= 1:
                            g.wait_ge(H["vv"], bs["vv"] + dc + 2)
                        g.dma_start(
                            out=wvv[:, 0, :],
                            in_=aw[dc * 128:(dc + 1) * 128, 2 * D:3 * D],
                        ).then_inc(H[f"dwv{dc % 2}"], 16)
                    for oc in range(16):
                        if oc >= 2:
                            g.wait_ge(H["tt"], bs["tt"] + oc + 1)
                        for dc in range(8):
                            g.dma_start(
                                out=woc[:, oc % 2, dc, :],
                                in_=aw[dc * 128:(dc + 1) * 128,
                                       oc * 128:(oc + 1) * 128],
                            ).then_inc(H[f"dwo{oc % 2}"], 16)
                    V["dwv0"] += 4 * 16
                    V["dwv1"] += 4 * 16
                    V["dwo0"] += 8 * 128
                    V["dwo1"] += 8 * 128

                @blk.tensor
                def _(t):
                    t.wait_ge(H["dsy"], bs["dsy"])
                    for hch in range(2):
                        t.matmul(pk[hch][:, :], onesb[0:1, 0:128],
                                 gar[seq].ap()[0:1, hch * 512:(hch + 1) * 512],
                                 start=True, stop=True)
                        t.drain().then_inc(H["tt"], 1)
                    for oc in range(16):
                        t.wait_ge(H[f"dwo{oc % 2}"],
                                  bs[f"dwo{oc % 2}"] + 128 * (oc // 2 + 1))
                        if oc >= 1:
                            t.wait_ge(H["vv"], bs["vv"] + 11 + 4 * oc)
                        for dc in range(8):
                            for gch in range(4):
                                t.matmul(pk[gch][:, :], woc[:, oc % 2, dc, :],
                                         h1T[:, dc, gch * 512:(gch + 1) * 512],
                                         start=(dc == 0), stop=False)
                        for gch in range(4):
                            t.matmul(pk[gch][:, :],
                                     beff[seq].ap()[0:1, oc * 128:(oc + 1) * 128],
                                     onesb[0:1, :], start=False, stop=True)
                        t.drain().then_inc(H["tt"], 1)
                    for ti in range(NT):
                        t.wait_ge(H["vv"], bs["vv"] + 11 + 64)
                        if ti >= 1:
                            t.wait_ge(H["vv"], bs["vv"] + 11 + 64 + 2 * ti)
                        for dc in range(8):
                            t.matmul(pk[4][:, :],
                                     h1T[:, dc, ti * 128:(ti + 1) * 128],
                                     wvs[:, dc, 0:512], start=(dc == 0), stop=False)
                            t.matmul(pk[5][:, :],
                                     h1T[:, dc, ti * 128:(ti + 1) * 128],
                                     wvs[:, dc, 512:1024], start=(dc == 0), stop=False)
                        t.matmul(pk[4][:, :], onesb[0:1, 0:128],
                                 beff[seq].ap()[0:1, 2 * D:2 * D + 512],
                                 start=False, stop=True)
                        t.matmul(pk[5][:, :], onesb[0:1, 0:128],
                                 beff[seq].ap()[0:1, 2 * D + 512:3 * D],
                                 start=False, stop=True)
                        t.drain().then_inc(H["tt"], 1)
                    V["tt"] += 2 + 16 + NT

                @blk.scalar
                def _(s):
                    s.wait_ge(H["dou0"], bs["dou0"])
                    s.wait_ge(H["dou1"], bs["dou1"])
                    s.wait_ge(H["sc2"], bs["sc2"])
                    s.wait_ge(H["dh20"], bs["dh20"])
                    s.wait_ge(H["dh21"], bs["dh21"])
                    for oc in range(16):
                        for k in alloc[oc]:
                            wg_k(s, k)
                        for ch in qchains:
                            ch[0](s, oc)
                        for k in alloc[oc]:
                            ws_k(s, k)
                        if oc >= 1:
                            for ch in qchains:
                                ch[2](s, oc - 1)
                    for ch in qchains:
                        ch[2](s, 15)

                @blk.sync
                def _(sy):
                    for nm in ("dsc", "dh10", "dh11", "dh20", "dh21"):
                        sy.wait_ge(H[nm], bs[nm])
                    for oc in range(16):
                        for k in alloc[oc]:
                            wsy_k(sy, k)
                        if oc >= 1:
                            for ch in qchains:
                                ch[3](sy, oc - 1)
                    for ch in qchains:
                        ch[3](sy, 15)
                if nwp:
                    wdone(nwp)
                for ch in qchains:
                    ch[4]()

                @blk.vector
                def _(v):
                    v.wait_ge(H["vv2"], bs["vv2"])
                    v.wait_ge(H["sc2"], bs["sc2"])
                    v.wait_ge(H["dh20"], bs["dh20"])
                    v.wait_ge(H["dh21"], bs["dh21"])
                    for hch in range(2):
                        v.wait_ge(H["tt"], bs["tt"] + hch + 1)
                        v.tensor_copy(garep[:, hch * 512:(hch + 1) * 512],
                                      pk[hch][:, :])
                        v.drain().then_inc(H["vv"], 1)
                    v.drain()
                    for dc in range(8):
                        v.wait_ge(H[f"dwv{dc % 2}"],
                                  bs[f"dwv{dc % 2}"] + 16 * (dc // 2 + 1))
                        v.tensor_tensor(wvs[:, dc, :], wvv[:, 0, :],
                                        garep[:, :], ALU.mult)
                        v.drain().then_inc(H["vv"], 1)
                    v.drain().then_inc(H["vv"], 1)
                    for oc in range(16):
                        v.wait_ge(H["tt"], bs["tt"] + 2 + oc + 1)
                        for gch in range(4):
                            v.tensor_copy(qkT[:, oc, gch * 512:(gch + 1) * 512],
                                          pk[gch][:, :])
                            v.drain().then_inc(H["vv"], 1)
                        for k in alloc[oc]:
                            wv_k(v, k)
                        if oc >= 1:
                            for ch in qchains:
                                ch[1](v, oc - 1)
                    for ch in qchains:
                        ch[1](v, 15)
                    for ti in range(NT):
                        if ti == 6:
                            for ch in qchains:
                                ch[5](v)
                        v.wait_ge(H["tt"], bs["tt"] + 2 + 16 + ti + 1)
                        v.tensor_copy(vsb[:, ti, 0:512], pk[4][:, :])
                        v.drain().then_inc(H["vv"], 1)
                        v.tensor_copy(vsb[:, ti, 512:1024], pk[5][:, :])
                        v.drain().then_inc(H["vv"], 1)
                    V["vv"] += 11 + 64 + 2 * NT
            if upto <= 3:
                return

            with (
                nc.sbuf_tensor(f"Pt{seq}", [128, 2, T], BF16) as Pt,
                nc.Block() as blk,
            ):
                bs = snap()
                x1t = nrm
                rst = idft.ap()[:, 0:32].rearrange("p (a b) -> p a b", b=16)
                h1b_ = h1T.ap().rearrange("p a b -> p (a b)")
                if seq == 0:
                    PTs = [h1b_[:, 6400:8448], h1b_[:, 8448:10496]]
                else:
                    PTs = [h1b_[:, 10240:12288], h1b_[:, 14336:16384]]
                PTt = [p.rearrange("p (a b) -> p a b", b=128) for p in PTs]

                tts = {0: 1}
                ttpv = {}
                pos = 1
                for qt in range(1, NT):
                    pos += 1
                    tts[qt] = pos
                    pos += 1
                    ttpv[qt - 1] = pos
                pos += 1
                ttpv[NT - 1] = pos
                ne = [qt // 2 + 1 for qt in range(NT)]
                no = [(qt + 1) // 2 for qt in range(NT)]
                cum_e = {}
                cum_o = {}
                tot = {0: [0, 0], 1: [0, 0]}
                for qt in range(NT):
                    p = qt % 2
                    tot[p][0] += ne[qt]
                    tot[p][1] += no[qt]
                    cum_e[qt] = tot[p][0]
                    cum_o[qt] = tot[p][1]

                chains = [f(bs) for f in attn_extras]

                @blk.gpsimd
                def _(g):
                    for qt in range(NT):
                        if qt >= 2:
                            g.wait_ge(H["vv"], bs["vv"] + 3 * (qt - 2) + 3)
                        g.dma_start(out=xt[qt % 2][:, :],
                                    in_=x_in[seq, qt * 128:(qt + 1) * 128, :]
                                    ).then_inc(H[f"dgx{qt % 2}"], 16)
                        for ch in chains:
                            ch[0](g, qt)
                    V["dgx0"] += NT * 8
                    V["dgx1"] += NT * 8

                @blk.tensor
                def _(t):
                    def emit_S(qt):
                        nk = NKC[qt]
                        if qt >= 1:
                            t.wait_ge(H["sc"], bs["sc"] + qt)
                        for dc in range(8):
                            for kc in range(nk):
                                w = KCW[qt][kc]
                                t.matmul(pk[kc][:, 0:w],
                                         qkT[:, dc, qt * 128:(qt + 1) * 128],
                                         qkT[:, 8 + dc, kc * 512:kc * 512 + w],
                                         start=(dc == 0), stop=(dc == 7))
                        t.drain().then_inc(H["tt"], 1)

                    def emit_PV(qt):
                        t.wait_ge(H[f"dpt{qt % 2}"],
                                  bs[f"dpt{qt % 2}"] + 16 * cum_e[qt])
                        t.wait_ge(H[f"dpts{qt % 2}"],
                                  bs[f"dpts{qt % 2}"] + 16 * cum_o[qt])
                        if qt >= 1:
                            t.wait_ge(H["vv"], bs["vv"] + 3 * (qt - 1) + 3)
                        for kc in range(qt + 1):
                            t.matmul(pk[4][:, :], PTt[qt % 2][:, kc, :],
                                     vsb[:, kc, 0:512],
                                     start=(kc == 0), stop=(kc == qt))
                            t.matmul(pk[5][:, :], PTt[qt % 2][:, kc, :],
                                     vsb[:, kc, 512:1024],
                                     start=(kc == 0), stop=(kc == qt))
                        t.drain().then_inc(H["tt"], 1)

                    emit_S(0)
                    for qt in range(1, NT):
                        emit_S(qt)
                        emit_PV(qt - 1)
                    emit_PV(NT - 1)
                    V["tt"] += 2 * NT

                @blk.vector
                def _(v):
                    v.wait_ge(H["vv2"], bs["vv2"])
                    v.wait_ge(H["sc2"], bs["sc2"])
                    for qt in range(NT):
                        nk = NKC[qt]
                        st = rst[:, qt % 2, :]
                        v.wait_ge(H["tt"], bs["tt"] + tts[qt])
                        if qt >= 2:
                            v.wait_ge(H["sc"], bs["sc"] + qt - 1)
                        dcol = qt * 128 - (nk - 1) * 512
                        v.tensor_tensor(pk[nk - 1][:, dcol:dcol + 128],
                                        pk[nk - 1][:, dcol:dcol + 128],
                                        maskt[:, :], ALU.add)
                        v.drain()
                        for kc in range(nk):
                            v.reduce_max(st[:, kc:kc + 1],
                                         pk[kc][:, 0:KCW[qt][kc]], AX)
                        v.drain()
                        v.reduce_max(st[:, 4:5], st[:, 0:nk], AX)
                        v.drain()
                        v.tensor_scalar(st[:, 5:6], st[:, 4:5], -1.0 / 32.0,
                                        None, ALU.mult)
                        v.drain().then_inc(H["vv"], 1)
                        v.wait_ge(H["sc"], bs["sc"] + qt + 1)
                        v.reduce_sum(st[:, 13:14], st[:, 8:8 + nk], AX)
                        v.drain()
                        v.reciprocal(st[:, 14:15], st[:, 13:14])
                        v.drain().then_inc(H["vv"], 1)
                        v.wait_ge(H["tt"], bs["tt"] + ttpv[qt])
                        if seq == 1 and qt >= 2 and chains:
                            v.wait_ge(H["sc3"], bs["sc3"] + 3 * (qt - 2) + 3)
                        v.wait_ge(H[f"dgx{qt % 2}"],
                                  bs[f"dgx{qt % 2}"] + 16 * (qt // 2 + 1))
                        if qt >= 2:
                            v.wait_ge(H[f"dou{qt % 2}"],
                                      bs[f"dou{qt % 2}"] + 16 * ((qt - 2) // 2 + 1))
                        else:
                            v.wait_ge(H[f"dou{qt % 2}"], bs[f"dou{qt % 2}"])
                        v.scalar_tensor_tensor(x1t[qt % 2][:, 0:512], pk[4][:, :],
                                               st[:, 14:15], xt[qt % 2][:, 0:512],
                                               ALU.mult, ALU.add)
                        v.scalar_tensor_tensor(x1t[qt % 2][:, 512:1024], pk[5][:, :],
                                               st[:, 14:15], xt[qt % 2][:, 512:1024],
                                               ALU.mult, ALU.add)
                        v.drain().then_inc(H["vv"], 1)
                        for ch in chains:
                            ch[1](v, qt)
                    V["vv"] += 3 * NT

                @blk.scalar
                def _(s):
                    s.wait_ge(H["vv2"], bs["vv2"])
                    s.wait_ge(H["sc2"], bs["sc2"])
                    for qt in range(NT):
                        nk = NKC[qt]
                        st = rst[:, qt % 2, :]
                        s.wait_ge(H["vv"], bs["vv"] + 3 * qt + 1)
                        if qt >= 2:
                            s.wait_ge(H[f"dps{qt % 2}"],
                                      bs[f"dps{qt % 2}"] + 16 * ((qt - 2) // 2 + 1))
                        for kc in range(nk):
                            w = KCW[qt][kc]
                            s.activation(Pt[:, qt % 2, kc * 512:kc * 512 + w],
                                         pk[kc][:, 0:w], ACT.Exp,
                                         bias=st[:, 5:6], scale=1.0 / 32.0,
                                         accum_out=st[:, 8 + kc:9 + kc])
                        s.drain().then_inc(H["sc"], 1)
                        s.wait_ge(H[f"dps{qt % 2}"],
                                  bs[f"dps{qt % 2}"] + 16 * (qt // 2 + 1))
                        if qt >= 2:
                            s.wait_ge(H["tt"], bs["tt"] + ttpv[qt - 2])
                        for kc in range(1, qt + 1, 2):
                            s.dma_start(out=PTt[qt % 2][:, kc, :],
                                        in_=pss[qt % 2, :, kc * 128:(kc + 1) * 128],
                                        transpose=True).then_inc(H[f"dpts{qt % 2}"], 16)
                        V[f"dpts{qt % 2}"] += 16 * no[qt]
                        for ch in chains:
                            ch[2](s, qt)
                    V["sc"] += NT

                @blk.sync
                def _(sy):
                    for qt in range(NT):
                        sy.wait_ge(H["sc"], bs["sc"] + qt + 1)
                        if qt >= 2:
                            sy.wait_ge(H["tt"], bs["tt"] + ttpv[qt - 2])
                            sy.wait_ge(H[f"dpts{qt % 2}"],
                                       bs[f"dpts{qt % 2}"] + 16 * cum_o.get(qt - 2, 0))
                            sy.wait_ge(H[f"dpt{qt % 2}"],
                                       bs[f"dpt{qt % 2}"] + 16 * cum_e.get(qt - 2, 0))
                        sy.dma_start(out=pss[qt % 2, :, 0:KEYS[qt]],
                                     in_=Pt[:, qt % 2, 0:KEYS[qt]]
                                     ).then_inc(H[f"dps{qt % 2}"], 16)
                        V[f"dps{qt % 2}"] += 16
                        sy.wait_ge(H[f"dps{qt % 2}"], V[f"dps{qt % 2}"])
                        for kc in range(0, qt + 1, 2):
                            sy.dma_start(out=PTt[qt % 2][:, kc, :],
                                         in_=pss[qt % 2, :, kc * 128:(kc + 1) * 128],
                                         transpose=True).then_inc(H[f"dpt{qt % 2}"], 16)
                        V[f"dpt{qt % 2}"] += 16 * ne[qt]
                        sy.wait_ge(H["vv"], bs["vv"] + 3 * qt + 3)
                        sy.dma_start(out=out[seq, qt * 128:(qt + 1) * 128, :],
                                     in_=x1t[qt % 2][:, :]).then_inc(H[f"dou{qt % 2}"], 16)
                        V[f"dou{qt % 2}"] += 16
                        for ch in chains:
                            ch[3](sy, qt)
                for ch in chains:
                    ch[4]()

        with (
            nc.sbuf_tensor("qkT", [128, 16, T], BF16) as qkT,
            nc.sbuf_tensor("vsb", [128, 16, D], BF16) as vsb,
            nc.sbuf_tensor("wvs", [128, 8, D], BF16) as wvs,
            nc.sbuf_tensor("h1T", [128, 8, T], BF16) as h1T,
            nc.sbuf_tensor("nrm0", [128, D], F32) as _n0,
            nc.sbuf_tensor("nrm1", [128, D], F32) as _n1,
            nc.sbuf_tensor("cb1repA", [128, D], BF16) as _cr0,
            nc.sbuf_tensor("cb1repB", [128, D], BF16) as _cr1,
            nc.sbuf_tensor("sbrepA", [128, D], BF16) as _sr0,
            nc.sbuf_tensor("sbrepB", [128, D], BF16) as _sr1,
        ):
            nrm = [_n0, _n1]
            cb1rep = [_cr0, _cr1]
            sbrep = [_sr0, _sr1]

            with nc.Block() as blk:
                bs = snap()

                @blk.tensor
                def _(t):
                    j = 0
                    for b in range(B):
                        for src_row, _d in ((cbr[b], cb1rep[b]),
                                            (sbr[b], sbrep[b])):
                            for hch in range(2):
                                if j >= 2:
                                    t.wait_ge(H["vv"], bs["vv"] + j - 1)
                                t.matmul(pk[j % 2][:, :], onesb[0:1, 0:128],
                                         src_row.ap()[0:1, hch * 512:
                                                      (hch + 1) * 512],
                                         start=True, stop=True)
                                t.drain().then_inc(H["tt"], 1)
                                j += 1
                    V["tt"] += 8

                @blk.vector
                def _(v):
                    j = 0
                    for b in range(B):
                        for _s, dst in ((cbr[b], cb1rep[b]),
                                        (sbr[b], sbrep[b])):
                            for hch in range(2):
                                v.wait_ge(H["tt"], bs["tt"] + j + 1)
                                v.tensor_copy(
                                    dst.ap()[:, hch * 512:(hch + 1) * 512],
                                    pk[j % 2][:, :])
                                v.drain().then_inc(H["vv"], 1)
                                j += 1
                    V["vv"] += 8

            wpt = [xt[0].ap()[:, :], xt[1].ap()[:, :]]
            wpst = [stt[0].ap()[:, 0:4], stt[1].ap()[:, 0:4]]
            wptm = nrm[0].ap()[:, :]
            wpi = nrm[1].ap().bitcast(I32)
            wpq = [h1t[0].ap()[:, :], h1t[1].ap()[:, :]]

            h1f = h1T.ap().rearrange("p a b -> p (a b)").bitcast(F32)
            h1h = h1T.ap().rearrange("p a b -> p (a b)").bitcast(F16)
            h1b = h1T.ap().rearrange("p a b -> p (a b)")
            wvf = wvs.ap().rearrange("p a b -> p (a b)").bitcast(F32)

            wp_tiles_g = [(wg_in, wgq, j) for j in range(22)]
            wp_tiles_u = [(wu_in, wuq, j) for j in range(22)]

            vsbb = vsb.ap().rearrange("p a b -> p (a b)")
            vsbf = vsb.ap().rearrange("p a b -> p (a b)").bitcast(F32)
            vsbh = vsb.ap().rearrange("p a b -> p (a b)").bitcast(F16)

            lnx_ = [vsbf[:, 3072:4096], vsbf[:, 4096:5120]]

            def ln1_s1_factory(bs):
                xt_ = lnx_
                out_ = [vsbb[:, 10240:11264], vsbb[:, 11264:12288]]
                stt_ = [stt_b[0].ap()[:, :], stt_b[1].ap()[:, :]]
                return make_ln_chain(
                    bs, 1, False,
                    lambda i: x_in[1, i * 128:(i + 1) * 128, :],
                    lambda i: h1s[i * 128:(i + 1) * 128, :],
                    xt_, stt_, out_, None, "vv2", "sc2", "dlxA", "dh2")

            def ln2_s0_factory(bs):
                xt_ = lnx_
                out_ = [vsbf[:, 5120:6144], vsbf[:, 6144:7168]]
                ht_ = [vsbh[:, 14336:15360], vsbh[:, 15360:16384]]
                stt_ = [stt_b[0].ap()[:, :], stt_b[1].ap()[:, :]]

                def xw(g, i):
                    g.wait_ge(H[f"dou{i % 2}"], bs[f"dou{i % 2}"])
                return make_ln_chain(
                    bs, 0, True,
                    lambda i: out[0, i * 128:(i + 1) * 128, :],
                    lambda i: hs[i * 128:(i + 1) * 128, :],
                    xt_, stt_, out_, ht_, "vv2", "sc2", "dlxA", "dh1",
                    extra_wait=xw)

            def ln2_s1_factory(bs):
                xt_ = [nrm[0].ap()[:, :], nrm[1].ap()[:, :]]
                out_ = [wvf[:, 2048:3072], wvf[:, 3072:4096]]
                ht_ = [h1h[:, 12288:13312], h1h[:, 13312:14336]]
                stt_ = [stt_b[0].ap()[:, :], stt_b[1].ap()[:, :]]

                def xr_(eng, i):
                    eng.wait_ge(H["vv"], bs["vv"] + 3 * i + 3)
                return make_ln_chain(
                    bs, 1, True, None,
                    lambda i: hs[T + i * 128:T + (i + 1) * 128, :],
                    xt_, stt_, out_, ht_, "vv3", "sc3", "dlxB", "dh2",
                    x_ready=xr_)

            if STAGE >= 3:
                emit_seq(0, upto=min(STAGE - 1, 4), wprep_tiles=wp_tiles_g,
                         qkv_extras=([ln1_s1_factory] if STAGE >= 6 else ()))
            if STAGE >= 6:
                emit_seq(1, wprep_tiles=wp_tiles_u,
                         qkv_extras=([ln2_s0_factory] if STAGE >= 7 else ()),
                         attn_extras=([ln2_s1_factory] if STAGE >= 7 else ()))
            if STAGE <= 7:
                return nc

        ring.close()
        mid.close()
        def global_scale(col, ccio, sig_idx, scl_lo, pre_s=None):
            with nc.Block() as blk:
                bs = snap()

                if pre_s is not None:
                    @blk.scalar
                    def _(s):
                        pre_s(s, bs)

                @blk.sync
                def _(sy):
                    sy.dma_start(out=pmx, in_=gacc[:, col:col + 1]
                                 ).then_inc(H["dsy"], 16)
                    sy.wait_ge(H["dsy"], bs["dsy"] + 16)
                    sy.dma_start(out=gmr[0:1, :], in_=pmx).then_inc(H["dsy"], 16)
                    V["dsy"] += 32

                @blk.vector
                def _(v):
                    v.memset(r16[:, :], 0.0)
                    v.wait_ge(H["dsy"], bs["dsy"] + 32)
                    v.reduce_max(sigf[0:1, 4:5], gmr[0:1, :], AX)
                    v.drain()
                    v.tensor_single_scalar(r16[0:1, :], r16[0:1, :],
                                           sigf[0:1, 4:5], ALU.max)
                    v.drain().then_inc(H["vv"], 1)
                    v.wait_ge(H["dg"], bs["dg"] + 32)
                    v.tensor_tensor(sigf[0:1, 5:6], gmr[0:1, 0:1],
                                    sigf[0:1, sig_idx:sig_idx + 1], ALU.mult)
                    v.drain()
                    v.reciprocal(sigf[0:1, 6:7], sigf[0:1, 5:6])
                    v.drain().then_inc(H["vv"], 1)
                    v.wait_ge(H["tt"], bs["tt"] + 2)
                    v.tensor_copy(scl[:, scl_lo:scl_lo + 1], pk[0][:, 0:1])
                    v.tensor_copy(scl[:, scl_lo + 1:scl_lo + 2], pk[1][:, 0:1])
                    v.drain().then_inc(H["vv"], 1)
                    V["vv"] += 3

                @blk.gpsimd
                def _(g):
                    g.wait_ge(H["vv"], bs["vv"] + 1)
                    g.dma_start(out=ccio[0].ap(), in_=r16[0:1, :]).then_inc(H["dg"], 16)
                    g.wait_ge(H["dg"], bs["dg"] + 16)
                    g.collective_compute(
                        "AllReduce", ALU.max,
                        replica_groups=[list(range(8))],
                        ins=[ccio[0].ap().opt()],
                        outs=[ccio[1].ap().opt()],
                    ).then_inc(H["cc"], 1)
                    V["cc"] += 1
                    g.wait_ge(H["cc"], V["cc"])
                    g.dma_start(out=gmr[0:1, 0:16], in_=ccio[1].ap()
                                ).then_inc(H["dg"], 16)
                    V["dg"] += 32

                @blk.tensor
                def _(t):
                    t.wait_ge(H["vv"], bs["vv"] + 2)
                    t.matmul(pk[0][:, 0:1], ones32[0:1, :], sigf[0:1, 5:6],
                             start=True, stop=True)
                    t.matmul(pk[1][:, 0:1], ones32[0:1, :], sigf[0:1, 6:7],
                             start=True, stop=True)
                    t.drain().then_inc(H["tt"], 1)
                    t.drain().then_inc(H["tt"], 1)
                    V["tt"] += 2

        if STAGE <= 8:
            return nc
        global_scale(0, ccb[0], 2, 0)
        if STAGE <= 9:
            return nc

        with (
            nc.sbuf_tensor("xq", [128, 8, B * T], BF16) as xq,
            nc.sbuf_tensor("wgT", [128, 8, DI], BF16) as wgT,
            nc.sbuf_tensor("wuT", [128, 8, DI], BF16) as wuT,
        ):
            with (
                nc.sbuf_tensor("hT", [128, 2, B * T], F16) as hT,
                nc.sbuf_tensor("atm", [128, 2, 2048], F32) as atm,
                nc.sbuf_tensor("qi", [128, 2, 2048], I32) as qi,
                nc.Block() as blk,
            ):
                bs = snap()

                @blk.scalar
                def _(s):
                    for nm in ("dh10", "dh11", "dh20", "dh21",
                               "dwq0", "dwq1", "dou0", "dou1"):
                        s.wait_ge(H[nm], bs[nm])
                    s.wait_ge(H["vv"], bs["vv"])
                    for dc in range(8):
                        s.dma_start(out=hT[:, dc % 2, :],
                                    in_=hs[0:B * T, dc * 128:(dc + 1) * 128],
                                    transpose=True).then_inc(H["dsc"], 16)
                        V["dsc"] += 16
                        for hf in range(2):
                            j = 2 * dc + hf
                            s.wait_ge(H["dsc"], V["dsc"])
                            if j >= 2:
                                s.wait_ge(H["vv"], bs["vv"] + j - 1)
                            s.activation(atm[:, hf, :],
                                         hT[:, dc % 2, hf * 2048:(hf + 1) * 2048],
                                         ACT.Identity, bias=zc[:, 0:1],
                                         scale=scl[:, 1:2]).then_inc(H["sc"], 1)
                        s.dma_start(out=wgT[:, dc, :],
                                    in_=wgq[0:DI, dc * 128:(dc + 1) * 128],
                                    transpose=True).then_inc(H["dsc"], 16)
                        s.dma_start(out=wuT[:, dc, :],
                                    in_=wuq[0:DI, dc * 128:(dc + 1) * 128],
                                    transpose=True).then_inc(H["dsc"], 16)
                        V["dsc"] += 32
                    V["sc"] += 16

                @blk.vector
                def _(v):
                    for nm in ("dwq0", "dwq1", "dou0", "dou1"):
                        v.wait_ge(H[nm], bs[nm])
                    for j in range(16):
                        dc, hf = j // 2, j % 2
                        v.wait_ge(H["sc"], bs["sc"] + j + 1)
                        v.tensor_scalar(qi[:, hf, :], atm[:, hf, :],
                                        -128.0, 127.0, ALU.max, ALU.min)
                        v.drain()
                        v.tensor_scalar(xq[:, dc, hf * 2048:(hf + 1) * 2048],
                                        qi[:, hf, :], scl[:, 0:1], None,
                                        ALU.mult).then_inc(H["vv"], 1)
                    V["vv"] += 16

            if STAGE <= 10:
                return nc
            IBW = [1024, 1024, 768]
            IBO = [0, 1024, 2048]
            with (
                nc.sbuf_tensor("sil", [128, 2, 1024], F32) as sil,
                nc.sbuf_tensor("spb", [128, 2, 1024], F16) as spb,
                nc.sbuf_tensor("fst", [128, 2, 2], F32) as fst,
                nc.psum_tensor("pk7", [128, 512], F32) as pk7,
                nc.Block() as blk,
            ):
                bs = snap()
                pkf = pk + [pk7]
                NIT = 32 * 3

                @blk.scalar
                def _(s):
                    load_set(s, "silu_and_others")
                    for it in range(NIT):
                        ib = it % 3
                        w = IBW[ib]
                        pg = (it % 2) * 4
                        s.wait_ge(H["tt"], bs["tt"] + it + 1)
                        if it >= 2:
                            s.wait_ge(H["vv"], bs["vv"] + it - 1)
                        s.activation(sil[:, it % 2, 0:512], pkf[pg][:, :], ACT.Silu)
                        s.activation(sil[:, it % 2, 512:w], pkf[pg + 1][:, 0:w - 512],
                                     ACT.Silu).then_inc(H["sc"], 1)
                    V["sc"] += NIT

                @blk.tensor
                def _(t):
                    t.wait_ge(H["dsc"], bs["dsc"])
                    for it in range(NIT):
                        tt_i, ib = it // 3, it % 3
                        w = IBW[ib]
                        pg = (it % 2) * 4
                        if it >= 2:
                            t.wait_ge(H["vv"], bs["vv"] + (it - 2) + 1)
                        for dc in range(8):
                            lh = xq[:, dc, tt_i * 128:(tt_i + 1) * 128]
                            st_, sp_ = (dc == 0), (dc == 7)
                            t.matmul(pkf[pg][:, :], lh,
                                     wgT[:, dc, IBO[ib]:IBO[ib] + 512],
                                     start=st_, stop=sp_)
                            t.matmul(pkf[pg + 1][:, 0:w - 512], lh,
                                     wgT[:, dc, IBO[ib] + 512:IBO[ib] + w],
                                     start=st_, stop=sp_)
                            t.matmul(pkf[pg + 2][:, :], lh,
                                     wuT[:, dc, IBO[ib]:IBO[ib] + 512],
                                     start=st_, stop=sp_)
                            t.matmul(pkf[pg + 3][:, 0:w - 512], lh,
                                     wuT[:, dc, IBO[ib] + 512:IBO[ib] + w],
                                     start=st_, stop=sp_)
                        t.drain().then_inc(H["tt"], 1)
                    V["tt"] += NIT

                @blk.vector
                def _(v):
                    for it in range(NIT):
                        ib = it % 3
                        w = IBW[ib]
                        pg = (it % 2) * 4
                        v.wait_ge(H["sc"], bs["sc"] + it + 1)
                        if it >= 2:
                            v.wait_ge(H[f"dss{it % 2}"],
                                      bs[f"dss{it % 2}"] + 16 * ((it - 2) // 2 + 1))
                        v.tensor_tensor(spb[:, it % 2, 0:512], sil[:, it % 2, 0:512],
                                        pkf[pg + 2][:, :], ALU.mult)
                        v.tensor_tensor(spb[:, it % 2, 512:w], sil[:, it % 2, 512:w],
                                        pkf[pg + 3][:, 0:w - 512], ALU.mult)
                        v.drain()
                        v.tensor_reduce(fst[:, it % 2, 0:1], spb[:, it % 2, 0:w], AX,
                                        ALU.max, apply_absolute_value=True)
                        v.drain()
                        v.tensor_tensor(gacc[:, 1:2], gacc[:, 1:2],
                                        fst[:, it % 2, 0:1], ALU.max)
                        v.drain().then_inc(H["vv"], 1)
                    V["vv"] += NIT

                @blk.sync
                def _(sy):
                    for it in range(NIT):
                        tt_i, ib = it // 3, it % 3
                        w = IBW[ib]
                        sy.wait_ge(H["vv"], bs["vv"] + it + 1)
                        sy.dma_start(out=ss[tt_i * 128:(tt_i + 1) * 128,
                                            IBO[ib]:IBO[ib] + w],
                                     in_=spb[:, it % 2, 0:w]).then_inc(H[f"dss{it % 2}"], 16)
                    V["dss0"] += NIT * 8
                    V["dss1"] += NIT * 8

        if STAGE <= 11:
            return nc

        with (
            nc.sbuf_tensor("dwt", [128, 2, DI], F32) as dwt,
            nc.sbuf_tensor("dstt", [128, 2, 4], F32) as dstt,
            nc.sbuf_tensor("dtmp", [128, DI], F32) as dtmp,
            nc.sbuf_tensor("di32", [128, DI], I32) as di32,
            nc.sbuf_tensor("dqb", [128, 2, B, DI], BF16) as dqb,
            nc.Block() as blk,
        ):
            bs = snap()

            @blk.gpsimd
            def _(g):
                for dc in range(8):
                    if dc >= 2:
                        g.wait_ge(H["scw"], bs["scw"] + (dc - 2) + 1)
                    g.dma_start(out=dwt[:, dc % 2, :],
                                in_=wd_in[dc * 128:(dc + 1) * 128, :]
                                ).then_inc(H[f"dwt{dc % 2}"], 16)
                V["dwt0"] += 4 * 16
                V["dwt1"] += 4 * 16

            @blk.vector
            def _(v):
                for dc in range(8):
                    st = dstt[:, dc % 2, :]
                    v.wait_ge(H[f"dwt{dc % 2}"], bs[f"dwt{dc % 2}"] + 16 * (dc // 2 + 1))
                    v.tensor_reduce(st[:, 0:1], dwt[:, dc % 2, :], AX, ALU.max,
                                    apply_absolute_value=True)
                    v.drain()
                    v.tensor_scalar(st[:, 1:2], st[:, 0:1], 1.0 / 127.0, None, ALU.mult)
                    v.drain()
                    v.reciprocal(st[:, 2:3], st[:, 1:2])
                    v.tensor_tensor(st[:, 3:4], st[:, 1:2], gbT[:, dc, 0:1], ALU.mult)
                    v.tensor_tensor(st[:, 0:1], st[:, 1:2], gbT[:, dc, 1:2], ALU.mult)
                    v.drain().then_inc(H["vvw"], 1)
                    v.wait_ge(H["scw"], bs["scw"] + dc + 1)
                    if dc >= 2:
                        v.wait_ge(H[f"dwq{dc % 2}"],
                                  bs[f"dwq{dc % 2}"] + 32 * ((dc - 2) // 2 + 1))
                    v.tensor_copy(di32[:, :], dtmp[:, :])
                    v.drain()
                    v.tensor_scalar(dqb[:, dc % 2, 0, :], di32[:, :],
                                    st[:, 3:4], None, ALU.mult)
                    v.tensor_scalar(dqb[:, dc % 2, 1, :], di32[:, :],
                                    st[:, 0:1], None, ALU.mult)
                    v.drain().then_inc(H["vvw"], 1)
                V["vvw"] += 16

            @blk.scalar
            def _(s):
                for dc in range(8):
                    s.wait_ge(H["vvw"], bs["vvw"] + 2 * dc + 1)
                    if dc >= 1:
                        s.wait_ge(H["vvw"], bs["vvw"] + 2 * (dc - 1) + 2)
                    s.activation(dtmp[:, :], dwt[:, dc % 2, :], ACT.Identity,
                                 bias=zc[:, 0:1],
                                 scale=dstt[:, dc % 2, 2:3]).then_inc(H["scw"], 1)
                V["scw"] += 8

            @blk.sync
            def _(sy):
                for dc in range(8):
                    sy.wait_ge(H["vvw"], bs["vvw"] + 2 * dc + 2)
                    for b in range(B):
                        sy.dma_start(out=wdq[b, dc * 128:(dc + 1) * 128, :],
                                     in_=dqb[:, dc % 2, b, :]).then_inc(H[f"dwq{dc % 2}"], 16)
                V["dwq0"] += 4 * 32
                V["dwq1"] += 4 * 32

        if STAGE <= 12:
            return nc
        with (
            nc.sbuf_tensor("wdT", [128, B, 22, D], BF16) as wdT,
            nc.sbuf_tensor("sT", [128, 2, 22, 512], F16) as sT,
            nc.sbuf_tensor("h2q", [128, 2, 22, 512], BF16) as h2q,
            nc.sbuf_tensor("qtm", [128, 2, 512], F32) as qtm,
            nc.sbuf_tensor("qii", [128, 2, 512], I32) as qii,
            nc.sbuf_tensor("xr0", [128, D], F32) as _r0,
            nc.sbuf_tensor("xr1", [128, D], F32) as _r1,
        ):
            xr = [_r0, _r1]

            def pre_s(s, bsx):
                for nm in ("dwq0", "dwq1", "dss0", "dss1"):
                    s.wait_ge(H[nm], bsx[nm])
                for ic in range(22):
                    s.dma_start(out=sT[:, 0, ic, :],
                                in_=ss[0:512, ic * 128:(ic + 1) * 128],
                                transpose=True).then_inc(H["dsc"], 16)
                s.wait_ge(H["dsc"], bsx["dsc"] + 22 * 16)
                for b in range(B):
                    for ic in range(22):
                        s.dma_start(out=wdT[:, b, ic, :],
                                    in_=wdq[b, 0:D, ic * 128:(ic + 1) * 128],
                                    transpose=True).then_inc(H["dsc"], 16)

            ST0_DSC = V["dsc"] + 22 * 16
            global_scale(1, ccb[1], 3, 2, pre_s)
            V["dsc"] += 66 * 16
            PRE_DSC = V["dsc"]
            if STAGE <= 13:
                return nc

            bs = snap()
            dstk = ExitStack()
            blk = dstk.enter_context(nc.Block())
            NC8 = B * 4
            q_done = {}
            r_done = {}
            pos = 0
            for tc in range(NC8):
                if tc >= 1:
                    for j in range(4 * (tc - 1), 4 * tc):
                        pos += 1
                        r_done[j] = pos
                for ic in range(22):
                    pos += 1
                    q_done[(tc, ic)] = pos
            for j in range(4 * (NC8 - 1), 4 * NC8):
                pos += 1
                r_done[j] = pos

            @blk.scalar
            def _(s):
                s.wait_ge(H["vv"], bs["vv"])
                for tc in range(NC8):
                    if tc >= 1:
                        for ic in range(22):
                            s.dma_start(out=sT[:, tc % 2, ic, :],
                                        in_=ss[tc * 512:(tc + 1) * 512,
                                               ic * 128:(ic + 1) * 128],
                                        transpose=True).then_inc(H["dsc"], 16)
                        V["dsc"] += 22 * 16
                    s.wait_ge(H["dsc"], V["dsc"] if tc >= 1 else ST0_DSC)
                    for ic in range(22):
                        ji = tc * 22 + ic
                        if ji >= 2:
                            prev = (tc * 22 + ic - 2)
                            s.wait_ge(H["vv"], bs["vv"]
                                      + q_done[(prev // 22, prev % 22)])
                        s.activation(qtm[:, ic % 2, :], sT[:, tc % 2, ic, :],
                                     ACT.Identity, bias=zc[:, 0:1],
                                     scale=scl[:, 3:4]).then_inc(H["sc"], 1)
                V["sc"] += NC8 * 22

            @blk.vector
            def _(v):
                for tc in range(NC8):
                    if tc >= 1:
                        for j in range(4 * (tc - 1), 4 * tc):
                            v.wait_ge(H["tt"], bs["tt"] + j + 1)
                            v.wait_ge(H[f"dgx{j % 2}"],
                                      bs[f"dgx{j % 2}"] + 16 * (j // 2 + 1))
                            if j >= 2:
                                v.wait_ge(H[f"dou{j % 2}"],
                                          bs[f"dou{j % 2}"]
                                          + 16 * ((j - 2) // 2 + 1))
                            pg = (j % 2) * 2
                            v.tensor_tensor(xr[j % 2][:, 0:512], pk[pg][:, :],
                                            xr[j % 2][:, 0:512], ALU.add)
                            v.tensor_tensor(xr[j % 2][:, 512:1024], pk[pg + 1][:, :],
                                            xr[j % 2][:, 512:1024], ALU.add)
                            v.drain().then_inc(H["vv"], 1)
                    for ic in range(22):
                        v.wait_ge(H["sc"], bs["sc"] + tc * 22 + ic + 1)
                        v.tensor_scalar(qii[:, ic % 2, :], qtm[:, ic % 2, :],
                                        -128.0, 127.0, ALU.max, ALU.min)
                        v.drain()
                        v.tensor_scalar(h2q[:, tc % 2, ic, :], qii[:, ic % 2, :],
                                        scl[:, 2:3], None,
                                        ALU.mult).then_inc(H["vv"], 1)
                for j in range(4 * (NC8 - 1), 4 * NC8):
                    v.wait_ge(H["tt"], bs["tt"] + j + 1)
                    v.wait_ge(H[f"dgx{j % 2}"],
                              bs[f"dgx{j % 2}"] + 16 * (j // 2 + 1))
                    v.wait_ge(H[f"dou{j % 2}"],
                              bs[f"dou{j % 2}"] + 16 * ((j - 2) // 2 + 1))
                    pg = (j % 2) * 2
                    v.tensor_tensor(xr[j % 2][:, 0:512], pk[pg][:, :],
                                    xr[j % 2][:, 0:512], ALU.add)
                    v.tensor_tensor(xr[j % 2][:, 512:1024], pk[pg + 1][:, :],
                                    xr[j % 2][:, 512:1024], ALU.add)
                    v.drain().then_inc(H["vv"], 1)
                V["vv"] += pos

            @blk.gpsimd
            def _(g):
                for j in range(B * NT):
                    b, ti = j // NT, j % NT
                    g.wait_ge(H[f"dou{j % 2}"], bs[f"dou{j % 2}"]
                              + (16 * ((j - 2) // 2 + 1) if j >= 2 else 0))
                    g.dma_start(out=xr[j % 2][:, :],
                                in_=out[b, ti * 128:(ti + 1) * 128, :]
                                ).then_inc(H[f"dgx{j % 2}"], 16)
                V["dgx0"] += B * NT * 8
                V["dgx1"] += B * NT * 8

            @blk.tensor
            def _(t):
                t.wait_ge(H["dsc"], PRE_DSC)
                for j in range(B * NT):
                    b, ti = j // NT, j % NT
                    tc = (b * NT + ti) // 4
                    tl = ti % 4
                    pg = (j % 2) * 2
                    t.wait_ge(H["vv"], bs["vv"] + q_done[(tc, 21)])
                    if j >= 2:
                        t.wait_ge(H["vv"], bs["vv"] + r_done[j - 2])
                    for ic in range(22):
                        lh = h2q[:, tc % 2, ic, tl * 128:(tl + 1) * 128]
                        t.matmul(pk[pg][:, :], lh, wdT[:, b, ic, 0:512],
                                 start=(ic == 0), stop=(ic == 21))
                        t.matmul(pk[pg + 1][:, :], lh, wdT[:, b, ic, 512:1024],
                                 start=(ic == 0), stop=(ic == 21))
                    t.drain().then_inc(H["tt"], 1)
                V["tt"] += B * NT

            @blk.sync
            def _(sy):
                for j in range(B * NT):
                    b, ti = j // NT, j % NT
                    sy.wait_ge(H["vv"], bs["vv"] + r_done[j])
                    sy.dma_start(out=out[b, ti * 128:(ti + 1) * 128, :],
                                 in_=xr[j % 2][:, :]).then_inc(H[f"dou{j % 2}"], 16)
                V["dou0"] += B * NT * 8
                V["dou1"] += B * NT * 8
                sy.wait_ge(H["dou0"], V["dou0"])
                sy.wait_ge(H["dou1"], V["dou1"])
            dstk.close()

    return nc


def kernel(**inputs):
    x = np.ascontiguousarray(np.asarray(inputs["L0_attn_x"], np.float32))
    W = np.ascontiguousarray(np.asarray(inputs["L0_attn_W"], np.float32))
    b = np.ascontiguousarray(
        np.asarray(inputs["L0_attn_b"], np.float32)).reshape(1, 3 * D)
    cx = np.asarray(inputs["L0_ffn_clip_x"], np.float32).reshape(1)
    cd = np.asarray(inputs["L0_ffn_clip_down"], np.float32).reshape(1)
    wgt = np.ascontiguousarray(np.asarray(inputs["L0_ffn_w_gate"], np.float32))
    wup = np.ascontiguousarray(np.asarray(inputs["L0_ffn_w_up"], np.float32))
    wdn = np.ascontiguousarray(np.asarray(inputs["L0_ffn_w_down"], np.float32))
    c = np.ascontiguousarray(np.asarray(inputs["L0_c"], np.float32))
    wm = np.ascontiguousarray(np.asarray(inputs["L0_wmod"], np.float32))

    clips = np.ascontiguousarray(
        np.concatenate([cx, cd]).reshape(1, 2).astype(np.float32))
    idb_np = np.eye(128, dtype=ml_dtypes.bfloat16)
    idf_np = np.eye(128, dtype=np.float32)
    mask_np = np.triu(np.full((128, 128), NEG, np.float32), k=1)

    nc = build_nc()
    in_maps = []
    for core in range(8):
        in_maps.append(dict(
            x=np.ascontiguousarray(x[core * B:(core + 1) * B]),
            c=np.ascontiguousarray(c[core * B:(core + 1) * B]),
            wmod=wm, aw=W, ab=b, clips=clips,
            wg=wgt, wu=wup, wd=wdn,
            idb=idb_np, idf=idf_np, maskc=mask_np,
        ))
    res = run_bass_kernel_spmd(nc, in_maps, core_ids=list(range(8)))
    return np.concatenate([r["out"] for r in res.results], axis=0)


if __name__ == "__main__":
    nc_ = build_nc()
    n = sum(1 for _ in nc_.all_instructions())
    print("instructions:", n)


# version 22 (final)
# speedup vs baseline: 1.0461x; 1.0232x over previous
from contextlib import ExitStack

import numpy as np
import ml_dtypes
import concourse.bass as bass
import concourse.mybir as mybir
from concourse.bass_utils import run_bass_kernel_spmd
from concourse.hw_specs import get_activation_tables

F32 = mybir.dt.float32
BF16 = mybir.dt.bfloat16
F16 = mybir.dt.float16
I32 = mybir.dt.int32
ALU = mybir.AluOpType
ACT = mybir.ActivationFunctionType
AX = mybir.AxisListType.X

B = 2
T = 2048
D = 1024
DI = 2816
NT = T // 128
NEG = -1.0e9

KEYS = [(qt + 1) * 128 for qt in range(NT)]
NKC = [(k + 511) // 512 for k in KEYS]
KCW = [[min(512, KEYS[qt] - kc * 512) for kc in range(NKC[qt])] for qt in range(NT)]
CUMPT = [0]
for _qt in range(NT):
    CUMPT.append(CUMPT[-1] + _qt + 1)

STAGE = 99


def build_nc():
    nc = bass.Bass("TRN2", target_bir_lowering=False, debug=False)

    x_in = nc.dram_tensor("x", [B, T, D], F32, kind="ExternalInput").ap()
    c_in = nc.dram_tensor("c", [B, D], F32, kind="ExternalInput").ap()
    wmod = nc.dram_tensor("wmod", [D, 6 * D], F32, kind="ExternalInput").ap()
    aw = nc.dram_tensor("aw", [D, 3 * D], F32, kind="ExternalInput").ap()
    ab = nc.dram_tensor("ab", [1, 3 * D], F32, kind="ExternalInput").ap()
    clips = nc.dram_tensor("clips", [1, 2], F32, kind="ExternalInput").ap()
    wg_in = nc.dram_tensor("wg", [DI, D], F32, kind="ExternalInput").ap()
    wu_in = nc.dram_tensor("wu", [DI, D], F32, kind="ExternalInput").ap()
    wd_in = nc.dram_tensor("wd", [D, DI], F32, kind="ExternalInput").ap()
    idb = nc.dram_tensor("idb", [128, 128], BF16, kind="ExternalInput").ap()
    idf = nc.dram_tensor("idf", [128, 128], F32, kind="ExternalInput").ap()
    maskc = nc.dram_tensor("maskc", [128, 128], F32, kind="ExternalInput").ap()
    out = nc.dram_tensor("out", [B, T, D], F32, kind="ExternalOutput").ap()

    h1s = nc.dram_tensor("h1s", [T, D], BF16).ap()
    hs = nc.dram_tensor("hs", [B * T, D], F16).ap()
    pss = nc.dram_tensor("pss", [2, 128, T], BF16).ap()
    ss = nc.dram_tensor("ss", [B * T, DI], F16).ap()
    wgq = nc.dram_tensor("wgq", [DI, D], BF16).ap()
    wuq = nc.dram_tensor("wuq", [DI, D], BF16).ap()
    wdq = nc.dram_tensor("wdq", [B, D, DI], BF16).ap()
    pmx = nc.dram_tensor("pmx", [1, 128], F32).ap()
    ccb = [[nc.dram_tensor(f"cc{j}{d}", [1, 16], F32) for d in "io"] for j in range(2)]

    act_sets = list(get_activation_tables(nc.m.arch).keys())

    def load_set(eng, name):
        eng.add_instruction(mybir.InstLoadActFuncSet(
            name=nc.get_next_instruction_name(),
            act_func_set_id=act_sets.index(name), ins=[], outs=[]))

    with ExitStack() as top:
        names = ["dg", "dsy", "dsc", "tt", "vv", "sc",
                 "vvw", "scw", "cc",
                 "dgx0", "dgx1", "dwv0", "dwv1", "dwo0", "dwo1",
                 "dwt0", "dwt1", "dh10", "dh11", "dps0", "dps1",
                 "dpt0", "dpt1", "dou0", "dou1", "dss0", "dss1",
                 "dwq0", "dwq1", "dpts0", "dpts1",
                 "vv2", "sc2", "dlxA0", "dlxA1",
                 "vv3", "sc3", "dlxB0", "dlxB1", "dh20", "dh21",
                 "dlxC0", "dlxC1", "dwtS0", "dwtS1"]
        H = {}
        V = {}
        for n in names:
            H[n] = top.enter_context(nc.semaphore(n))
            V[n] = 0

        def snap():
            return dict(V)

        sb = top.enter_context
        ones32 = sb(nc.sbuf_tensor("ones32", [1, 128], F32))
        onesb = sb(nc.sbuf_tensor("onesb", [1, 512], BF16))
        zc = sb(nc.sbuf_tensor("zc", [128, 2], F32))
        idbt = sb(nc.sbuf_tensor("idbt", [128, 128], BF16))
        idft = sb(nc.sbuf_tensor("idft", [128, 128], F32))
        maskt = sb(nc.sbuf_tensor("maskt", [128, 128], F32))

        ca1T = sb(nc.sbuf_tensor("ca1T", [128, 8, B], F32))
        gbT = sb(nc.sbuf_tensor("gbT", [128, 8, B], F32))
        sigf = sb(nc.sbuf_tensor("sigf", [1, 8], F32))
        scl = sb(nc.sbuf_tensor("scl", [128, 4], F32))
        gacc = sb(nc.sbuf_tensor("gacc", [128, 2], F32))
        gmr = sb(nc.sbuf_tensor("gmr", [1, 128], F32))
        r16 = sb(nc.sbuf_tensor("r16", [1, 16], F32))

        pk = [sb(nc.psum_tensor(f"pk{i}", [128, 512], F32)) for i in range(7)]

        mid = top.enter_context(ExitStack())
        cbr = [mid.enter_context(nc.sbuf_tensor(f"cbr{b}", [1, D], BF16))
               for b in range(B)]
        sbr = [mid.enter_context(nc.sbuf_tensor(f"sbr{b}", [1, D], BF16))
               for b in range(B)]
        gar = [mid.enter_context(nc.sbuf_tensor(f"gar{b}", [1, D], BF16))
               for b in range(B)]
        beff = [mid.enter_context(nc.sbuf_tensor(f"beff{b}", [1, 3 * D], BF16))
                for b in range(B)]

        ring = top.enter_context(ExitStack())
        rng = ring.enter_context
        xt = [rng(nc.sbuf_tensor(f"xt{i}", [128, D], F32)) for i in range(2)]
        stt = [rng(nc.sbuf_tensor(f"stt{i}", [128, 16], F32)) for i in range(2)]
        stt_b = [rng(nc.sbuf_tensor(f"stt{i}", [128, 16], F32))
                 for i in range(2, 4)]
        h1t = [rng(nc.sbuf_tensor(f"h1t{i}", [128, D], BF16)) for i in range(2)]
        ht = [h1t[0].ap().bitcast(F16), h1t[1].ap().bitcast(F16)]

        def make_ln_chain(bs, seq, second, src_x, spill, xt_, stt_, out_, ht_,
                          SVV, SSC, SDX, SDH, extra_wait=None, x_ready=None):
            spt, vpt = 3, (3 if second else 2)

            def g_i(g, i):
                if src_x is None:
                    return
                if extra_wait is not None:
                    extra_wait(g, i)
                if i >= 2:
                    g.wait_ge(H[SSC], bs[SSC] + spt * (i - 2) + 3)
                g.dma_start(out=xt_[i % 2], in_=src_x(i)).then_inc(
                    H[SDX + str(i % 2)], 16)

            def xwait(eng, i):
                if x_ready is not None:
                    x_ready(eng, i)
                else:
                    eng.wait_ge(H[SDX + str(i % 2)], bs[SDX + str(i % 2)]
                                + 16 * (i // 2 + 1))

            def v_i(v, i):
                st = stt_[i % 2]
                xwait(v, i)
                if i >= 2:
                    v.wait_ge(H[SSC], bs[SSC] + spt * (i - 2) + 3)
                v.reduce_sum(st[:, 0:1], xt_[i % 2], AX)
                v.wait_ge(H[SSC], bs[SSC] + spt * i + 1)
                v.tensor_tensor(st[:, 1:2], st[:, 11:12], st[:, 12:13], ALU.add)
                v.drain()
                v.tensor_scalar(st[:, 2:3], st[:, 0:1], 1.0 / D, None, ALU.mult)
                v.tensor_scalar(st[:, 3:4], st[:, 1:2], 1.0 / D, None, ALU.mult)
                v.drain()
                v.tensor_tensor(st[:, 4:5], st[:, 2:3], st[:, 2:3], ALU.mult)
                v.drain()
                v.tensor_tensor(st[:, 5:6], st[:, 3:4], st[:, 4:5], ALU.subtract)
                v.drain().then_inc(H[SVV], 1)
                v.wait_ge(H[SSC], bs[SSC] + spt * i + 2)
                v.tensor_tensor(st[:, 8:9], st[:, 2:3], st[:, 7:8], ALU.mult)
                v.drain()
                v.tensor_scalar(st[:, 9:10], st[:, 8:9], -1.0, None, ALU.mult)
                v.drain().then_inc(H[SVV], 1)
                if second:
                    v.wait_ge(H[SSC], bs[SSC] + spt * i + 3)
                    v.wait_ge(H[SDH + str(i % 2)], bs[SDH + str(i % 2)]
                              + (16 * ((i - 2) // 2 + 1) if i >= 2 else 0))
                    v.tensor_tensor(out_[i % 2], out_[i % 2],
                                    cb1rep[seq].ap()[:, :], ALU.mult)
                    v.drain()
                    v.tensor_tensor(ht_[i % 2], out_[i % 2],
                                    sbrep[seq].ap()[:, :], ALU.add)
                    v.drain()
                    v.tensor_reduce(st[:, 10:11], ht_[i % 2], AX,
                                    ALU.max, apply_absolute_value=True)
                    v.drain()
                    v.tensor_tensor(gacc[:, 0:1], gacc[:, 0:1],
                                    st[:, 10:11], ALU.max)
                    v.drain().then_inc(H[SVV], 1)

            def s_i(s, i):
                st = stt_[i % 2]
                xwait(s, i)
                if not second:
                    s.wait_ge(H[SDH + str(i % 2)], bs[SDH + str(i % 2)]
                              + (16 * ((i - 2) // 2 + 1) if i >= 2 else 0))
                if second and i >= 2:
                    s.wait_ge(H[SVV], bs[SVV] + vpt * (i - 2) + 3)
                s.drain()
                s.activation(pk[6][:, :], xt_[i % 2][:, 0:512], ACT.Square,
                             accum_out=st[:, 11:12])
                s.drain()
                s.activation(pk[6][:, :], xt_[i % 2][:, 512:1024], ACT.Square,
                             accum_out=st[:, 12:13]).then_inc(H[SSC], 1)
                s.wait_ge(H[SVV], bs[SVV] + vpt * i + 1)
                s.activation(st[:, 6:7], st[:, 5:6], ACT.Ln,
                             bias=zc[:, 1:2], scale=1.0)
                s.drain()
                s.activation(st[:, 7:8], st[:, 6:7], ACT.Exp,
                             bias=zc[:, 0:1], scale=-0.5).then_inc(H[SSC], 1)
                s.wait_ge(H[SVV], bs[SVV] + vpt * i + 2)
                s.drain()
                s.activation(out_[i % 2], xt_[i % 2], ACT.Identity,
                             bias=st[:, 9:10], scale=st[:, 7:8]).then_inc(H[SSC], 1)

            def sy_i(sy, i):
                if second:
                    sy.wait_ge(H[SVV], bs[SVV] + vpt * i + 3)
                    srcap = ht_[i % 2]
                else:
                    sy.wait_ge(H[SSC], bs[SSC] + spt * i + 3)
                    srcap = out_[i % 2]
                sy.dma_start(out=spill(i), in_=srcap).then_inc(
                    H[SDH + str(i % 2)], 16)

            def done():
                V[SVV] += vpt * NT
                V[SSC] += spt * NT
                if src_x is not None:
                    V[SDX + "0"] += NT * 8
                    V[SDX + "1"] += NT * 8
                V[SDH + "0"] += NT * 8
                V[SDH + "1"] += NT * 8

            def tg(eng):
                eng.wait_ge(H[SSC], bs[SSC] + spt * NT)
                eng.wait_ge(H[SDH + "0"], bs[SDH + "0"] + NT * 8)
                eng.wait_ge(H[SDH + "1"], bs[SDH + "1"] + NT * 8)

            return g_i, v_i, s_i, sy_i, done, tg


        with (
            nc.sbuf_tensor("scrow", [B, D], F32) as scrow,
            nc.sbuf_tensor("scb", [B, D], BF16) as scb,
            nc.sbuf_tensor("scT", [128, 8, B], BF16) as scT,
            nc.sbuf_tensor("saT", [128, 8, B], BF16) as saT,
            nc.sbuf_tensor("modsb", [B, 6 * D], F32) as modsb,
            nc.psum_tensor("pT", [128, 128], BF16) as pT,
        ):
            with nc.Block() as blk:
                bs = snap()

                @blk.gpsimd
                def _(g):
                    for dst, src in [(idbt[:, :], idb), (idft[:, :], idf),
                                     (maskt[:, :], maskc), (scrow[:, :], c_in),
                                     (sigf[0:1, 0:2], clips)]:
                        g.dma_start(out=dst, in_=src).then_inc(H["dg"], 16)
                    V["dg"] += 5 * 16

                @blk.vector
                def _(v):
                    v.memset(zc[:, 0:1], 0.0)
                    v.memset(zc[:, 1:2], 1.0e-5)
                    v.memset(ones32[:, :], 1.0)
                    v.memset(onesb[:, :], 1.0)
                    v.memset(gacc[:, :], 0.0)
                    v.drain().then_inc(H["vv"], 1)
                    v.wait_ge(H["sc"], bs["sc"] + 1)
                    v.tensor_copy(scb[:, :], scrow[:, :])
                    v.wait_ge(H["sc"], bs["sc"] + 2)
                    v.tensor_scalar(sigf[0:1, 2:4], sigf[0:1, 2:4], 1.0, None, ALU.add)
                    v.drain()
                    v.reciprocal(sigf[0:1, 2:4], sigf[0:1, 2:4])
                    v.drain()
                    v.tensor_scalar(sigf[0:1, 2:4], sigf[0:1, 2:4],
                                    1.0 / 127.0, None, ALU.mult)
                    v.drain().then_inc(H["vv"], 1)
                    V["vv"] += 2

                @blk.scalar
                def _(s):
                    load_set(s, "silu_and_others")
                    s.wait_ge(H["dg"], bs["dg"] + 5 * 16)
                    s.activation(scrow[:, :], scrow[:, :], ACT.Silu).then_inc(H["sc"], 1)
                    load_set(s, "natural_log_exp_and_others")
                    s.wait_ge(H["vv"], bs["vv"] + 1)
                    s.activation(sigf[0:1, 2:4], sigf[0:1, 0:2], ACT.Exp,
                                 bias=zc[0:1, 0:1], scale=-1.0).then_inc(H["sc"], 1)
                    V["sc"] += 2

            if STAGE == 101:
                return nc
            ln1_0 = None
            if 2 <= STAGE < 100:
                ln1_0 = make_ln_chain(
                    snap(), 0, False,
                    lambda i: x_in[0, i * 128:(i + 1) * 128, :],
                    lambda i: h1s[i * 128:(i + 1) * 128, :],
                    [xt[0].ap()[:, :], xt[1].ap()[:, :]],
                    [stt[0].ap()[:, :], stt[1].ap()[:, :]],
                    [h1t[0].ap()[:, :], h1t[1].ap()[:, :]], None,
                    "vv2", "sc2", "dlxC", "dh1")
            with nc.sbuf_tensor("wmodsb", [128, 8, 3 * D], BF16) as wmodsb:
                for half in range(1 if STAGE in (1016, 1017) else 2):
                    with nc.Block() as blk:
                        bs = snap()

                        @blk.gpsimd
                        def _(g, half=half):
                            for dc in range(8):
                                g.dma_start(
                                    out=wmodsb[:, dc, :],
                                    in_=wmod[dc * 128:(dc + 1) * 128,
                                             half * 3 * D:(half + 1) * 3 * D],
                                ).then_inc(H["dg"], 16)
                            V["dg"] += 8 * 16

                        @blk.scalar
                        def _(s, half=half):
                            if ln1_0:
                                for j in range(8):
                                    ln1_0[2](s, 8 * half + j)

                        @blk.sync
                        def _(sy, half=half):
                            if ln1_0:
                                if half == 0:
                                    ln1_0[0](sy, 0)
                                    ln1_0[0](sy, 1)
                                for j in range(8 * half, 8 * half + 8):
                                    ln1_0[3](sy, j)
                                    if j + 2 <= 15:
                                        ln1_0[0](sy, j + 2)

                        @blk.tensor
                        def _(t, half=half, bs=bs):
                            if half == 0:
                                for dc in range(8):
                                    if dc > 0:
                                        t.wait_ge(H["vv"], bs["vv"] + dc)
                                    t.transpose(pT[:, 0:B],
                                                scb[0:B, dc * 128:(dc + 1) * 128],
                                                idbt[0:B, 0:B]).then_inc(H["tt"], 1)
                                V["tt"] += 8
                            t.wait_ge(H["dg"], bs["dg"] + 8 * 16)
                            nb = bs["vv"] + (8 if half == 0 else 0)
                            t.wait_ge(H["vv"], nb)
                            for nch in range(0 if STAGE == 1015 else (1 if STAGE == 1017 else 6)):
                                if nch >= 2:
                                    t.wait_ge(H["vv"], nb + nch - 1)
                                for dc in range(8):
                                    t.matmul(pk[nch % 2][0:B, :], scT[:, dc, :],
                                             wmodsb[:, dc, nch * 512:(nch + 1) * 512],
                                             start=(dc == 0), stop=(dc == 7))
                                t.drain().then_inc(H["tt"], 1)
                            V["tt"] += 0 if STAGE == 1015 else (1 if STAGE == 1017 else 6)

                        @blk.vector
                        def _(v, half=half, bs=bs):
                            tb = bs["tt"]
                            if half == 0:
                                for dc in range(8):
                                    v.wait_ge(H["tt"], bs["tt"] + dc + 1)
                                    v.tensor_copy(scT[:, dc, :], pT[:, 0:B])
                                    v.drain().then_inc(H["vv"], 1)
                                V["vv"] += 8
                                tb = bs["tt"] + 8
                            for nch in range(0 if STAGE == 1015 else 6):
                                v.wait_ge(H["tt"], tb + nch + 1)
                                off = half * 3 * D + nch * 512
                                v.tensor_copy(modsb[0:B, off:off + 512],
                                              pk[nch % 2][0:B, :])
                                v.drain().then_inc(H["vv"], 1)
                                if ln1_0:
                                    ln1_0[1](v, 8 * half + nch)
                            if ln1_0:
                                ln1_0[1](v, 8 * half + 6)
                                ln1_0[1](v, 8 * half + 7)
                            V["vv"] += 0 if STAGE == 1015 else (1 if STAGE == 1017 else 6)

            if ln1_0:
                ln1_0[4]()
            if STAGE in (102, 1015, 1016, 1017):
                return nc
            with (
                nc.sbuf_tensor("tmpg", [B, D], F32) as tmpg,
                nc.sbuf_tensor("tmpa", [B, D], F32) as tmpa,
                nc.sbuf_tensor("sab", [B, D], BF16) as sab,
                nc.Block() as blk,
            ):
                bs = snap()

                @blk.scalar
                def _(s):
                    s.activation(tmpa[:, :], modsb[0:B, 2 * D:3 * D],
                                 ACT.Abs).then_inc(H["sc"], 1)
                    s.wait_ge(H["vv"], bs["vv"] + 1)
                    s.activation(tmpa[:, :], modsb[0:B, 5 * D:6 * D],
                                 ACT.Abs).then_inc(H["sc"], 1)
                    V["sc"] += 2

                @blk.vector
                def _(v):
                    v.tensor_scalar(modsb[0:B, D:2 * D], modsb[0:B, D:2 * D],
                                    1.0, None, ALU.add)
                    v.tensor_scalar(modsb[0:B, 4 * D:5 * D], modsb[0:B, 4 * D:5 * D],
                                    1.0, None, ALU.add)
                    for gi, off in enumerate((2 * D, 5 * D)):
                        sl = modsb[0:B, off:off + D]
                        v.tensor_scalar(tmpg[:, :], sl, 0.0, 2.0, ALU.is_ge, ALU.mult)
                        v.drain()
                        v.tensor_scalar(tmpg[:, :], tmpg[:, :], -1.0, None, ALU.add)
                        v.wait_ge(H["sc"], bs["sc"] + gi + 1)
                        v.tensor_scalar(tmpa[:, :], tmpa[:, :], 0.25, None, ALU.max)
                        v.drain()
                        v.tensor_tensor(sl, tmpg[:, :], tmpa[:, :], ALU.mult)
                        v.drain().then_inc(H["vv"], 1)
                    v.tensor_copy(sab[:, :], modsb[0:B, 0:D])
                    v.drain().then_inc(H["vv"], 1)
                    V["vv"] += 3
                    for dc in range(8):
                        v.wait_ge(H["tt"], bs["tt"] + 3 * dc + 1)
                        v.tensor_copy(saT[:, dc, :], pT[:, 0:B])
                        v.drain().then_inc(H["vv"], 1)
                        v.wait_ge(H["tt"], bs["tt"] + 3 * dc + 2)
                        v.tensor_copy(ca1T[:, dc, :], pk[6][0:128, 0:B])
                        v.drain().then_inc(H["vv"], 1)
                        v.wait_ge(H["tt"], bs["tt"] + 3 * dc + 3)
                        v.tensor_copy(gbT[:, dc, :], pk[5][0:128, 0:B])
                        v.drain().then_inc(H["vv"], 1)

                    V["vv"] += 24

                @blk.gpsimd
                def _(g):
                    g.wait_ge(H["vv"], bs["vv"] + 3)
                    for b in range(B):
                        g.dma_start(out=gar[b].ap()[0:1, :],
                                    in_=modsb[b:b + 1, 2 * D:3 * D]
                                    ).then_inc(H["dg"], 16)
                        g.dma_start(out=cbr[b].ap()[0:1, :],
                                    in_=modsb[b:b + 1, 4 * D:5 * D]
                                    ).then_inc(H["dg"], 16)
                        g.dma_start(out=sbr[b].ap()[0:1, :],
                                    in_=modsb[b:b + 1, 3 * D:4 * D]
                                    ).then_inc(H["dg"], 16)
                    V["dg"] += 6 * 16

                @blk.tensor
                def _(t):
                    t.wait_ge(H["vv"], bs["vv"] + 3)
                    for dc in range(8):
                        if dc > 0:
                            t.wait_ge(H["vv"], bs["vv"] + 3 + 3 * dc)
                        t.transpose(pT[:, 0:B], sab[0:B, dc * 128:(dc + 1) * 128],
                                    idbt[0:B, 0:B]).then_inc(H["tt"], 1)
                        t.transpose(pk[6][0:128, 0:B],
                                    modsb[0:B, D + dc * 128:D + (dc + 1) * 128],
                                    idft[0:B, 0:B]).then_inc(H["tt"], 1)
                        t.transpose(pk[5][0:128, 0:B],
                                    modsb[0:B, 5 * D + dc * 128:
                                          5 * D + (dc + 1) * 128],
                                    idft[0:B, 0:B]).then_inc(H["tt"], 1)
                    V["tt"] += 24

            if STAGE == 103:
                return nc
            with (
                nc.sbuf_tensor("wqsb", [128, 8, 512], BF16) as wqsb,
                nc.sbuf_tensor("befr", [B, 3 * D], F32) as befr,
                nc.sbuf_tensor("brow", [B, 3 * D], F32) as brow,
                nc.sbuf_tensor("befb", [B, 3 * D], BF16) as befb,
                nc.Block() as blk,
            ):
                bs = snap()

                @blk.gpsimd
                def _(g):
                    for nch in range(6):
                        if nch >= 1:
                            g.wait_ge(H["tt"], bs["tt"] + nch)
                        for dc in range(8):
                            g.dma_start(
                                out=wqsb[:, dc, :],
                                in_=aw[dc * 128:(dc + 1) * 128,
                                       nch * 512:(nch + 1) * 512],
                            ).then_inc(H["dg"], 16)
                    V["dg"] += 6 * 8 * 16

                @blk.tensor
                def _(t):
                    for nch in range(6):
                        t.wait_ge(H["dg"], bs["dg"] + (nch + 1) * 8 * 16)
                        if nch >= 2:
                            t.wait_ge(H["vv"], bs["vv"] + nch - 1)
                        for dc in range(8):
                            t.matmul(pk[nch % 2][0:B, :], saT[:, dc, :],
                                     wqsb[:, dc, :],
                                     start=(dc == 0), stop=(dc == 7))
                        t.drain().then_inc(H["tt"], 1)
                    V["tt"] += 6

                @blk.vector
                def _(v):
                    for nch in range(6):
                        v.wait_ge(H["tt"], bs["tt"] + nch + 1)
                        v.tensor_copy(befr[0:B, nch * 512:(nch + 1) * 512],
                                      pk[nch % 2][0:B, :])
                        v.drain().then_inc(H["vv"], 1)
                    v.wait_ge(H["dsy"], bs["dsy"] + 32)
                    v.drain()
                    v.tensor_tensor(befr[0:B, :], befr[0:B, :], brow[0:B, :], ALU.add)
                    v.drain()
                    v.tensor_tensor(befr[0:B, 2 * D:3 * D], befr[0:B, 2 * D:3 * D],
                                    modsb[0:B, 2 * D:3 * D], ALU.mult)
                    v.drain()
                    v.tensor_copy(befb[0:B, :], befr[0:B, :])
                    v.drain().then_inc(H["vv"], 1)
                    V["vv"] += 7

                @blk.sync
                def _(sy):
                    sy.dma_start(out=brow[0:1, :], in_=ab).then_inc(H["dsy"], 16)
                    sy.dma_start(out=brow[1:2, :], in_=ab).then_inc(H["dsy"], 16)
                    V["dsy"] += 32
                    sy.wait_ge(H["vv"], bs["vv"] + 7)
                    for b in range(B):
                        sy.dma_start(out=beff[b].ap()[0:1, :],
                                     in_=befb[b:b + 1, :]).then_inc(H["dsy"], 16)
                    V["dsy"] += 32

        if STAGE <= 1:
            return nc

        def ln_block(seq, src_x, spill, second):
            with nc.Block() as blk:
                bs = snap()
                xt_ = [xt[0][:, :], xt[1][:, :]]
                stt_ = [stt[0][:, :], stt[1][:, :]]
                out_ = ([nrm[0][:, :], nrm[1][:, :]] if second
                        else [h1t[0][:, :], h1t[1][:, :]])
                ht2 = [ht[0][:, 0:D], ht[1][:, 0:D]] if second else None
                g_i, v_i, s_i, sy_i, done, _tg = make_ln_chain(
                    bs, seq, second, src_x, spill, xt_, stt_, out_, ht2,
                    "vv", "sc", "dgx", "dh1")

                @blk.gpsimd
                def _(g):
                    for i in range(NT):
                        g_i(g, i)

                @blk.vector
                def _(v):
                    for i in range(NT):
                        v_i(v, i)

                @blk.scalar
                def _(s):
                    for i in range(NT):
                        s_i(s, i)

                @blk.sync
                def _(sy):
                    for i in range(NT):
                        sy_i(sy, i)
                done()

        def make_wprep_chain(bs, tiles):
            def g_k(g, k):
                src_, _dst, j = tiles[k]
                if k >= 2:
                    g.wait_ge(H["scw"], bs["scw"] + (k - 2) + 1)
                g.dma_start(out=wpt[k % 2],
                            in_=src_[j * 128:(j + 1) * 128, :]
                            ).then_inc(H[f"dwtS{k % 2}"], 16)

            def v_k(v, k):
                st = wpst[k % 2]
                v.wait_ge(H[f"dwtS{k % 2}"],
                          bs[f"dwtS{k % 2}"] + 16 * (k // 2 + 1))
                v.tensor_reduce(st[:, 0:1], wpt[k % 2], AX, ALU.max,
                                apply_absolute_value=True)
                v.drain()
                v.tensor_scalar(st[:, 1:2], st[:, 0:1], 1.0 / 127.0, None, ALU.mult)
                v.drain()
                v.reciprocal(st[:, 2:3], st[:, 1:2])
                v.drain().then_inc(H["vvw"], 1)
                v.wait_ge(H["scw"], bs["scw"] + k + 1)
                v.wait_ge(H[f"dwq{k % 2}"], bs[f"dwq{k % 2}"]
                          + (16 * ((k - 2) // 2 + 1) if k >= 2 else 0))
                v.tensor_copy(wpi[:, :], wptm[:, :])
                v.drain()
                v.tensor_scalar(wpq[k % 2], wpi[:, :], st[:, 1:2],
                                None, ALU.mult)
                v.drain().then_inc(H["vvw"], 1)

            def s_k(s, k):
                s.wait_ge(H["vvw"], bs["vvw"] + 2 * k + 1)
                if k >= 1:
                    s.wait_ge(H["vvw"], bs["vvw"] + 2 * (k - 1) + 2)
                s.drain()
                s.activation(wptm[:, :], wpt[k % 2], ACT.Identity,
                             bias=zc[:, 0:1],
                             scale=wpst[k % 2][:, 2:3]).then_inc(H["scw"], 1)

            def sy_k(sy, k):
                _src, dst, j = tiles[k]
                sy.wait_ge(H["vvw"], bs["vvw"] + 2 * k + 2)
                sy.dma_start(out=dst[j * 128:(j + 1) * 128, :],
                             in_=wpq[k % 2]).then_inc(H[f"dwq{k % 2}"], 16)

            def done(n):
                V["vvw"] += 2 * n
                V["scw"] += n
                V["dwtS0"] += ((n + 1) // 2) * 16
                V["dwtS1"] += (n // 2) * 16
                V["dwq0"] += ((n + 1) // 2) * 16
                V["dwq1"] += (n // 2) * 16

            return g_k, v_k, s_k, sy_k, done

        def emit_seq(seq, upto=4, wprep_tiles=None, attn_extras=(),
                     qkv_extras=()):
            with nc.Block() as blk:
                bs = snap()

                @blk.scalar
                def _(s):
                    for nm in ("dh10", "dh11", "dh20", "dh21"):
                        s.wait_ge(H[nm], bs[nm])
                    for gch in range(4):
                        for dc in range(8):
                            s.dma_start(
                                out=h1T[:, dc, gch * 512:(gch + 1) * 512],
                                in_=h1s[gch * 512:(gch + 1) * 512,
                                        dc * 128:(dc + 1) * 128],
                                transpose=True,
                            ).then_inc(H["dsc"], 16)
                    V["dsc"] += 32 * 16

                @blk.vector
                def _(v):
                    v.wait_ge(H["dsc"], bs["dsc"] + 32 * 16)
                    for dc in range(8):
                        v.tensor_scalar(h1T[:, dc, :], h1T[:, dc, :],
                                        ca1T[:, dc, seq:seq + 1], None, ALU.mult)
                    v.drain().then_inc(H["vv"], 1)
                    V["vv"] += 1
            if upto <= 2:
                return

            with (
                nc.sbuf_tensor(f"woc{seq}", [128, 2, 8, 128], BF16) as woc,
                nc.sbuf_tensor(f"wvv{seq}", [128, 1, D], BF16) as wvv,
                nc.sbuf_tensor(f"garep{seq}", [128, D], BF16) as garep,
                nc.Block() as blk,
            ):
                bs = snap()
                nwp = len(wprep_tiles) if wprep_tiles else 0
                if nwp:
                    wg_k, wv_k, ws_k, wsy_k, wdone = make_wprep_chain(bs, wprep_tiles)
                qchains = [f(bs) for f in qkv_extras]
                alloc = [[] for _ in range(16)]
                for k in range(nwp):
                    alloc[k % 16].append(k) if False else None
                kk = 0
                for oc in range(16):
                    take = (nwp + 15 - oc) // 16
                    alloc[oc] = list(range(kk, kk + take))
                    kk += take

                @blk.gpsimd
                def _(g):
                    g.wait_ge(H["vv2"], bs["vv2"])
                    g.wait_ge(H["sc2"], bs["sc2"])
                    for dc in range(8):
                        if dc >= 1:
                            g.wait_ge(H["vv"], bs["vv"] + dc + 2)
                        g.dma_start(
                            out=wvv[:, 0, :],
                            in_=aw[dc * 128:(dc + 1) * 128, 2 * D:3 * D],
                        ).then_inc(H[f"dwv{dc % 2}"], 16)
                    for oc in range(16):
                        if oc >= 2:
                            g.wait_ge(H["tt"], bs["tt"] + oc + 1)
                        for dc in range(8):
                            g.dma_start(
                                out=woc[:, oc % 2, dc, :],
                                in_=aw[dc * 128:(dc + 1) * 128,
                                       oc * 128:(oc + 1) * 128],
                            ).then_inc(H[f"dwo{oc % 2}"], 16)
                    V["dwv0"] += 4 * 16
                    V["dwv1"] += 4 * 16
                    V["dwo0"] += 8 * 128
                    V["dwo1"] += 8 * 128

                @blk.tensor
                def _(t):
                    t.wait_ge(H["dsy"], bs["dsy"])
                    for hch in range(2):
                        t.matmul(pk[hch][:, :], onesb[0:1, 0:128],
                                 gar[seq].ap()[0:1, hch * 512:(hch + 1) * 512],
                                 start=True, stop=True)
                        t.drain().then_inc(H["tt"], 1)
                    for oc in range(16):
                        t.wait_ge(H[f"dwo{oc % 2}"],
                                  bs[f"dwo{oc % 2}"] + 128 * (oc // 2 + 1))
                        if oc >= 1:
                            t.wait_ge(H["vv"], bs["vv"] + 11 + 4 * oc)
                        for dc in range(8):
                            for gch in range(4):
                                t.matmul(pk[gch][:, :], woc[:, oc % 2, dc, :],
                                         h1T[:, dc, gch * 512:(gch + 1) * 512],
                                         start=(dc == 0), stop=False)
                        for gch in range(4):
                            t.matmul(pk[gch][:, :],
                                     beff[seq].ap()[0:1, oc * 128:(oc + 1) * 128],
                                     onesb[0:1, :], start=False, stop=True)
                        t.drain().then_inc(H["tt"], 1)
                    for ti in range(NT):
                        t.wait_ge(H["vv"], bs["vv"] + 11 + 64)
                        if ti >= 1:
                            t.wait_ge(H["vv"], bs["vv"] + 11 + 64 + 2 * ti)
                        for dc in range(8):
                            t.matmul(pk[4][:, :],
                                     h1T[:, dc, ti * 128:(ti + 1) * 128],
                                     wvs[:, dc, 0:512], start=(dc == 0), stop=False)
                            t.matmul(pk[5][:, :],
                                     h1T[:, dc, ti * 128:(ti + 1) * 128],
                                     wvs[:, dc, 512:1024], start=(dc == 0), stop=False)
                        t.matmul(pk[4][:, :], onesb[0:1, 0:128],
                                 beff[seq].ap()[0:1, 2 * D:2 * D + 512],
                                 start=False, stop=True)
                        t.matmul(pk[5][:, :], onesb[0:1, 0:128],
                                 beff[seq].ap()[0:1, 2 * D + 512:3 * D],
                                 start=False, stop=True)
                        t.drain().then_inc(H["tt"], 1)
                    V["tt"] += 2 + 16 + NT

                @blk.scalar
                def _(s):
                    s.wait_ge(H["dou0"], bs["dou0"])
                    s.wait_ge(H["dou1"], bs["dou1"])
                    s.wait_ge(H["sc2"], bs["sc2"])
                    s.wait_ge(H["dh20"], bs["dh20"])
                    s.wait_ge(H["dh21"], bs["dh21"])
                    for oc in range(16):
                        for k in alloc[oc]:
                            wg_k(s, k)
                        for ch in qchains:
                            ch[0](s, oc)
                        for k in alloc[oc]:
                            ws_k(s, k)
                        if oc >= 1:
                            for ch in qchains:
                                ch[2](s, oc - 1)
                    for ch in qchains:
                        ch[2](s, 15)

                @blk.sync
                def _(sy):
                    for nm in ("dsc", "dh10", "dh11", "dh20", "dh21"):
                        sy.wait_ge(H[nm], bs[nm])
                    for oc in range(16):
                        for k in alloc[oc]:
                            wsy_k(sy, k)
                        if oc >= 1:
                            for ch in qchains:
                                ch[3](sy, oc - 1)
                    for ch in qchains:
                        ch[3](sy, 15)
                if nwp:
                    wdone(nwp)
                for ch in qchains:
                    ch[4]()

                @blk.vector
                def _(v):
                    v.wait_ge(H["vv2"], bs["vv2"])
                    v.wait_ge(H["sc2"], bs["sc2"])
                    v.wait_ge(H["dh20"], bs["dh20"])
                    v.wait_ge(H["dh21"], bs["dh21"])
                    for hch in range(2):
                        v.wait_ge(H["tt"], bs["tt"] + hch + 1)
                        v.tensor_copy(garep[:, hch * 512:(hch + 1) * 512],
                                      pk[hch][:, :])
                        v.drain().then_inc(H["vv"], 1)
                    v.drain()
                    for dc in range(8):
                        v.wait_ge(H[f"dwv{dc % 2}"],
                                  bs[f"dwv{dc % 2}"] + 16 * (dc // 2 + 1))
                        v.tensor_tensor(wvs[:, dc, :], wvv[:, 0, :],
                                        garep[:, :], ALU.mult)
                        v.drain().then_inc(H["vv"], 1)
                    v.drain().then_inc(H["vv"], 1)
                    for oc in range(16):
                        v.wait_ge(H["tt"], bs["tt"] + 2 + oc + 1)
                        for gch in range(4):
                            v.tensor_copy(qkT[:, oc, gch * 512:(gch + 1) * 512],
                                          pk[gch][:, :])
                            v.drain().then_inc(H["vv"], 1)
                        for k in alloc[oc]:
                            wv_k(v, k)
                        if oc >= 1:
                            for ch in qchains:
                                ch[1](v, oc - 1)
                    for ch in qchains:
                        ch[1](v, 15)
                    for ti in range(NT):
                        if ti == 6:
                            for ch in qchains:
                                ch[5](v)
                        v.wait_ge(H["tt"], bs["tt"] + 2 + 16 + ti + 1)
                        v.tensor_copy(vsb[:, ti, 0:512], pk[4][:, :])
                        v.drain().then_inc(H["vv"], 1)
                        v.tensor_copy(vsb[:, ti, 512:1024], pk[5][:, :])
                        v.drain().then_inc(H["vv"], 1)
                    V["vv"] += 11 + 64 + 2 * NT
            if upto <= 3:
                return

            with (
                nc.sbuf_tensor(f"Pt{seq}", [128, 2, T], BF16) as Pt,
                nc.Block() as blk,
            ):
                bs = snap()
                x1t = nrm
                rst = idft.ap()[:, 0:32].rearrange("p (a b) -> p a b", b=16)
                h1b_ = h1T.ap().rearrange("p a b -> p (a b)")
                if seq == 0:
                    PTs = [h1b_[:, 6400:8448], h1b_[:, 8448:10496]]
                else:
                    PTs = [h1b_[:, 10240:12288], h1b_[:, 14336:16384]]
                PTt = [p.rearrange("p (a b) -> p a b", b=128) for p in PTs]

                tts = {0: 1}
                ttpv = {}
                pos = 1
                for qt in range(1, NT):
                    pos += 1
                    tts[qt] = pos
                    pos += 1
                    ttpv[qt - 1] = pos
                pos += 1
                ttpv[NT - 1] = pos
                ne = [qt // 2 + 1 for qt in range(NT)]
                no = [(qt + 1) // 2 for qt in range(NT)]
                cum_e = {}
                cum_o = {}
                tot = {0: [0, 0], 1: [0, 0]}
                for qt in range(NT):
                    p = qt % 2
                    tot[p][0] += ne[qt]
                    tot[p][1] += no[qt]
                    cum_e[qt] = tot[p][0]
                    cum_o[qt] = tot[p][1]

                chains = [f(bs) for f in attn_extras]

                @blk.gpsimd
                def _(g):
                    for qt in range(NT):
                        if qt >= 2:
                            g.wait_ge(H["vv"], bs["vv"] + 3 * (qt - 2) + 3)
                        g.dma_start(out=xt[qt % 2][:, :],
                                    in_=x_in[seq, qt * 128:(qt + 1) * 128, :]
                                    ).then_inc(H[f"dgx{qt % 2}"], 16)
                        for ch in chains:
                            ch[0](g, qt)
                    V["dgx0"] += NT * 8
                    V["dgx1"] += NT * 8

                @blk.tensor
                def _(t):
                    def emit_S(qt):
                        nk = NKC[qt]
                        if qt >= 1:
                            t.wait_ge(H["sc"], bs["sc"] + qt)
                        for dc in range(8):
                            for kc in range(nk):
                                w = KCW[qt][kc]
                                t.matmul(pk[kc][:, 0:w],
                                         qkT[:, dc, qt * 128:(qt + 1) * 128],
                                         qkT[:, 8 + dc, kc * 512:kc * 512 + w],
                                         start=(dc == 0), stop=(dc == 7))
                        t.drain().then_inc(H["tt"], 1)

                    def emit_PV(qt):
                        t.wait_ge(H[f"dpt{qt % 2}"],
                                  bs[f"dpt{qt % 2}"] + 16 * cum_e[qt])
                        t.wait_ge(H[f"dpts{qt % 2}"],
                                  bs[f"dpts{qt % 2}"] + 16 * cum_o[qt])
                        if qt >= 1:
                            t.wait_ge(H["vv"], bs["vv"] + 3 * (qt - 1) + 3)
                        for kc in range(qt + 1):
                            t.matmul(pk[4][:, :], PTt[qt % 2][:, kc, :],
                                     vsb[:, kc, 0:512],
                                     start=(kc == 0), stop=(kc == qt))
                            t.matmul(pk[5][:, :], PTt[qt % 2][:, kc, :],
                                     vsb[:, kc, 512:1024],
                                     start=(kc == 0), stop=(kc == qt))
                        t.drain().then_inc(H["tt"], 1)

                    emit_S(0)
                    for qt in range(1, NT):
                        emit_S(qt)
                        emit_PV(qt - 1)
                    emit_PV(NT - 1)
                    V["tt"] += 2 * NT

                @blk.vector
                def _(v):
                    v.wait_ge(H["vv2"], bs["vv2"])
                    v.wait_ge(H["sc2"], bs["sc2"])
                    for qt in range(NT):
                        nk = NKC[qt]
                        st = rst[:, qt % 2, :]
                        v.wait_ge(H["tt"], bs["tt"] + tts[qt])
                        if qt >= 2:
                            v.wait_ge(H["sc"], bs["sc"] + qt - 1)
                        dcol = qt * 128 - (nk - 1) * 512
                        v.tensor_tensor(pk[nk - 1][:, dcol:dcol + 128],
                                        pk[nk - 1][:, dcol:dcol + 128],
                                        maskt[:, :], ALU.add)
                        v.drain()
                        for kc in range(nk):
                            v.reduce_max(st[:, kc:kc + 1],
                                         pk[kc][:, 0:KCW[qt][kc]], AX)
                        v.drain()
                        v.reduce_max(st[:, 4:5], st[:, 0:nk], AX)
                        v.drain()
                        v.tensor_scalar(st[:, 5:6], st[:, 4:5], -1.0 / 32.0,
                                        None, ALU.mult)
                        v.drain().then_inc(H["vv"], 1)
                        v.wait_ge(H["sc"], bs["sc"] + qt + 1)
                        v.reduce_sum(st[:, 13:14], st[:, 8:8 + nk], AX)
                        v.drain()
                        v.reciprocal(st[:, 14:15], st[:, 13:14])
                        v.drain().then_inc(H["vv"], 1)
                        v.wait_ge(H["tt"], bs["tt"] + ttpv[qt])
                        if seq == 1 and qt >= 2 and chains:
                            v.wait_ge(H["sc3"], bs["sc3"] + 3 * (qt - 2) + 3)
                        v.wait_ge(H[f"dgx{qt % 2}"],
                                  bs[f"dgx{qt % 2}"] + 16 * (qt // 2 + 1))
                        if qt >= 2:
                            v.wait_ge(H[f"dou{qt % 2}"],
                                      bs[f"dou{qt % 2}"] + 16 * ((qt - 2) // 2 + 1))
                        else:
                            v.wait_ge(H[f"dou{qt % 2}"], bs[f"dou{qt % 2}"])
                        v.scalar_tensor_tensor(x1t[qt % 2][:, 0:512], pk[4][:, :],
                                               st[:, 14:15], xt[qt % 2][:, 0:512],
                                               ALU.mult, ALU.add)
                        v.scalar_tensor_tensor(x1t[qt % 2][:, 512:1024], pk[5][:, :],
                                               st[:, 14:15], xt[qt % 2][:, 512:1024],
                                               ALU.mult, ALU.add)
                        v.drain().then_inc(H["vv"], 1)
                        for ch in chains:
                            ch[1](v, qt)
                    V["vv"] += 3 * NT

                @blk.scalar
                def _(s):
                    s.wait_ge(H["vv2"], bs["vv2"])
                    s.wait_ge(H["sc2"], bs["sc2"])
                    for qt in range(NT):
                        nk = NKC[qt]
                        st = rst[:, qt % 2, :]
                        s.wait_ge(H["vv"], bs["vv"] + 3 * qt + 1)
                        if qt >= 2:
                            s.wait_ge(H[f"dps{qt % 2}"],
                                      bs[f"dps{qt % 2}"] + 16 * ((qt - 2) // 2 + 1))
                        for kc in range(nk):
                            w = KCW[qt][kc]
                            s.activation(Pt[:, qt % 2, kc * 512:kc * 512 + w],
                                         pk[kc][:, 0:w], ACT.Exp,
                                         bias=st[:, 5:6], scale=1.0 / 32.0,
                                         accum_out=st[:, 8 + kc:9 + kc])
                        s.drain().then_inc(H["sc"], 1)
                        s.wait_ge(H[f"dps{qt % 2}"],
                                  bs[f"dps{qt % 2}"] + 16 * (qt // 2 + 1))
                        if qt >= 2:
                            s.wait_ge(H["tt"], bs["tt"] + ttpv[qt - 2])
                        for kc in range(1, qt + 1, 2):
                            s.dma_start(out=PTt[qt % 2][:, kc, :],
                                        in_=pss[qt % 2, :, kc * 128:(kc + 1) * 128],
                                        transpose=True).then_inc(H[f"dpts{qt % 2}"], 16)
                        V[f"dpts{qt % 2}"] += 16 * no[qt]
                        for ch in chains:
                            ch[2](s, qt)
                    V["sc"] += NT

                @blk.sync
                def _(sy):
                    for qt in range(NT):
                        sy.wait_ge(H["sc"], bs["sc"] + qt + 1)
                        if qt >= 2:
                            sy.wait_ge(H["tt"], bs["tt"] + ttpv[qt - 2])
                            sy.wait_ge(H[f"dpts{qt % 2}"],
                                       bs[f"dpts{qt % 2}"] + 16 * cum_o.get(qt - 2, 0))
                            sy.wait_ge(H[f"dpt{qt % 2}"],
                                       bs[f"dpt{qt % 2}"] + 16 * cum_e.get(qt - 2, 0))
                        sy.dma_start(out=pss[qt % 2, :, 0:KEYS[qt]],
                                     in_=Pt[:, qt % 2, 0:KEYS[qt]]
                                     ).then_inc(H[f"dps{qt % 2}"], 16)
                        V[f"dps{qt % 2}"] += 16
                        sy.wait_ge(H[f"dps{qt % 2}"], V[f"dps{qt % 2}"])
                        for kc in range(0, qt + 1, 2):
                            sy.dma_start(out=PTt[qt % 2][:, kc, :],
                                         in_=pss[qt % 2, :, kc * 128:(kc + 1) * 128],
                                         transpose=True).then_inc(H[f"dpt{qt % 2}"], 16)
                        V[f"dpt{qt % 2}"] += 16 * ne[qt]
                        sy.wait_ge(H["vv"], bs["vv"] + 3 * qt + 3)
                        sy.dma_start(out=out[seq, qt * 128:(qt + 1) * 128, :],
                                     in_=x1t[qt % 2][:, :]).then_inc(H[f"dou{qt % 2}"], 16)
                        V[f"dou{qt % 2}"] += 16
                        for ch in chains:
                            ch[3](sy, qt)
                for ch in chains:
                    ch[4]()

        with (
            nc.sbuf_tensor("qkT", [128, 16, T], BF16) as qkT,
            nc.sbuf_tensor("vsb", [128, 16, D], BF16) as vsb,
            nc.sbuf_tensor("wvs", [128, 8, D], BF16) as wvs,
            nc.sbuf_tensor("h1T", [128, 8, T], BF16) as h1T,
            nc.sbuf_tensor("nrm0", [128, D], F32) as _n0,
            nc.sbuf_tensor("nrm1", [128, D], F32) as _n1,
            nc.sbuf_tensor("cb1repA", [128, D], BF16) as _cr0,
            nc.sbuf_tensor("cb1repB", [128, D], BF16) as _cr1,
            nc.sbuf_tensor("sbrepA", [128, D], BF16) as _sr0,
            nc.sbuf_tensor("sbrepB", [128, D], BF16) as _sr1,
        ):
            nrm = [_n0, _n1]
            cb1rep = [_cr0, _cr1]
            sbrep = [_sr0, _sr1]

            with nc.Block() as blk:
                bs = snap()

                @blk.tensor
                def _(t):
                    j = 0
                    for b in range(B):
                        for src_row, _d in ((cbr[b], cb1rep[b]),
                                            (sbr[b], sbrep[b])):
                            for hch in range(2):
                                if j >= 2:
                                    t.wait_ge(H["vv"], bs["vv"] + j - 1)
                                t.matmul(pk[j % 2][:, :], onesb[0:1, 0:128],
                                         src_row.ap()[0:1, hch * 512:
                                                      (hch + 1) * 512],
                                         start=True, stop=True)
                                t.drain().then_inc(H["tt"], 1)
                                j += 1
                    V["tt"] += 8

                @blk.vector
                def _(v):
                    j = 0
                    for b in range(B):
                        for _s, dst in ((cbr[b], cb1rep[b]),
                                        (sbr[b], sbrep[b])):
                            for hch in range(2):
                                v.wait_ge(H["tt"], bs["tt"] + j + 1)
                                v.tensor_copy(
                                    dst.ap()[:, hch * 512:(hch + 1) * 512],
                                    pk[j % 2][:, :])
                                v.drain().then_inc(H["vv"], 1)
                                j += 1
                    V["vv"] += 8

            wpt = [xt[0].ap()[:, :], xt[1].ap()[:, :]]
            wpst = [stt[0].ap()[:, 0:4], stt[1].ap()[:, 0:4]]
            wptm = nrm[0].ap()[:, :]
            wpi = nrm[1].ap().bitcast(I32)
            wpq = [h1t[0].ap()[:, :], h1t[1].ap()[:, :]]

            h1f = h1T.ap().rearrange("p a b -> p (a b)").bitcast(F32)
            h1h = h1T.ap().rearrange("p a b -> p (a b)").bitcast(F16)
            h1b = h1T.ap().rearrange("p a b -> p (a b)")
            wvf = wvs.ap().rearrange("p a b -> p (a b)").bitcast(F32)

            wp_tiles_g = [(wg_in, wgq, j) for j in range(22)]
            wp_tiles_u = [(wu_in, wuq, j) for j in range(22)]

            vsbb = vsb.ap().rearrange("p a b -> p (a b)")
            vsbf = vsb.ap().rearrange("p a b -> p (a b)").bitcast(F32)
            vsbh = vsb.ap().rearrange("p a b -> p (a b)").bitcast(F16)

            lnx_ = [vsbf[:, 3072:4096], vsbf[:, 4096:5120]]

            def ln1_s1_factory(bs):
                xt_ = lnx_
                out_ = [vsbb[:, 10240:11264], vsbb[:, 11264:12288]]
                stt_ = [stt_b[0].ap()[:, :], stt_b[1].ap()[:, :]]
                return make_ln_chain(
                    bs, 1, False,
                    lambda i: x_in[1, i * 128:(i + 1) * 128, :],
                    lambda i: h1s[i * 128:(i + 1) * 128, :],
                    xt_, stt_, out_, None, "vv2", "sc2", "dlxA", "dh2")

            def ln2_s0_factory(bs):
                xt_ = lnx_
                out_ = [vsbf[:, 5120:6144], vsbf[:, 6144:7168]]
                ht_ = [vsbh[:, 14336:15360], vsbh[:, 15360:16384]]
                stt_ = [stt_b[0].ap()[:, :], stt_b[1].ap()[:, :]]

                def xw(g, i):
                    g.wait_ge(H[f"dou{i % 2}"], bs[f"dou{i % 2}"])
                return make_ln_chain(
                    bs, 0, True,
                    lambda i: out[0, i * 128:(i + 1) * 128, :],
                    lambda i: hs[i * 128:(i + 1) * 128, :],
                    xt_, stt_, out_, ht_, "vv2", "sc2", "dlxA", "dh1",
                    extra_wait=xw)

            def ln2_s1_factory(bs):
                xt_ = [nrm[0].ap()[:, :], nrm[1].ap()[:, :]]
                out_ = [wvf[:, 2048:3072], wvf[:, 3072:4096]]
                ht_ = [h1h[:, 12288:13312], h1h[:, 13312:14336]]
                stt_ = [stt_b[0].ap()[:, :], stt_b[1].ap()[:, :]]

                def xr_(eng, i):
                    eng.wait_ge(H["vv"], bs["vv"] + 3 * i + 3)
                return make_ln_chain(
                    bs, 1, True, None,
                    lambda i: hs[T + i * 128:T + (i + 1) * 128, :],
                    xt_, stt_, out_, ht_, "vv3", "sc3", "dlxB", "dh2",
                    x_ready=xr_)

            if STAGE >= 3:
                emit_seq(0, upto=min(STAGE - 1, 4), wprep_tiles=wp_tiles_g,
                         qkv_extras=([ln1_s1_factory] if STAGE >= 6 else ()))
            if STAGE >= 6:
                emit_seq(1, wprep_tiles=wp_tiles_u,
                         qkv_extras=([ln2_s0_factory] if STAGE >= 7 else ()),
                         attn_extras=([ln2_s1_factory] if STAGE >= 7 else ()))
            if STAGE <= 7:
                return nc

        ring.close()
        mid.close()
        def global_scale(col, ccio, sig_idx, scl_lo, pre_s=None):
            with nc.Block() as blk:
                bs = snap()

                if pre_s is not None:
                    @blk.scalar
                    def _(s):
                        pre_s(s, bs)

                @blk.sync
                def _(sy):
                    sy.dma_start(out=pmx, in_=gacc[:, col:col + 1]
                                 ).then_inc(H["dsy"], 16)
                    sy.wait_ge(H["dsy"], bs["dsy"] + 16)
                    sy.dma_start(out=gmr[0:1, :], in_=pmx).then_inc(H["dsy"], 16)
                    V["dsy"] += 32

                @blk.vector
                def _(v):
                    v.memset(r16[:, :], 0.0)
                    v.wait_ge(H["dsy"], bs["dsy"] + 32)
                    v.reduce_max(sigf[0:1, 4:5], gmr[0:1, :], AX)
                    v.drain()
                    v.tensor_single_scalar(r16[0:1, :], r16[0:1, :],
                                           sigf[0:1, 4:5], ALU.max)
                    v.drain().then_inc(H["vv"], 1)
                    v.wait_ge(H["dg"], bs["dg"] + 32)
                    v.tensor_tensor(sigf[0:1, 5:6], gmr[0:1, 0:1],
                                    sigf[0:1, sig_idx:sig_idx + 1], ALU.mult)
                    v.drain()
                    v.reciprocal(sigf[0:1, 6:7], sigf[0:1, 5:6])
                    v.drain().then_inc(H["vv"], 1)
                    v.wait_ge(H["tt"], bs["tt"] + 2)
                    v.tensor_copy(scl[:, scl_lo:scl_lo + 1], pk[0][:, 0:1])
                    v.tensor_copy(scl[:, scl_lo + 1:scl_lo + 2], pk[1][:, 0:1])
                    v.drain().then_inc(H["vv"], 1)
                    V["vv"] += 3

                @blk.gpsimd
                def _(g):
                    g.wait_ge(H["vv"], bs["vv"] + 1)
                    g.dma_start(out=ccio[0].ap(), in_=r16[0:1, :]).then_inc(H["dg"], 16)
                    g.wait_ge(H["dg"], bs["dg"] + 16)
                    g.collective_compute(
                        "AllReduce", ALU.max,
                        replica_groups=[list(range(8))],
                        ins=[ccio[0].ap().opt()],
                        outs=[ccio[1].ap().opt()],
                    ).then_inc(H["cc"], 1)
                    V["cc"] += 1
                    g.wait_ge(H["cc"], V["cc"])
                    g.dma_start(out=gmr[0:1, 0:16], in_=ccio[1].ap()
                                ).then_inc(H["dg"], 16)
                    V["dg"] += 32

                @blk.tensor
                def _(t):
                    t.wait_ge(H["vv"], bs["vv"] + 2)
                    t.matmul(pk[0][:, 0:1], ones32[0:1, :], sigf[0:1, 5:6],
                             start=True, stop=True)
                    t.matmul(pk[1][:, 0:1], ones32[0:1, :], sigf[0:1, 6:7],
                             start=True, stop=True)
                    t.drain().then_inc(H["tt"], 1)
                    t.drain().then_inc(H["tt"], 1)
                    V["tt"] += 2

        if STAGE <= 8:
            return nc
        global_scale(0, ccb[0], 2, 0)
        if STAGE <= 9:
            return nc

        with (
            nc.sbuf_tensor("xq", [128, 8, B * T], BF16) as xq,
            nc.sbuf_tensor("wgT", [128, 8, DI], BF16) as wgT,
            nc.sbuf_tensor("wuT", [128, 8, DI], BF16) as wuT,
        ):
            with (
                nc.sbuf_tensor("hT", [128, 2, B * T], F16) as hT,
                nc.sbuf_tensor("atm", [128, 2, 2048], F32) as atm,
                nc.sbuf_tensor("qi", [128, 2, 2048], I32) as qi,
                nc.Block() as blk,
            ):
                bs = snap()

                @blk.scalar
                def _(s):
                    for nm in ("dh10", "dh11", "dh20", "dh21",
                               "dwq0", "dwq1", "dou0", "dou1"):
                        s.wait_ge(H[nm], bs[nm])
                    s.wait_ge(H["vv"], bs["vv"])
                    for dc in range(8):
                        s.dma_start(out=hT[:, dc % 2, :],
                                    in_=hs[0:B * T, dc * 128:(dc + 1) * 128],
                                    transpose=True).then_inc(H["dsc"], 16)
                        V["dsc"] += 16
                        for hf in range(2):
                            j = 2 * dc + hf
                            s.wait_ge(H["dsc"], V["dsc"])
                            if j >= 2:
                                s.wait_ge(H["vv"], bs["vv"] + j - 1)
                            s.activation(atm[:, hf, :],
                                         hT[:, dc % 2, hf * 2048:(hf + 1) * 2048],
                                         ACT.Identity, bias=zc[:, 0:1],
                                         scale=scl[:, 1:2]).then_inc(H["sc"], 1)
                        s.dma_start(out=wgT[:, dc, :],
                                    in_=wgq[0:DI, dc * 128:(dc + 1) * 128],
                                    transpose=True).then_inc(H["dsc"], 16)
                        s.dma_start(out=wuT[:, dc, :],
                                    in_=wuq[0:DI, dc * 128:(dc + 1) * 128],
                                    transpose=True).then_inc(H["dsc"], 16)
                        V["dsc"] += 32
                    V["sc"] += 16

                @blk.vector
                def _(v):
                    for nm in ("dwq0", "dwq1", "dou0", "dou1"):
                        v.wait_ge(H[nm], bs[nm])
                    for j in range(16):
                        dc, hf = j // 2, j % 2
                        v.wait_ge(H["sc"], bs["sc"] + j + 1)
                        v.tensor_scalar(qi[:, hf, :], atm[:, hf, :],
                                        -128.0, 127.0, ALU.max, ALU.min)
                        v.drain()
                        v.tensor_scalar(xq[:, dc, hf * 2048:(hf + 1) * 2048],
                                        qi[:, hf, :], scl[:, 0:1], None,
                                        ALU.mult).then_inc(H["vv"], 1)
                    V["vv"] += 16

            if STAGE <= 10:
                return nc
            IBW = [1024, 1024, 768]
            IBO = [0, 1024, 2048]
            with (
                nc.sbuf_tensor("sil", [128, 2, 1024], F32) as sil,
                nc.sbuf_tensor("spb", [128, 2, 1024], F16) as spb,
                nc.sbuf_tensor("fst", [128, 2, 2], F32) as fst,
                nc.sbuf_tensor("fwt", [128, 4, 704], F32) as fwt,
                nc.sbuf_tensor("fss", [128, 8], F32) as fss,
                nc.sbuf_tensor("ftmp", [128, 704], F32) as ftmp,
                nc.sbuf_tensor("fi32", [128, 704], I32) as fi32,
                nc.sbuf_tensor("fqb", [128, 2, B, 704], BF16) as fqb,
                nc.psum_tensor("pk7", [128, 512], F32) as pk7,
                nc.Block() as blk,
            ):
                bs = snap()
                pkf = pk + [pk7]
                NIT = 32 * 3
                QW = 704

                def wdp_load(g, dc, j):
                    gidx = 8 * dc + j
                    q = j % 4
                    if j <= 3:
                        if dc >= 1:
                            g.wait_ge(H["scw"],
                                      bs["scw"] + 8 * (dc - 1) + 2 * q + 1)
                    else:
                        g.wait_ge(H["vvw"], bs["vvw"] + 9 * dc + q + 1)
                    if gidx >= 2:
                        g.wait_ge(H[f"dwt{gidx % 2}"],
                                  bs[f"dwt{gidx % 2}"] + 16 * (gidx // 2))
                    g.dma_start(out=fwt[:, j % 4, :],
                                in_=wd_in[dc * 128:(dc + 1) * 128,
                                          q * QW:(q + 1) * QW]
                                ).then_inc(H[f"dwt{gidx % 2}"], 16)

                def wdp_vec(v, dc, st):
                    if st <= 3:
                        g = 8 * dc + st
                        v.wait_ge(H[f"dwt{g % 2}"],
                                  bs[f"dwt{g % 2}"] + 16 * (g // 2 + 1))
                        v.tensor_reduce(fss[:, st:st + 1], fwt[:, st, :], AX,
                                        ALU.max, apply_absolute_value=True)
                        v.drain().then_inc(H["vvw"], 1)
                    elif st == 4:
                        if dc >= 1:
                            v.wait_ge(H["scw"], bs["scw"] + 8 * dc)
                        v.reduce_max(fss[:, 4:5], fss[:, 0:4], AX)
                        v.drain()
                        v.tensor_scalar(fss[:, 4:5], fss[:, 4:5], 1.0 / 127.0,
                                        None, ALU.mult)
                        v.drain()
                        v.reciprocal(fss[:, 5:6], fss[:, 4:5])
                        v.tensor_tensor(fss[:, 6:7], fss[:, 4:5],
                                        gbT[:, dc, 0:1], ALU.mult)
                        v.tensor_tensor(fss[:, 7:8], fss[:, 4:5],
                                        gbT[:, dc, 1:2], ALU.mult)
                        v.drain().then_inc(H["vvw"], 1)
                    else:
                        q = st - 5
                        v.wait_ge(H["scw"], bs["scw"] + 8 * dc + 2 * q + 1)
                        v.tensor_copy(fi32[:, :], ftmp[:, :])
                        v.drain().then_inc(H["vvw"], 1)

                def wdp_sca(s, dc, st):
                    q = st // 2
                    if st % 2 == 0:
                        g = 8 * dc + 4 + q
                        s.wait_ge(H["vvw"], bs["vvw"] + 9 * dc + 5)
                        s.wait_ge(H["vvw"], bs["vvw"] + 9 * dc + 5 + q)
                        s.wait_ge(H[f"dwt{g % 2}"],
                                  bs[f"dwt{g % 2}"] + 16 * (g // 2 + 1))
                        s.drain()
                        s.activation(ftmp[:, :], fwt[:, q, :], ACT.Identity,
                                     bias=zc[:, 0:1],
                                     scale=fss[:, 5:6]).then_inc(H["scw"], 1)
                    else:
                        SP = 4 * dc + q
                        s.wait_ge(H["vvw"], bs["vvw"] + 9 * dc + 6 + q)
                        if SP >= 2:
                            s.wait_ge(H[f"dwq{SP % 2}"], bs[f"dwq{SP % 2}"]
                                      + 32 * ((SP - 2) // 2 + 1))
                        s.drain()
                        s.activation(fqb[:, q % 2, 0, :], fi32[:, :],
                                     ACT.Identity, bias=zc[:, 0:1],
                                     scale=fss[:, 6:7])
                        s.drain()
                        s.activation(fqb[:, q % 2, 1, :], fi32[:, :],
                                     ACT.Identity, bias=zc[:, 0:1],
                                     scale=fss[:, 7:8]).then_inc(H["scw"], 1)

                def wdp_spl(sy, dc, q):
                    SP = 4 * dc + q
                    sy.wait_ge(H["scw"], bs["scw"] + 8 * dc + 2 * q + 2)
                    for b in range(B):
                        sy.dma_start(out=wdq[b, dc * 128:(dc + 1) * 128,
                                             q * QW:(q + 1) * QW],
                                     in_=fqb[:, q % 2, b, :]
                                     ).then_inc(H[f"dwq{SP % 2}"], 16)

                VPOS = {1: 0, 2: 1, 3: 2, 4: 3, 5: 4, 7: 5, 8: 6, 9: 7, 10: 8}
                SPOS = {6: [0], 7: [1, 2], 8: [3, 4], 9: [5, 6], 10: [7]}

                @blk.gpsimd
                def _(g):
                    for it in range(NIT):
                        dc, k = it // 12, it % 12
                        if k < 8:
                            wdp_load(g, dc, k)

                @blk.scalar
                def _(s):
                    load_set(s, "silu_and_others")
                    for it in range(NIT):
                        ib = it % 3
                        w = IBW[ib]
                        pg = (it % 2) * 4
                        s.wait_ge(H["tt"], bs["tt"] + it + 1)
                        if it >= 2:
                            s.wait_ge(H["vv"], bs["vv"] + it - 1)
                        s.activation(sil[:, it % 2, 0:512], pkf[pg][:, :], ACT.Silu)
                        s.activation(sil[:, it % 2, 512:w], pkf[pg + 1][:, 0:w - 512],
                                     ACT.Silu).then_inc(H["sc"], 1)
                        for st in SPOS.get(it % 12, ()):
                            wdp_sca(s, it // 12, st)
                    V["sc"] += NIT

                @blk.tensor
                def _(t):
                    t.wait_ge(H["dsc"], bs["dsc"])
                    for it in range(NIT):
                        tt_i, ib = it // 3, it % 3
                        w = IBW[ib]
                        pg = (it % 2) * 4
                        if it >= 2:
                            t.wait_ge(H["vv"], bs["vv"] + (it - 2) + 1)
                        for dc in range(8):
                            lh = xq[:, dc, tt_i * 128:(tt_i + 1) * 128]
                            st_, sp_ = (dc == 0), (dc == 7)
                            t.matmul(pkf[pg][:, :], lh,
                                     wgT[:, dc, IBO[ib]:IBO[ib] + 512],
                                     start=st_, stop=sp_)
                            t.matmul(pkf[pg + 1][:, 0:w - 512], lh,
                                     wgT[:, dc, IBO[ib] + 512:IBO[ib] + w],
                                     start=st_, stop=sp_)
                            t.matmul(pkf[pg + 2][:, :], lh,
                                     wuT[:, dc, IBO[ib]:IBO[ib] + 512],
                                     start=st_, stop=sp_)
                            t.matmul(pkf[pg + 3][:, 0:w - 512], lh,
                                     wuT[:, dc, IBO[ib] + 512:IBO[ib] + w],
                                     start=st_, stop=sp_)
                        t.drain().then_inc(H["tt"], 1)
                    V["tt"] += NIT

                @blk.vector
                def _(v):
                    for it in range(NIT):
                        ib = it % 3
                        w = IBW[ib]
                        pg = (it % 2) * 4
                        v.wait_ge(H["sc"], bs["sc"] + it + 1)
                        if it >= 2:
                            v.wait_ge(H[f"dss{it % 2}"],
                                      bs[f"dss{it % 2}"] + 16 * ((it - 2) // 2 + 1))
                        v.tensor_tensor(spb[:, it % 2, 0:512], sil[:, it % 2, 0:512],
                                        pkf[pg + 2][:, :], ALU.mult)
                        v.tensor_tensor(spb[:, it % 2, 512:w], sil[:, it % 2, 512:w],
                                        pkf[pg + 3][:, 0:w - 512], ALU.mult)
                        v.drain()
                        v.tensor_reduce(fst[:, it % 2, 0:1], spb[:, it % 2, 0:w], AX,
                                        ALU.max, apply_absolute_value=True)
                        v.drain()
                        v.tensor_tensor(gacc[:, 1:2], gacc[:, 1:2],
                                        fst[:, it % 2, 0:1], ALU.max)
                        v.drain().then_inc(H["vv"], 1)
                        if it % 12 in VPOS:
                            wdp_vec(v, it // 12, VPOS[it % 12])
                    V["vv"] += NIT

                @blk.sync
                def _(sy):
                    for it in range(NIT):
                        tt_i, ib = it // 3, it % 3
                        w = IBW[ib]
                        sy.wait_ge(H["vv"], bs["vv"] + it + 1)
                        sy.dma_start(out=ss[tt_i * 128:(tt_i + 1) * 128,
                                            IBO[ib]:IBO[ib] + w],
                                     in_=spb[:, it % 2, 0:w]).then_inc(H[f"dss{it % 2}"], 16)
                        if it % 12 in (8, 9, 10, 11):
                            wdp_spl(sy, it // 12, it % 12 - 8)
                    V["dss0"] += NIT * 8
                    V["dss1"] += NIT * 8
                V["vvw"] += 9 * 8
                V["scw"] += 8 * 8
                V["dwt0"] += 32 * 16
                V["dwt1"] += 32 * 16
                V["dwq0"] += 16 * 32
                V["dwq1"] += 16 * 32

        if STAGE <= 11:
            return nc

        if STAGE <= 12:
            return nc
        with (
            nc.sbuf_tensor("wdT", [128, B, 22, D], BF16) as wdT,
            nc.sbuf_tensor("sT", [128, 2, 22, 512], F16) as sT,
            nc.sbuf_tensor("h2q", [128, 2, 22, 512], BF16) as h2q,
            nc.sbuf_tensor("qtm", [128, 2, 512], F32) as qtm,
            nc.sbuf_tensor("qii", [128, 2, 512], I32) as qii,
            nc.sbuf_tensor("xr0", [128, D], F32) as _r0,
            nc.sbuf_tensor("xr1", [128, D], F32) as _r1,
        ):
            xr = [_r0, _r1]

            def pre_s(s, bsx):
                for nm in ("dwq0", "dwq1", "dss0", "dss1"):
                    s.wait_ge(H[nm], bsx[nm])
                for ic in range(22):
                    s.dma_start(out=sT[:, 0, ic, :],
                                in_=ss[0:512, ic * 128:(ic + 1) * 128],
                                transpose=True).then_inc(H["dsc"], 16)
                s.wait_ge(H["dsc"], bsx["dsc"] + 22 * 16)
                for b in range(B):
                    for ic in range(22):
                        s.dma_start(out=wdT[:, b, ic, :],
                                    in_=wdq[b, 0:D, ic * 128:(ic + 1) * 128],
                                    transpose=True).then_inc(H["dsc"], 16)

            ST0_DSC = V["dsc"] + 22 * 16
            global_scale(1, ccb[1], 3, 2, pre_s)
            V["dsc"] += 66 * 16
            PRE_DSC = V["dsc"]
            if STAGE <= 13:
                return nc

            bs = snap()
            dstk = ExitStack()
            blk = dstk.enter_context(nc.Block())
            NC8 = B * 4
            q_done = {}
            r_done = {}
            pos = 0
            for tc in range(NC8):
                if tc >= 1:
                    for j in range(4 * (tc - 1), 4 * tc):
                        pos += 1
                        r_done[j] = pos
                for ic in range(22):
                    pos += 1
                    q_done[(tc, ic)] = pos
            for j in range(4 * (NC8 - 1), 4 * NC8):
                pos += 1
                r_done[j] = pos

            @blk.scalar
            def _(s):
                s.wait_ge(H["vv"], bs["vv"])
                for tc in range(NC8):
                    if tc >= 1:
                        for ic in range(22):
                            s.dma_start(out=sT[:, tc % 2, ic, :],
                                        in_=ss[tc * 512:(tc + 1) * 512,
                                               ic * 128:(ic + 1) * 128],
                                        transpose=True).then_inc(H["dsc"], 16)
                        V["dsc"] += 22 * 16
                    s.wait_ge(H["dsc"], V["dsc"] if tc >= 1 else ST0_DSC)
                    for ic in range(22):
                        ji = tc * 22 + ic
                        if ji >= 2:
                            prev = (tc * 22 + ic - 2)
                            s.wait_ge(H["vv"], bs["vv"]
                                      + q_done[(prev // 22, prev % 22)])
                        s.activation(qtm[:, ic % 2, :], sT[:, tc % 2, ic, :],
                                     ACT.Identity, bias=zc[:, 0:1],
                                     scale=scl[:, 3:4]).then_inc(H["sc"], 1)
                V["sc"] += NC8 * 22

            @blk.vector
            def _(v):
                for tc in range(NC8):
                    if tc >= 1:
                        for j in range(4 * (tc - 1), 4 * tc):
                            v.wait_ge(H["tt"], bs["tt"] + j + 1)
                            v.wait_ge(H[f"dgx{j % 2}"],
                                      bs[f"dgx{j % 2}"] + 16 * (j // 2 + 1))
                            if j >= 2:
                                v.wait_ge(H[f"dou{j % 2}"],
                                          bs[f"dou{j % 2}"]
                                          + 16 * ((j - 2) // 2 + 1))
                            pg = (j % 2) * 2
                            v.tensor_tensor(xr[j % 2][:, 0:512], pk[pg][:, :],
                                            xr[j % 2][:, 0:512], ALU.add)
                            v.tensor_tensor(xr[j % 2][:, 512:1024], pk[pg + 1][:, :],
                                            xr[j % 2][:, 512:1024], ALU.add)
                            v.drain().then_inc(H["vv"], 1)
                    for ic in range(22):
                        v.wait_ge(H["sc"], bs["sc"] + tc * 22 + ic + 1)
                        v.tensor_scalar(qii[:, ic % 2, :], qtm[:, ic % 2, :],
                                        -128.0, 127.0, ALU.max, ALU.min)
                        v.drain()
                        v.tensor_scalar(h2q[:, tc % 2, ic, :], qii[:, ic % 2, :],
                                        scl[:, 2:3], None,
                                        ALU.mult).then_inc(H["vv"], 1)
                for j in range(4 * (NC8 - 1), 4 * NC8):
                    v.wait_ge(H["tt"], bs["tt"] + j + 1)
                    v.wait_ge(H[f"dgx{j % 2}"],
                              bs[f"dgx{j % 2}"] + 16 * (j // 2 + 1))
                    v.wait_ge(H[f"dou{j % 2}"],
                              bs[f"dou{j % 2}"] + 16 * ((j - 2) // 2 + 1))
                    pg = (j % 2) * 2
                    v.tensor_tensor(xr[j % 2][:, 0:512], pk[pg][:, :],
                                    xr[j % 2][:, 0:512], ALU.add)
                    v.tensor_tensor(xr[j % 2][:, 512:1024], pk[pg + 1][:, :],
                                    xr[j % 2][:, 512:1024], ALU.add)
                    v.drain().then_inc(H["vv"], 1)
                V["vv"] += pos

            @blk.gpsimd
            def _(g):
                for j in range(B * NT):
                    b, ti = j // NT, j % NT
                    g.wait_ge(H[f"dou{j % 2}"], bs[f"dou{j % 2}"]
                              + (16 * ((j - 2) // 2 + 1) if j >= 2 else 0))
                    g.dma_start(out=xr[j % 2][:, :],
                                in_=out[b, ti * 128:(ti + 1) * 128, :]
                                ).then_inc(H[f"dgx{j % 2}"], 16)
                V["dgx0"] += B * NT * 8
                V["dgx1"] += B * NT * 8

            @blk.tensor
            def _(t):
                t.wait_ge(H["dsc"], PRE_DSC)
                for j in range(B * NT):
                    b, ti = j // NT, j % NT
                    tc = (b * NT + ti) // 4
                    tl = ti % 4
                    pg = (j % 2) * 2
                    t.wait_ge(H["vv"], bs["vv"] + q_done[(tc, 21)])
                    if j >= 2:
                        t.wait_ge(H["vv"], bs["vv"] + r_done[j - 2])
                    for ic in range(22):
                        lh = h2q[:, tc % 2, ic, tl * 128:(tl + 1) * 128]
                        t.matmul(pk[pg][:, :], lh, wdT[:, b, ic, 0:512],
                                 start=(ic == 0), stop=(ic == 21))
                        t.matmul(pk[pg + 1][:, :], lh, wdT[:, b, ic, 512:1024],
                                 start=(ic == 0), stop=(ic == 21))
                    t.drain().then_inc(H["tt"], 1)
                V["tt"] += B * NT

            @blk.sync
            def _(sy):
                for j in range(B * NT):
                    b, ti = j // NT, j % NT
                    sy.wait_ge(H["vv"], bs["vv"] + r_done[j])
                    sy.dma_start(out=out[b, ti * 128:(ti + 1) * 128, :],
                                 in_=xr[j % 2][:, :]).then_inc(H[f"dou{j % 2}"], 16)
                V["dou0"] += B * NT * 8
                V["dou1"] += B * NT * 8
                sy.wait_ge(H["dou0"], V["dou0"])
                sy.wait_ge(H["dou1"], V["dou1"])
            dstk.close()

    return nc


def kernel(**inputs):
    x = np.ascontiguousarray(np.asarray(inputs["L0_attn_x"], np.float32))
    W = np.ascontiguousarray(np.asarray(inputs["L0_attn_W"], np.float32))
    b = np.ascontiguousarray(
        np.asarray(inputs["L0_attn_b"], np.float32)).reshape(1, 3 * D)
    cx = np.asarray(inputs["L0_ffn_clip_x"], np.float32).reshape(1)
    cd = np.asarray(inputs["L0_ffn_clip_down"], np.float32).reshape(1)
    wgt = np.ascontiguousarray(np.asarray(inputs["L0_ffn_w_gate"], np.float32))
    wup = np.ascontiguousarray(np.asarray(inputs["L0_ffn_w_up"], np.float32))
    wdn = np.ascontiguousarray(np.asarray(inputs["L0_ffn_w_down"], np.float32))
    c = np.ascontiguousarray(np.asarray(inputs["L0_c"], np.float32))
    wm = np.ascontiguousarray(np.asarray(inputs["L0_wmod"], np.float32))

    clips = np.ascontiguousarray(
        np.concatenate([cx, cd]).reshape(1, 2).astype(np.float32))
    idb_np = np.eye(128, dtype=ml_dtypes.bfloat16)
    idf_np = np.eye(128, dtype=np.float32)
    mask_np = np.triu(np.full((128, 128), NEG, np.float32), k=1)

    nc = build_nc()
    in_maps = []
    for core in range(8):
        in_maps.append(dict(
            x=np.ascontiguousarray(x[core * B:(core + 1) * B]),
            c=np.ascontiguousarray(c[core * B:(core + 1) * B]),
            wmod=wm, aw=W, ab=b, clips=clips,
            wg=wgt, wu=wup, wd=wdn,
            idb=idb_np, idf=idf_np, maskc=mask_np,
        ))
    res = run_bass_kernel_spmd(nc, in_maps, core_ids=list(range(8)))
    return np.concatenate([r["out"] for r in res.results], axis=0)


if __name__ == "__main__":
    nc_ = build_nc()
    n = sum(1 for _ in nc_.all_instructions())
    print("instructions:", n)
